# version 07
# baseline (speedup 1.0000x reference)
import numpy as np
from contextlib import ExitStack

import concourse.bass as bass
import concourse.mybir as mybir
from concourse.bass_utils import run_bass_kernel_spmd

F32 = mybir.dt.float32
BF16 = mybir.dt.bfloat16
AF = mybir.ActivationFunctionType
ALU = mybir.AluOpType
AX = mybir.AxisListType

B = 16
S = 4096
E = 1024
NCORE = 8
BPC = B // NCORE
NH = 16
HD = 64
DEPTH = 6
WIN = 256
DI = 2730
EPS_MP = 1e-4
EPS_LN = 1e-5
QK_SCALE = 8.0
ATTN_MIX = 0.7071067811865476
NEG = -1.0e30
EC = E // 128
FFN_CH = [42] + [128] * 21
FFN_OFF = np.concatenate([[0], np.cumsum(FFN_CH)]).tolist()
NFC = len(FFN_CH)



def _host_constants():
    c = {}
    c["ident"] = np.eye(128, dtype=np.float32)
    rot = np.zeros((128, 128), dtype=np.float32)
    for h0 in (0, 64):
        for d in range(32):
            rot[h0 + d + 32, h0 + d] = -1.0
        for d in range(32, 64):
            rot[h0 + d - 32, h0 + d] = 1.0
    c["rotm"] = rot
    blk2 = np.zeros((128, 2), dtype=np.float32)
    blk2[0:64, 0] = 1.0
    blk2[64:128, 1] = 1.0
    c["blk2"] = blk2
    exp2 = np.zeros((2, 128), dtype=np.float32)
    exp2[0, 0:64] = 1.0
    exp2[1, 64:128] = 1.0
    c["exp2"] = exp2
    c["ones_col"] = np.ones((128, 1), dtype=np.float32)
    c["ones_row"] = np.ones((1, 128), dtype=np.float32)
    i = np.arange(128)[:, None]
    j = np.arange(128)[None, :]
    maskA = np.where(j >= i, 0.0, NEG).astype(np.float32)
    maskB = np.where(j <= i, 0.0, NEG).astype(np.float32)
    c["maskband"] = np.concatenate(
        [maskA, np.zeros((128, 128), np.float32), maskB], axis=1)
    half = np.arange(0, HD, 2, dtype=np.float64)
    inv_freq = 1.0 / (10000.0 ** (half / HD))
    t = np.arange(S, dtype=np.float64)
    freqs = t[:, None] * inv_freq[None, :]
    freqs = np.concatenate([freqs, freqs], axis=1)
    base_scale = (half + 0.4 * HD) / (1.4 * HD)
    power = (t - S // 2) / (WIN // 2)
    scale = base_scale[None, :] ** power[:, None]
    scale = np.concatenate([scale, scale], axis=1)
    qc = (np.cos(freqs) * scale).T.astype(np.float32)
    qs_ = (np.sin(freqs) * scale).T.astype(np.float32)
    kc = (np.cos(freqs) / scale).T.astype(np.float32)
    ks_ = (np.sin(freqs) / scale).T.astype(np.float32)
    c["rot_qc"] = np.concatenate([qc, qc], axis=0)
    c["rot_qs"] = np.concatenate([qs_, qs_], axis=0)
    c["rot_kc"] = np.concatenate([kc, kc], axis=0)
    c["rot_ks"] = np.concatenate([ks_, ks_], axis=0)
    consts = np.zeros((128, 8), dtype=np.float32)
    consts[:, 0] = EPS_LN
    consts[:, 1] = EPS_MP
    consts[:, 2] = 1.0
    c["consts"] = consts
    return c


def _layout_inputs(inp, core):
    m = {}
    m["x"] = np.ascontiguousarray(np.asarray(inp["L0_attn_query"])[2 * core: 2 * core + 2])
    m["wq"] = np.asarray(inp["L0_attn_w_q"])
    m["wk"] = np.asarray(inp["L0_attn_w_k"])
    m["wv"] = np.asarray(inp["L0_attn_w_v"])
    m["wo"] = np.asarray(inp["L0_attn_w_o"])
    m["gain128"] = np.ascontiguousarray(np.broadcast_to(
        np.asarray(inp["L0_attn_gain"], np.float32).reshape(1, 1), (128, 1)))
    m["wqkv"] = np.asarray(inp["L0_ffn_Wqkv"])
    m["woi"] = np.asarray(inp["L0_ffn_Wo"])
    m["w1"] = np.asarray(inp["L0_ffn_W1"])
    m["w2"] = np.asarray(inp["L0_ffn_W2"])
    qs = np.asarray(inp["L0_ffn_q_scale"], np.float32)
    ks = np.asarray(inp["L0_ffn_k_scale"], np.float32)
    m["qs_all"] = np.ascontiguousarray(np.tile(qs, (1, 2)).T)
    m["ks_all"] = np.ascontiguousarray(np.tile(ks, (1, 2)).T)
    lng = np.asarray(inp["L0_ffn_ln_g"], np.float32)
    lnb = np.asarray(inp["L0_ffn_ln_b"], np.float32)
    m["lng"] = np.ascontiguousarray(
        lng.reshape(DEPTH, EC, 128).transpose(2, 0, 1).reshape(128, DEPTH * EC))
    m["lnb"] = np.ascontiguousarray(
        lnb.reshape(DEPTH, EC, 128).transpose(2, 0, 1).reshape(128, DEPTH * EC))
    b1 = np.asarray(inp["L0_ffn_b1"], np.float32)
    b2 = np.asarray(inp["L0_ffn_b2"], np.float32)
    b1a = np.zeros((128, DEPTH * NFC), np.float32)
    b1g = np.zeros((128, DEPTH * NFC), np.float32)
    for l in range(DEPTH):
        for ci, (off, sz) in enumerate(zip(FFN_OFF[:-1], FFN_CH)):
            b1a[0:sz, l * NFC + ci] = b1[l, off:off + sz]
            b1g[0:sz, l * NFC + ci] = b1[l, DI + off:DI + off + sz]
    m["b1a"] = b1a
    m["b1g"] = b1g
    m["b2v"] = np.ascontiguousarray(
        b2.reshape(DEPTH, EC, 128).transpose(2, 0, 1).reshape(128, DEPTH * EC))
    m.update(_host_constants())
    return m



class Prog:
    def __init__(self, nc):
        self.nc = nc
        self.c = {}
        self.sem = {}

    def S(self, name):
        if name not in self.sem:
            self.sem[name] = self.nc.alloc_semaphore(name)
            self.c[name] = 0
        return self.sem[name]

    def inc(self, instr, name, n=1):
        instr.then_inc(self.S(name), n)
        self.c[name] += n
        return instr

    def w(self, eng, *names):
        for name in names:
            if name in self.c and self.c[name] > 0:
                eng.wait_ge(self.S(name), self.c[name])

    def mark(self):
        return dict(self.c)

    def wat(self, eng, snap, *names):
        for name in names:
            v = snap.get(name, 0)
            if v > 0:
                eng.wait_ge(self.S(name), v)

    def flush(self, q, name):
        if name in self.c and self.c[name] > 0:
            q.wait_ge(self.S(name), self.c[name])


def build_program(dbg=(), stage="FULL"):
    nc = bass.Bass("TRN2", target_bir_lowering=False)
    p = Prog(nc)

    def din(name, shape, dt=F32):
        return nc.dram_tensor(name, list(shape), dt, kind="ExternalInput").ap()

    x_in = din("x", [BPC, S, E])
    wq_in = din("wq", [E, E]); wk_in = din("wk", [E, E])
    wv_in = din("wv", [E, E]); wo_in = din("wo", [E, E])
    gain_in = din("gain128", [128, 1])
    wqkv_in = din("wqkv", [DEPTH, E, 3 * E])
    woi_in = din("woi", [DEPTH, E, E])
    w1_in = din("w1", [DEPTH, E, 2 * DI])
    w2_in = din("w2", [DEPTH, DI, E])
    qs_in = din("qs_all", [128, DEPTH]); ks_in = din("ks_all", [128, DEPTH])
    lng_in = din("lng", [128, DEPTH * EC]); lnb_in = din("lnb", [128, DEPTH * EC])
    b1a_in = din("b1a", [128, DEPTH * NFC]); b1g_in = din("b1g", [128, DEPTH * NFC])
    b2_in = din("b2v", [128, DEPTH * EC])
    ident_in = din("ident", [128, 128])
    rotm_in = din("rotm", [128, 128])
    blk2_in = din("blk2", [128, 2]); exp2_in = din("exp2", [2, 128])
    onescol_in = din("ones_col", [128, 1]); onesrow_in = din("ones_row", [1, 128])
    mask_in = din("maskband", [128, 384])
    rqc_in = din("rot_qc", [128, S]); rqs_in = din("rot_qs", [128, S])
    rkc_in = din("rot_kc", [128, S]); rks_in = din("rot_ks", [128, S])
    consts_in = din("consts", [128, 8])

    y_out = nc.dram_tensor("y", [BPC, S, E], F32, kind="ExternalOutput").ap()

    def scr(name, shape, dt=F32):
        return nc.dram_tensor(name, list(shape), dt, kind="Internal").ap()

    xn0 = scr("xn0", [BPC, E, S])
    xf1 = scr("xf1", [BPC, E, S])
    xf2 = scr("xf2", [BPC, E, S])
    s_buf = scr("s_buf", [BPC, E, S])
    v_buf = scr("v_buf", [BPC, E, S], BF16)
    mu_dram = scr("mu_dram", [BPC, 16, WIN])
    rstd_dram = scr("rstd_dram", [BPC, 16, WIN])
    scratch_map = {"xn": xn0, "s": s_buf, "v": v_buf, "xf1": xf1, "xf2": xf2}

    dbg_outs = {}
    for name in dbg:
        src = scratch_map[name]
        dbg_outs[name] = nc.dram_tensor(
            "dbg_" + name, list(src.shape), src.dtype, kind="ExternalOutput").ap()

    ctx = ExitStack()

    def sb(name, shape, dt):
        return ctx.enter_context(nc.sbuf_tensor(name, list(shape), dt))

    def psum(name, shape, dt=F32):
        return ctx.enter_context(nc.psum_tensor(name, list(shape), dt))

    ident_bf = sb("ident_bf", [128, 128], BF16)
    ident_f = sb("ident_f", [128, 128], F32)
    rotm_bf = sb("rotm_bf", [128, 128], BF16)
    blk2_bf = sb("blk2_bf", [128, 2], BF16)
    exp2_f = sb("exp2_f", [2, 128], F32)
    onescol_bf = sb("onescol_bf", [128, 1], BF16)
    onesrow_f = sb("onesrow_f", [1, 128], F32)
    maskband = sb("maskband_sb", [128, 384], F32)
    consts = sb("consts_sb", [128, 8], F32)
    gain_sb = sb("gain_sb", [128, 1], F32)
    qs_sb = sb("qs_sb", [128, DEPTH], F32)
    ks_sb = sb("ks_sb", [128, DEPTH], F32)
    lng_sb = sb("lng_sb", [128, DEPTH * EC], F32)
    lnb_sb = sb("lnb_sb", [128, DEPTH * EC], F32)
    b1a_sb = sb("b1a_sb", [128, DEPTH * NFC], F32)
    b1g_sb = sb("b1g_sb", [128, DEPTH * NFC], F32)
    b2_sb = sb("b2_sb", [128, DEPTH * EC], F32)

    arenaA = sb("arenaA", [128, 8 * 5460], BF16)
    arenaB = sb("arenaB", [128, NFC * 1024], BF16)

    WQT = arenaA[:, 0 * 8192: 1 * 8192]
    WKT = arenaA[:, 1 * 8192: 2 * 8192]
    WVT = arenaA[:, 2 * 8192: 3 * 8192]
    WOT = arenaA[:, 3 * 8192: 4 * 8192]

    PS = [psum(f"ps{i}", [128, 512], F32) for i in range(6)]
    PT = psum("pt_bf", [128, 1024], BF16)
    PT2 = psum("pt2_bf", [128, 1024], BF16)

    sync = nc.sync
    scl = nc.scalar
    vec = nc.vector
    ten = nc.tensor
    gp = nc.gpsimd

    for src, dst in [(ident_in, ident_f), (exp2_in, exp2_f), (onesrow_in, onesrow_f),
                     (mask_in, maskband), (consts_in, consts), (gain_in, gain_sb),
                     (qs_in, qs_sb), (ks_in, ks_sb), (lng_in, lng_sb),
                     (lnb_in, lnb_sb), (b1a_in, b1a_sb), (b1g_in, b1g_sb),
                     (b2_in, b2_sb)]:
        p.inc(sync.dma_start(out=dst[:], in_=src), "ld", 16)
    for src, dst in [(ident_in, ident_bf), (rotm_in, rotm_bf), (blk2_in, blk2_bf),
                     (onescol_in, onescol_bf)]:
        p.inc(gp.dma_start(out=dst[:], in_=src), "ldg", 16)

    p0ctx = ExitStack()
    w_stage = p0ctx.enter_context(nc.sbuf_tensor("w_stage", [128, 1024], F32))
    w_rows = p0ctx.enter_context(nc.sbuf_tensor("w_rows", [128, 4], F32))
    w_dump = p0ctx.enter_context(nc.sbuf_tensor("w_dump", [128, 1024], F32))
    wn_bf = p0ctx.enter_context(nc.sbuf_tensor("wn_bf", [128, 1024], BF16))

    p.w(vec, "ld", "ldg")
    p.w(ten, "ld", "ldg")
    for w_src, w_dst in [(wq_in, WQT), (wk_in, WKT), (wv_in, WVT), (wo_in, WOT)]:
        for oc in range(EC):
            p.w(sync, "se", "pe")
            p.flush(sync, "ld")
            p.inc(sync.dma_start(out=w_stage[:],
                                 in_=w_src[oc * 128:(oc + 1) * 128, :]), "ld", 16)
            p.w(scl, "ld", "ldg")
            scl.activation(out=w_dump[:], in_=w_stage[:], func=AF.Square,
                           accum_out=w_rows[:, 0:1])
            scl.drain()
            scl.activation(out=w_rows[:, 1:2], in_=w_rows[:, 0:1], func=AF.Ln)
            scl.drain()
            p.inc(scl.activation(out=w_rows[:, 1:2], in_=w_rows[:, 1:2],
                                 func=AF.Exp, scale=0.5), "se")
            p.w(vec, "se")
            vec.tensor_scalar(out=w_rows[:, 2:3], in0=w_rows[:, 1:2],
                              scalar1=1.0 / 32.0, scalar2=EPS_MP,
                              op0=ALU.mult, op1=ALU.add)
            vec.drain()
            vec.reciprocal(w_rows[:, 2:3], w_rows[:, 2:3])
            vec.drain()
            p.inc(vec.tensor_scalar(out=w_rows[:, 3:4], in0=w_rows[:, 2:3],
                                    scalar1=gain_sb[:, 0:1], scalar2=1.0 / 32.0,
                                    op0=ALU.mult, op1=ALU.mult), "ve")
            p.w(scl, "ve")
            scl.drain()
            p.inc(scl.activation(out=wn_bf[:], in_=w_stage[:], func=AF.Copy,
                                 scale=w_rows[:, 3:4]), "se")
            p.w(ten, "se", "ve")
            for e8 in range(EC):
                mm = ten.transpose(PT[:, e8 * 128:(e8 + 1) * 128],
                                   wn_bf[:, e8 * 128:(e8 + 1) * 128], ident_bf[:])
            p.inc(mm, "pe")
            p.w(vec, "pe")
            p.inc(vec.tensor_copy(
                w_dst.rearrange("p (e o) -> p e o", o=1024)
                [:, :, oc * 128:(oc + 1) * 128],
                PT[:].rearrange("p (e o) -> p e o", o=128)), "ve")

    p0ctx.close()

    TT = 512
    NTO = S // TT
    outctx = ExitStack()

    def osb(name, shape, dt):
        return outctx.enter_context(nc.sbuf_tensor(name, list(shape), dt))

    xt_sb = osb("xt_sb", [128, 4096], F32)
    sqs_sb = osb("sqs_sb", [128, 1024], F32)
    ln_sc = osb("ln_sc", [128, 8], F32)
    xn_tm = osb("xn_tm", [128, 4096], F32)
    xn_fm = osb("xn_fm", [128, 8 * 512], F32)
    xn_bf = osb("xn_bf", [128, 8 * 512], BF16)
    qn_sb = osb("qn_sb", [128, 1024], BF16)
    kn_sb = osb("kn_sb", [128, 1024], BF16)
    nrm_sc = osb("nrm_sc", [128, 48], F32)
    v_sb = osb("v_sb", [128, 512], BF16)
    attnb = osb("attnb", [128, 512], BF16)
    attnT = osb("attnT", [128, 512], BF16)
    sm_sc = osb("sm_sc", [128, 4], F32)
    o_bf = arenaB[:, 0:4096]
    s_out = osb("s_out", [128, 512], F32)
    scores_acc = osb("scores_acc", [128, 512], F32)

    SCORES = PS[5]

    def phase_A(s):
        p.w(sync, "ld", "se", "ve", "pe")
        p.w(ten, "ve", "se", "ld", "ldg")
        p.w(scl, "ve", "ld")
        p.w(vec, "se", "ld")
        for tau in range(NTO):
            xt = xt_sb[:, 0:4096]
            xnt = xn_tm[:, 0:4096]
            p.w(sync, "seA", "veA", "peA")
            p.flush(sync, "lx")
            for j in range(4):
                p.inc(sync.dma_start(
                    out=xt[:, j * 1024:(j + 1) * 1024],
                    in_=x_in[s, tau * TT + j * 128: tau * TT + (j + 1) * 128, :]),
                    "lx", 16)
            p.w(scl, "lx")
            p.w(vec, "lx")
            for j in range(4):
                xj = xt[:, j * 1024:(j + 1) * 1024]
                p.inc(scl.activation(out=sqs_sb[:], in_=xj, func=AF.Square,
                                     accum_out=ln_sc[:, 0:1]), "seA")
                p.w(vec, "seA")
                vec.reduce_sum(ln_sc[:, 1:2], xj, axis=AX.X)
                vec.drain()
                vec.tensor_scalar_mul(ln_sc[:, 2:3], ln_sc[:, 1:2], 1.0 / E)
                vec.drain()
                vec.tensor_tensor(ln_sc[:, 3:4], ln_sc[:, 2:3], ln_sc[:, 2:3],
                                  op=ALU.mult)
                vec.drain()
                p.inc(vec.scalar_tensor_tensor(
                    out=ln_sc[:, 4:5], in0=ln_sc[:, 0:1], scalar=1.0 / E,
                    in1=ln_sc[:, 3:4], op0=ALU.mult, op1=ALU.subtract), "veA")
                p.w(scl, "veA")
                scl.activation(out=ln_sc[:, 5:6], in_=ln_sc[:, 4:5], func=AF.Ln,
                               bias=consts[:, 0:1])
                scl.drain()
                p.inc(scl.activation(out=ln_sc[:, 5:6], in_=ln_sc[:, 5:6],
                                     func=AF.Exp, scale=-0.5), "seA")
                p.w(vec, "seA")
                vec.drain()
                p.inc(vec.scalar_tensor_tensor(
                    out=ln_sc[:, 6:7], in0=ln_sc[:, 2:3], scalar=-1.0,
                    in1=ln_sc[:, 5:6], op0=ALU.mult, op1=ALU.mult), "veA")
                vec.drain()
                p.inc(vec.tensor_scalar(out=xnt[:, j * 1024:(j + 1) * 1024],
                                        in0=xj, scalar1=ln_sc[:, 5:6],
                                        scalar2=ln_sc[:, 6:7],
                                        op0=ALU.mult, op1=ALU.add), "veA")
            if stage == "A1":
                continue
            p.w(ten, "seA")
            for ec in range(EC):
                bank = PS[ec % 2]
                if stage != "A2noT":
                    p.w(ten, "seA", "veA")
                    for j in range(4):
                        mm = ten.transpose(bank[:, j * 128:(j + 1) * 128],
                                           xnt[:, j * 1024 + ec * 128: j * 1024 + ec * 128 + 128],
                                           ident_f[:])
                    p.inc(mm, "peA")
                    p.w(scl, "peA", "sx")
                    p.w(vec, "peA")
                    p.inc(scl.activation(out=xn_fm[:, ec * 512:(ec + 1) * 512],
                                         in_=bank[:], func=AF.Copy), "seA")
                    p.w(vec, "seA")
                    p.inc(vec.tensor_copy(xn_bf[:, ec * 512:(ec + 1) * 512],
                                          xn_fm[:, ec * 512:(ec + 1) * 512]), "veA")
                else:
                    p.w(scl, "sx")
                    p.inc(scl.activation(out=xn_fm[:, ec * 512:(ec + 1) * 512],
                                         in_=xnt[:, ec * 512:(ec + 1) * 512],
                                         func=AF.Copy), "seA")
                    p.inc(vec.tensor_copy(xn_bf[:, ec * 512:(ec + 1) * 512],
                                          xnt[:, ec * 512:(ec + 1) * 512]), "veA")
            if stage in ("A2noS", "A2noV"):
                continue
            p.w(sync, "seA")
            p.flush(sync, "sx")
            for ec in range(EC):
                p.inc(sync.dma_start(
                    out=xn0[s, ec * 128:(ec + 1) * 128, tau * TT:(tau + 1) * TT],
                    in_=xn_fm[:, ec * 512:(ec + 1) * 512]), "sx", 16)
                p.inc(sync.dma_start(
                    out=xf1[s, ec * 128:(ec + 1) * 128, tau * TT:(tau + 1) * TT],
                    in_=xn_fm[:, ec * 512:(ec + 1) * 512]), "sx", 16)
            if stage in ("A2", "A2noT", "A2noS", "A2noV"):
                continue
            for tc in range(4):
                qb, qb2 = PS[0], PS[1]
                kb, kb2 = PS[2], PS[3]
                lhs = lambda ecc: xn_bf[:, ecc * 512 + tc * 128: ecc * 512 + tc * 128 + 128]
                p.w(ten, "veA", "seA")
                for ecc in range(EC):
                    ten.matmul(qb[:], lhs(ecc), WQT[:, ecc * 1024:ecc * 1024 + 512],
                               start=(ecc == 0), stop=(ecc == EC - 1))
                for ecc in range(EC):
                    ten.matmul(qb2[:], lhs(ecc), WQT[:, ecc * 1024 + 512:(ecc + 1) * 1024],
                               start=(ecc == 0), stop=(ecc == EC - 1))
                for ecc in range(EC):
                    ten.matmul(kb[:], lhs(ecc), WKT[:, ecc * 1024:ecc * 1024 + 512],
                               start=(ecc == 0), stop=(ecc == EC - 1))
                for ecc in range(EC):
                    mm = ten.matmul(kb2[:], lhs(ecc), WKT[:, ecc * 1024 + 512:(ecc + 1) * 1024],
                                    start=(ecc == 0), stop=(ecc == EC - 1))
                p.inc(mm, "peA")
                p.w(scl, "peA")
                p.w(vec, "peA")
                for proj, (b0, b1k, dst) in enumerate(
                        [(qb, qb2, qn_sb), (kb, kb2, kn_sb)]):
                    scl.activation(out=sqs_sb[:, 0:512], in_=b0[:], func=AF.Square)
                    p.inc(scl.activation(out=sqs_sb[:, 512:1024], in_=b1k[:],
                                         func=AF.Square), "seA")
                    p.w(vec, "seA")
                    p.inc(vec.reduce_sum(
                        nrm_sc[:, 0:16],
                        sqs_sb[:].rearrange("p (h d) -> p h d", d=64), axis=AX.X),
                        "veA")
                    p.w(scl, "veA")
                    scl.drain()
                    scl.activation(out=nrm_sc[:, 16:32], in_=nrm_sc[:, 0:16],
                                   func=AF.Ln)
                    scl.drain()
                    p.inc(scl.activation(out=nrm_sc[:, 16:32], in_=nrm_sc[:, 16:32],
                                         func=AF.Exp, scale=0.5), "seA")
                    p.w(vec, "seA")
                    vec.drain()
                    vec.tensor_scalar(out=nrm_sc[:, 32:48], in0=nrm_sc[:, 16:32],
                                      scalar1=1.0 / 8.0, scalar2=EPS_MP,
                                      op0=ALU.mult, op1=ALU.add)
                    vec.drain()
                    vec.reciprocal(nrm_sc[:, 32:48], nrm_sc[:, 32:48])
                    vec.drain()
                    if proj == 1:
                        vec.tensor_scalar_mul(nrm_sc[:, 32:48], nrm_sc[:, 32:48],
                                              1.0 / 64.0)
                        vec.drain()
                    rb = nrm_sc[:, 32:48].rearrange("p (h o) -> p h o", o=1)\
                        .broadcast_to([128, 16, 64])
                    dv = dst[:].rearrange("p (h d) -> p h d", d=64)
                    vec.tensor_tensor(dv[:, 0:8, :],
                                      b0[:].rearrange("p (h d) -> p h d", d=64),
                                      rb[:, 0:8, :], op=ALU.mult)
                    p.inc(vec.tensor_tensor(dv[:, 8:16, :],
                                            b1k[:].rearrange("p (h d) -> p h d", d=64),
                                            rb[:, 8:16, :], op=ALU.mult), "veA")
                if stage == "A3":
                    continue
                p.w(ten, "veA")
                for h in range(NH):
                    mm = ten.matmul(
                        SCORES[(h % 2) * 64:(h % 2) * 64 + 64,
                               (h // 2) * 64:(h // 2) * 64 + 64],
                        qn_sb[:, h * 64:(h + 1) * 64],
                        kn_sb[:, h * 64:(h + 1) * 64],
                        start=True, stop=True)
                p.inc(mm, "peA")
                p.w(vec, "peA")
                vec.drain()
                if tau == 0 and tc == 0:
                    p.inc(vec.tensor_copy(scores_acc[:], SCORES[:]), "veA")
                else:
                    p.inc(vec.tensor_tensor(scores_acc[:], scores_acc[:],
                                            SCORES[:], op=ALU.add), "veA")
            if stage in ("A3", "A4"):
                continue
            for oc in range(EC):
                vb = PS[4]
                p.w(ten, "seA")
                for ecc in range(EC):
                    mm = ten.matmul(vb[:],
                                    WVT[:, ecc * 1024 + oc * 128: ecc * 1024 + oc * 128 + 128],
                                    xn_bf[:, ecc * 512:(ecc + 1) * 512],
                                    start=(ecc == 0), stop=(ecc == EC - 1))
                p.inc(mm, "peA")
                p.w(scl, "peA", "sv")
                p.inc(scl.activation(out=v_sb[:], in_=vb[:], func=AF.Copy), "seA")
                p.w(gp, "seA")
                p.flush(gp, "sv")
                p.inc(gp.dma_start(
                    out=v_buf[s, oc * 128:(oc + 1) * 128, tau * TT:(tau + 1) * TT],
                    in_=v_sb[:]), "sv", 16)

    def phase_B(s):
        p.w(vec, "peA")
        p.w(scl, "peA", "veA")
        vec.drain()
        for h in range(NH):
            pb = (h % 2) * 64
            co = (h // 2) * 64
            sc_v = scores_acc[pb:pb + 64, co:co + 64]
            p.inc(vec.reduce_max(sm_sc[pb:pb + 64, 0:1], sc_v, axis=AX.X,
                                 negate=True), "veB")
            p.w(scl, "veB")
            p.inc(scl.activation(out=sqs_sb[pb:pb + 64, co:co + 64], in_=sc_v,
                                 func=AF.Exp, bias=sm_sc[pb:pb + 64, 0:1],
                                 accum_out=sm_sc[pb:pb + 64, 1:2]), "seB")
            p.w(vec, "seB")
            p.inc(vec.reciprocal(sm_sc[pb:pb + 64, 2:3], sm_sc[pb:pb + 64, 1:2]),
                  "veB")
            p.w(scl, "veB")
            scl.drain()
            p.inc(scl.activation(out=attnb[pb:pb + 64, co:co + 64],
                                 in_=sqs_sb[pb:pb + 64, co:co + 64], func=AF.Copy,
                                 scale=sm_sc[pb:pb + 64, 2:3]), "seB")
        p.w(ten, "seB", "veB2")
        for h in range(NH):
            pb = (h % 2) * 64
            co = (h // 2) * 64
            mm = ten.transpose(PT[pb:pb + 64, co:co + 64],
                               attnb[pb:pb + 64, co:co + 64],
                               ident_bf[pb:pb + 64, pb:pb + 64])
        p.inc(mm, "peB")
        p.w(vec, "peB")
        p.inc(vec.tensor_copy(attnT[:], PT[:, 0:512]), "veB2")

    def phase_C(s):
        p.w(gp, "sv")
        p.w(sync, "sx")
        for tau in range(NTO):
            p.w(sync, "veC", "peC", "veA", "seA", "peA")
            p.w(gp, "peC", "veA", "peA")
            p.flush(sync, "lc")
            p.flush(gp, "lcv")
            for ec in range(EC):
                p.inc(sync.dma_start(
                    out=xn_fm[:, ec * 512:(ec + 1) * 512],
                    in_=xn0[s, ec * 128:(ec + 1) * 128, tau * TT:(tau + 1) * TT]),
                    "lc", 16)
                p.inc(gp.dma_start(
                    out=xn_bf[:, ec * 512:(ec + 1) * 512],
                    in_=v_buf[s, ec * 128:(ec + 1) * 128, tau * TT:(tau + 1) * TT]),
                    "lcv", 16)
            p.w(ten, "lcv", "veB2")
            for hc in range(EC):
                ob = PS[hc % 2]
                p.w(ten, "seC")
                for sub in range(2):
                    h = hc * 2 + sub
                    mm = ten.matmul(ob[sub * 64:sub * 64 + 64, :],
                                    attnT[(h % 2) * 64:(h % 2) * 64 + 64,
                                          (h // 2) * 64:(h // 2) * 64 + 64],
                                    xn_bf[:, hc * 512:(hc + 1) * 512]
                                    [(h % 2) * 64:(h % 2) * 64 + 64, :],
                                    start=True, stop=True)
                p.inc(mm, "peC")
                p.w(scl, "peC")
                p.inc(scl.activation(out=o_bf[:, hc * 512:(hc + 1) * 512],
                                     in_=ob[:], func=AF.Copy), "seC")
            p.w(ten, "seC", "lc")
            for oc in range(EC):
                wb = PS[2 + oc % 2]
                p.w(ten, "veC")
                for ecc in range(EC):
                    mm = ten.matmul(wb[:],
                                    WOT[:, ecc * 1024 + oc * 128: ecc * 1024 + oc * 128 + 128],
                                    o_bf[:, ecc * 512:(ecc + 1) * 512],
                                    start=(ecc == 0), stop=(ecc == EC - 1))
                p.inc(mm, "peC")
                p.w(vec, "peC", "sc")
                vec.drain()
                p.inc(vec.tensor_tensor(s_out[:], wb[:],
                                        xn_fm[:, oc * 512:(oc + 1) * 512],
                                        op=ALU.add), "veC")
                p.w(sync, "veC")
                p.flush(sync, "sc")
                p.inc(sync.dma_start(
                    out=s_buf[s, oc * 128:(oc + 1) * 128, tau * TT:(tau + 1) * TT],
                    in_=s_out[:]), "sc", 16)

    ostage = stage if (stage.startswith("A") or stage == "P0") else "ABC"
    for s in range(BPC):
        if "A" in ostage:
            phase_A(s)
        if "B" in ostage:
            phase_B(s)
        if "C" in ostage:
            phase_C(s)
        if ostage == "P0":
            break

    outctx.close()

    if stage.startswith("L") or stage == "FULL":
        if stage == "FULL":
            n_layers, half = DEPTH, False
        elif stage.endswith("A"):
            n_layers, half = int(stage[1:-1]), True
        else:
            n_layers, half = int(stage[1:]), False

        ictx = ExitStack()

        def isb(name, shape, dt):
            return ictx.enter_context(nc.sbuf_tensor(name, list(shape), dt))

        TI = 256
        NTI = S // TI
        x_cur = isb("x_cur", [128, 8 * TI], F32)
        xq_bf = isb("xq_bf", [128, 8 * TI], BF16)
        xsh4 = isb("xsh4", [128, 4 * TI], F32)
        q0_sb = isb("q0_sb", [128, 2 * TI], F32)
        sqi_bf = isb("sqi_bf", [128, 2 * TI], BF16)
        nrmi = isb("nrmi", [128, 16], F32)
        q1_bf = isb("q1_bf", [128, 2 * TI], BF16)
        scrX = isb("scrX", [128, 4 * TI], F32)
        t1_sb = scrX[:, 0:2 * TI]
        t2_sb = scrX[:, 2 * TI:4 * TI]
        ptB = scrX.bitcast(BF16)
        rnrow = isb("rnrow", [2, 2 * TI], F32)
        qrot_bf = isb("qrot_bf", [128, 8 * TI], BF16)
        kv_ring = isb("kv_ring", [128, 4 * 8 * TI], BF16)
        k_ring = kv_ring[:, 0:2 * 8 * TI]
        v_ring = kv_ring[:, 2 * 8 * TI:4 * 8 * TI]
        rot4 = isb("rot4", [128, 4 * TI], F32)
        zm_sb = isb("zm_sb", [128, 2 * 2 * 2 * 384], BF16)
        smx = isb("smx", [128, 16], F32)
        pt_sb = isb("pt_sb", [128, 2048], BF16)
        oa_bf = isb("oa_bf", [128, 8 * TI], BF16)
        oi_small = isb("oi_small", [128, 2 * TI], F32)
        xst_bf = qrot_bf
        brow = isb("brow", [1, WIN + 1], F32)
        b2row = isb("b2row", [1, WIN + 1], F32)
        r2row = isb("r2row", [1, WIN], F32)
        murow = isb("murow", [1, WIN], F32)
        rstdrow = isb("rstdrow", [1, WIN], F32)
        xn_bfi = qrot_bf
        ga_sb = t1_sb
        tln_sb = t2_sb
        h_bf = kv_ring[:, 0:NFC * TI]

        WQKV = arenaA[:, 0:8 * 3072]
        WOI = arenaA[:, 8 * 3072:8 * 3072 + 8 * 1024]
        W1A = arenaA
        W2A = arenaB

        def load_attn_weights(l):
            p.w(gp, "peI")
            p.flush(gp, "lw")
            for e in range(EC):
                p.inc(gp.dma_start(out=WQKV[:, e * 3072:(e + 1) * 3072],
                                   in_=wqkv_in[l, e * 128:(e + 1) * 128, :]), "lw", 16)
            for e in range(EC):
                p.inc(gp.dma_start(out=WOI[:, e * 1024:(e + 1) * 1024],
                                   in_=woi_in[l, e * 128:(e + 1) * 128, :]), "lw", 16)

        def load_ffn_weights(l):
            p.w(gp, "peI")
            p.flush(gp, "lw")
            for e in range(EC):
                p.inc(gp.dma_start(out=W1A[:, e * 5460:(e + 1) * 5460],
                                   in_=w1_in[l, e * 128:(e + 1) * 128, :]), "lw", 16)
            p.w(vec, "lw", "peI")
            p.inc(vec.memset(W2A[:, 0:1024], 0.0), "veI")
            p.w(gp, "veI")
            for ci, (off, szw) in enumerate(zip(FFN_OFF[:-1], FFN_CH)):
                p.inc(gp.dma_start(out=W2A[0:szw, ci * 1024:(ci + 1) * 1024],
                                   in_=w2_in[l, off:off + szw, :]), "lw", 16)

        def attn_pass(l, XIN, XOUT):
            load_attn_weights(l)
            for s in range(BPC):
                p.w(vec, "smu")
                vec.memset(brow[0:1, 0:1], 0.0)
                vec.drain()
                p.inc(vec.memset(b2row[0:1, 0:1], 0.0), "veI")
                for tau in range(NTI):
                    slot = tau % 2
                    pslot = 1 - slot
                    c0 = tau * TI
                    if tau > 0:
                        p.wat(sync, mk[("wo_done",)], "veI")
                        p.wat(sync, mk[("rot_done", 15)], "veI")
                    else:
                        p.w(sync, "seI", "veI", "peI")
                    p.flush(sync, "lxi")
                    for e in range(EC):
                        p.inc(sync.dma_start(
                            out=x_cur[:, e * TI:(e + 1) * TI],
                            in_=XIN[s, e * 128:(e + 1) * 128, c0:c0 + TI]), "lxi", 16)
                    for t4 in range(4):
                        p.inc(sync.dma_start(
                            out=rot4[:, t4 * TI:(t4 + 1) * TI],
                            in_=[rqc_in, rqs_in, rkc_in, rks_in][t4][:, c0:c0 + TI]),
                            "lxi", 16)
                    if tau > 0:
                        p.wat(gp, mk[("mm", 15)], "peI")
                    else:
                        p.w(gp, "peI", "seI")
                    p.flush(gp, "lqi")
                    for e in range(EC):
                        if e < 4:
                            p.inc(gp.dma_start(
                                out=xq_bf[:, e * TI:(e + 1) * TI],
                                in_=XIN[s, e * 128:(e + 1) * 128, c0:c0 + TI]),
                                "lqi", 16)
                        elif tau > 0:
                            p.inc(gp.dma_start(
                                out=xq_bf[:, e * TI:(e + 1) * TI],
                                in_=XIN[s, e * 128:(e + 1) * 128, c0 - 1:c0 + TI - 1]),
                                "lqi", 16)
                        else:
                            p.w(vec, "lqi")
                            p.inc(vec.memset(xq_bf[:, e * TI:e * TI + 1], 0.0), "veI")
                            p.w(gp, "veI")
                            p.inc(gp.dma_start(
                                out=xq_bf[:, e * TI + 1:(e + 1) * TI],
                                in_=XIN[s, e * 128:(e + 1) * 128, c0:c0 + TI - 1]),
                                "lqi", 16)
                    p.w(ten, "lqi", "lw", "seI", "veI")
                    for tc in range(2):
                        vb0, vb1 = PS[2 * tc], PS[2 * tc + 1]
                        for e in range(EC):
                            ten.matmul(vb0[:],
                                       xq_bf[:, e * TI + tc * 128: e * TI + tc * 128 + 128],
                                       WQKV[:, e * 3072 + 2048:e * 3072 + 2560],
                                       start=(e == 0), stop=(e == EC - 1))
                        for e in range(EC):
                            mm = ten.matmul(vb1[:],
                                            xq_bf[:, e * TI + tc * 128: e * TI + tc * 128 + 128],
                                            WQKV[:, e * 3072 + 2560:e * 3072 + 3072],
                                            start=(e == 0), stop=(e == EC - 1))
                        p.inc(mm, "peI")
                        p.w(scl, "peI")
                        scl.activation(out=v_ring[:, slot * 2048 + tc * 1024:
                                                  slot * 2048 + tc * 1024 + 512],
                                       in_=vb0[:], func=AF.Copy)
                        p.inc(scl.activation(out=v_ring[:, slot * 2048 + tc * 1024 + 512:
                                                        slot * 2048 + tc * 1024 + 1024],
                                             in_=vb1[:], func=AF.Copy), "seI")
                    vsnap = p.mark()
                    NCH = 16
                    mk = {}

                    def qkv_dst(c):
                        kind, cc = divmod(c, 8)
                        if kind == 0:
                            return qrot_bf[:, cc * TI:(cc + 1) * TI]
                        return k_ring[:, slot * 2048 + cc * TI:slot * 2048 + (cc + 1) * TI]

                    base_snap = p.mark()
                    for t in range(NCH + 4):
                        c = t - 3
                        if 0 <= c < NCH:
                            par = c % 2
                            rb = PS[2 + par]
                            p.wat(ten, mk[("q1", c)], "veI")
                            if c >= 2:
                                p.wat(ten, mk[("rot_done", c - 2)], "veI")
                            mm = ten.matmul(rb[:, 0:TI], rotm_bf[:],
                                            q1_bf[:, par * TI:(par + 1) * TI],
                                            start=True, stop=True)
                            p.inc(mm, "peI")
                            mk[("rot", c)] = p.mark()
                            ro = 0 if c < 8 else 2
                            p.wat(vec, mk[("rot", c)], "peI", "lxi")
                            vec.tensor_tensor(t1_sb[:, par * TI:(par + 1) * TI],
                                              q1_bf[:, par * TI:(par + 1) * TI],
                                              rot4[:, ro * TI:(ro + 1) * TI],
                                              op=ALU.mult)
                            vec.tensor_tensor(t2_sb[:, par * TI:(par + 1) * TI],
                                              rb[:, 0:TI],
                                              rot4[:, (ro + 1) * TI:(ro + 2) * TI],
                                              op=ALU.mult)
                            vec.drain()
                            p.inc(vec.tensor_tensor(qkv_dst(c),
                                                    t1_sb[:, par * TI:(par + 1) * TI],
                                                    t2_sb[:, par * TI:(par + 1) * TI],
                                                    op=ALU.add), "veI")
                            mk[("rot_done", c)] = p.mark()
                        c = t - 2
                        if 0 <= c < NCH:
                            par = c % 2
                            p.wat(ten, mk[("min", c)], "veI", "seI")
                            if c >= 2:
                                p.wat(ten, mk[("q1", c - 2)], "veI")
                            mm = ten.matmul(PS[5][:, par * TI:(par + 1) * TI],
                                            exp2_f[:],
                                            rnrow[0:2, par * TI:(par + 1) * TI],
                                            start=True, stop=True)
                            p.inc(mm, "peI")
                            mk[("exp", c)] = p.mark()
                            p.wat(vec, mk[("exp", c)], "peI")
                            p.inc(vec.tensor_tensor(
                                q1_bf[:, par * TI:(par + 1) * TI],
                                q0_sb[:, par * TI:(par + 1) * TI],
                                PS[5][:, par * TI:(par + 1) * TI], op=ALU.mult),
                                "veI")
                            mk[("q1", c)] = p.mark()
                        c = t - 1
                        if 0 <= c < NCH:
                            par = c % 2
                            p.wat(ten, mk[("evac", c)], "seI", "veI")
                            if c >= 2:
                                p.wat(ten, mk[("min", c - 2)], "veI")
                            elif ("stats_rows",) in mk:
                                p.wat(ten, mk[("stats_rows",)], "veI")
                            mm = ten.matmul(PS[4][0:2, par * TI:(par + 1) * TI],
                                            blk2_bf[:],
                                            sqi_bf[:, par * TI:(par + 1) * TI],
                                            start=True, stop=True)
                            p.inc(mm, "peI")
                            mk[("ssq", c)] = p.mark()
                            p.wat(scl, mk[("ssq", c)], "peI")
                            scl.drain()
                            scl.activation(out=PS[4][0:2, par * TI:(par + 1) * TI],
                                           in_=PS[4][0:2, par * TI:(par + 1) * TI],
                                           func=AF.Ln)
                            scl.drain()
                            p.inc(scl.activation(out=PS[4][0:2, par * TI:(par + 1) * TI],
                                                 in_=PS[4][0:2, par * TI:(par + 1) * TI],
                                                 func=AF.Exp, scale=-0.5), "seI")
                            mk[("rn", c)] = p.mark()
                            p.wat(vec, mk[("rn", c)], "seI")
                            p.inc(vec.tensor_scalar_min(
                                rnrow[0:2, par * TI:(par + 1) * TI],
                                PS[4][0:2, par * TI:(par + 1) * TI], 1e12), "veI")
                            mk[("min", c)] = p.mark()
                        c = t
                        if c < NCH:
                            kind, cc = divmod(c, 8)
                            par = c % 2
                            qb = PS[par]
                            if c >= 2:
                                p.wat(ten, mk[("evac", c - 2)], "seI", "veI")
                            else:
                                p.wat(ten, vsnap, "seI")
                            for e in range(EC):
                                mm = ten.matmul(
                                    qb[:, 0:TI],
                                    WQKV[:, e * 3072 + kind * 1024 + cc * 128:
                                         e * 3072 + kind * 1024 + cc * 128 + 128],
                                    xq_bf[:, e * TI:(e + 1) * TI],
                                    start=(e == 0), stop=(e == EC - 1))
                            p.inc(mm, "peI")
                            mk[("mm", c)] = p.mark()
                            kindc = c // 8
                            scv = (qs_sb if kindc == 0 else ks_sb)[:, l:l + 1]
                            p.wat(scl, mk[("mm", c)], "peI")
                            if c >= 2:
                                p.wat(scl, mk[("q1", c - 2)], "veI")
                            scl.activation(out=q0_sb[:, par * TI:(par + 1) * TI],
                                           in_=qb[:, 0:TI], func=AF.Copy, scale=scv)
                            p.inc(scl.activation(out=sqi_bf[:, par * TI:(par + 1) * TI],
                                                 in_=qb[:, 0:TI], func=AF.Square),
                                  "seI")
                            mk[("evac", c)] = p.mark()
                    if tau > 0:
                        live = {0: [(0, 0), (1, 1), (2, 2)],
                                1: [(1, 0), (2, 1), (3, 2)]}
                    else:
                        live = {0: [(2, 2)], 1: [(2, 1), (3, 2)]}

                    def kv(h, which):
                        sl = pslot if which < 2 else slot
                        jj = which % 2
                        cch = h // 2
                        sub = h % 2
                        return k_ring[sub * 64:sub * 64 + 64,
                                      sl * 2048 + cch * TI + jj * 128:
                                      sl * 2048 + cch * TI + jj * 128 + 128]

                    def vv(h, which):
                        sl = pslot if which < 2 else slot
                        jj = which % 2
                        return v_ring[:, sl * 2048 + jj * 1024 + h * 64:
                                      sl * 2048 + jj * 1024 + h * 64 + 64]

                    def emit_sim(hp):
                        for sub in range(2):
                            h = 2 * hp + sub
                            cch = h // 2
                            for ic in (0, 1):
                                sim = PS[2 * sub + ic]
                                qv = qrot_bf[(h % 2) * 64:(h % 2) * 64 + 64,
                                             cch * TI + ic * 128:cch * TI + (ic + 1) * 128]
                                for which, simcol in live[ic]:
                                    mm = ten.matmul(
                                        sim[:, simcol * 128:(simcol + 1) * 128],
                                        qv, kv(h, which), start=True, stop=True)
                        p.inc(mm, "peI")
                        mk[("sim", hp)] = p.mark()

                    def emit_zm(hp):
                        par = hp % 2
                        p.wat(vec, mk[("sim", hp)], "peI")
                        if hp >= 2:
                            p.wat(vec, mk[("T", hp - 2)], "peI")
                        for sub in range(2):
                            for ic in (0, 1):
                                lo = live[ic][0][1] * 128
                                p.inc(vec.scalar_tensor_tensor(
                                    out=zm_sb[:, par * 1536 + sub * 768 + ic * 384 + lo:
                                              par * 1536 + sub * 768 + ic * 384 + 384],
                                    in0=PS[2 * sub + ic][:, lo:384], scalar=QK_SCALE,
                                    in1=maskband[:, lo:384],
                                    op0=ALU.mult, op1=ALU.add), "veI")
                        mk[("zm", hp)] = p.mark()
                        if hp >= 2:
                            p.wat(vec, mk[("pn", hp - 2)], "seI")
                        vec.drain()
                        for sub in range(2):
                            for ic in (0, 1):
                                lo = live[ic][0][1] * 128
                                p.inc(vec.reduce_max(
                                    smx[:, par * 8 + sub * 2 + ic:
                                        par * 8 + sub * 2 + ic + 1],
                                    zm_sb[:, par * 1536 + sub * 768 + ic * 384 + lo:
                                          par * 1536 + sub * 768 + ic * 384 + 384],
                                    axis=AX.X, negate=True), "veI")
                        mk[("rm", hp)] = p.mark()

                    def emit_softmax(hp):
                        par = hp % 2
                        p.wat(scl, mk[("rm", hp)], "veI")
                        for sub in range(2):
                            for ic in (0, 1):
                                lo = live[ic][0][1] * 128
                                co = par * 1536 + sub * 768 + ic * 384
                                p.inc(scl.activation(
                                    out=zm_sb[:, co + lo:co + 384],
                                    in_=zm_sb[:, co + lo:co + 384],
                                    func=AF.Exp,
                                    bias=smx[:, par * 8 + sub * 2 + ic:
                                             par * 8 + sub * 2 + ic + 1],
                                    accum_out=smx[:, par * 8 + 4 + sub * 2 + ic:
                                                  par * 8 + 5 + sub * 2 + ic]), "seI")
                        mk[("exp", "a", hp)] = p.mark()
                        p.wat(vec, mk[("exp", "a", hp)], "seI")
                        vec.drain()
                        p.inc(vec.reciprocal(smx[:, par * 8 + 12 - 8:par * 8 + 16 - 8]
                                             if False else
                                             smx[:, par * 8 + 4:par * 8 + 8],
                                             smx[:, par * 8 + 4:par * 8 + 8]), "veI")
                        mk[("rcp", hp)] = p.mark()
                        p.wat(scl, mk[("rcp", hp)], "veI")
                        scl.drain()
                        for sub in range(2):
                            for ic in (0, 1):
                                lo = live[ic][0][1] * 128
                                co = par * 1536 + sub * 768 + ic * 384
                                p.inc(scl.activation(
                                    out=zm_sb[:, co + lo:co + 384],
                                    in_=zm_sb[:, co + lo:co + 384],
                                    func=AF.Copy,
                                    scale=smx[:, par * 8 + 4 + sub * 2 + ic:
                                              par * 8 + 5 + sub * 2 + ic]), "seI")
                        mk[("pn", hp)] = p.mark()

                    def emit_T(hp):
                        par = hp % 2
                        ptx = pt_sb if par == 0 else ptB
                        p.wat(ten, mk[("pn", hp)], "seI")
                        if hp >= 1:
                            p.wat(ten, mk[("pte", hp - 1)], "seI", "veI")
                        for sub in range(2):
                            PTx = PT if sub == 0 else PT2
                            for ic in (0, 1):
                                co = par * 1536 + sub * 768 + ic * 384
                                for which, simcol in live[ic]:
                                    mm = ten.transpose(
                                        PTx[:, which * 256 + ic * 128:
                                            which * 256 + (ic + 1) * 128],
                                        zm_sb[:, co + simcol * 128:
                                              co + (simcol + 1) * 128],
                                        ident_bf[:])
                        p.inc(mm, "peI")
                        mk[("T", hp)] = p.mark()
                        p.wat(scl, mk[("T", hp)], "peI")
                        if hp >= 2:
                            p.wat(scl, mk[("av", hp - 2)], "peI")
                        p.inc(scl.activation(out=ptx[:, 0:1024], in_=PT[:],
                                             func=AF.Copy), "seI")
                        p.wat(vec, mk[("T", hp)], "peI")
                        if hp >= 2:
                            p.wat(vec, mk[("av", hp - 2)], "peI")
                        p.inc(vec.tensor_copy(ptx[:, 1024:2048], PT2[:]), "veI")
                        mk[("pte", hp)] = p.mark()

                    def emit_av(hp):
                        ptx = pt_sb if hp % 2 == 0 else ptB
                        p.wat(ten, mk[("pte", hp)], "seI", "veI")
                        if hp >= 1:
                            p.wat(ten, mk[("oe", hp - 1)], "seI", "veI")
                        if tau > 0:
                            order = [(1, 0, 256, True), (0, 0, 128, False),
                                     (2, 0, 256, False), (3, 128, 256, False)]
                        else:
                            order = [(2, 0, 256, True), (3, 128, 256, False)]
                        for sub in range(2):
                            h = 2 * hp + sub
                            ob = PS[4 + sub]
                            for oi_, (which, i0, i1, st_) in enumerate(order):
                                mm = ten.matmul(
                                    ob[0:64, i0:i1],
                                    vv(h, which),
                                    ptx[:, sub * 1024 + which * 256 + i0:
                                        sub * 1024 + which * 256 + i1],
                                    start=st_, stop=(oi_ == len(order) - 1))
                        p.inc(mm, "peI")
                        mk[("av", hp)] = p.mark()
                        h0 = 2 * hp
                        p.wat(scl, mk[("av", hp)], "peI")
                        p.inc(scl.activation(
                            out=oa_bf[0:64, (h0 // 2) * TI:(h0 // 2 + 1) * TI],
                            in_=PS[4][0:64, 0:TI], func=AF.Copy), "seI")
                        p.wat(vec, mk[("av", hp)], "peI")
                        p.inc(vec.tensor_copy(
                            oa_bf[64:128, (h0 // 2) * TI:(h0 // 2 + 1) * TI],
                            PS[5][0:64, 0:TI]), "veI")
                        mk[("oe", hp)] = p.mark()

                    for hp in range(11):
                        if hp < 8:
                            if hp >= 1:
                                p.wat(ten, mk[("zm", hp - 1)], "veI")
                            if hp == 0:
                                p.wat(ten, mk[("rot_done", 14)], "veI")
                                p.wat(ten, mk[("rot_done", 15)], "veI", "seI")
                            emit_sim(hp)
                        if 2 <= hp < 10:
                            emit_T(hp - 2)
                        if hp < 8:
                            emit_zm(hp)
                            emit_softmax(hp)
                        if 3 <= hp:
                            emit_av(hp - 3)
                    p.wat(ten, mk[("oe", 7)], "seI", "veI")
                    p.wat(ten, mk[("zm", 7)], "veI")
                    wo_snap = {}
                    for oc in range(EC):
                        wb = PS[oc % 2]
                        if oc >= 2:
                            p.wat(ten, wo_snap[oc - 2], "veI", "seI")
                        for e in range(EC):
                            mm = ten.matmul(
                                wb[:, 0:TI],
                                WOI[:, e * 1024 + oc * 128:e * 1024 + oc * 128 + 128],
                                oa_bf[:, e * TI:(e + 1) * TI],
                                start=(e == 0), stop=(e == EC - 1))
                        p.inc(mm, "peI")
                        par = oc % 2
                        sxn = "sxa" if par == 0 else "sxb"
                        p.w(vec, "lxi")
                        p.wat(vec, mk[("wo", oc)] if ("wo", oc) in mk else p.mark(),
                              "peI")
                        p.w(vec, sxn)
                        p.inc(vec.tensor_tensor(oi_small[:, par * TI:(par + 1) * TI],
                                                wb[:, 0:TI],
                                                x_cur[:, oc * TI:(oc + 1) * TI],
                                                op=ALU.add), "veI")
                        p.w(scl, "veI")
                        p.inc(scl.activation(out=xst_bf[:, oc * TI:(oc + 1) * TI],
                                             in_=oi_small[:, par * TI:(par + 1) * TI],
                                             func=AF.Copy), "seI")
                        p.w(sync, "seI")
                        p.inc(sync.dma_start(
                            out=XOUT[s, oc * 128:(oc + 1) * 128, c0:c0 + TI],
                            in_=oi_small[:, par * TI:(par + 1) * TI]), sxn, 16)
                        wo_snap[oc] = p.mark()
                    mk[("wo_done",)] = p.mark()
                    p.w(scl, "peI")
                    scl.drain()
                    for e in range(EC):
                        mmq = scl.activation(out=oa_bf[:, e * TI:(e + 1) * TI],
                                             in_=xst_bf[:, e * TI:(e + 1) * TI],
                                             func=AF.Square)
                    p.inc(mmq, "seI")
                    p.w(ten, "seI", "veI")
                    for e in range(4):
                        ten.matmul(PS[4][0:1, 0:TI], onescol_bf[:],
                                   xst_bf[:, e * TI:(e + 1) * TI],
                                   start=(e == 0), stop=(e == 3))
                    for e in range(4, EC):
                        ten.matmul(PS[4][0:1, TI:2 * TI], onescol_bf[:],
                                   xst_bf[:, e * TI:(e + 1) * TI],
                                   start=(e == 4), stop=(e == 7))
                    for e in range(4):
                        ten.matmul(PS[5][0:1, 0:TI], onescol_bf[:],
                                   oa_bf[:, e * TI:(e + 1) * TI],
                                   start=(e == 0), stop=(e == 3))
                    for e in range(4, EC):
                        mm = ten.matmul(PS[5][0:1, TI:2 * TI], onescol_bf[:],
                                        oa_bf[:, e * TI:(e + 1) * TI],
                                        start=(e == 4), stop=(e == 7))
                    p.inc(mm, "peI")
                    mk[("stats_mm",)] = p.mark()
                    p.w(vec, "peI", "smu")
                    vec.drain()
                    if tau > 0:
                        vec.tensor_copy(brow[0:1, 0:1], brow[0:1, WIN:WIN + 1])
                        vec.drain()
                        vec.tensor_copy(b2row[0:1, 0:1], b2row[0:1, WIN:WIN + 1])
                        vec.drain()
                    vec.tensor_copy(brow[0:1, 1:WIN + 1], PS[4][0:1, TI:2 * TI])
                    vec.tensor_copy(b2row[0:1, 1:WIN + 1], PS[5][0:1, TI:2 * TI])
                    vec.drain()
                    vec.tensor_tensor(murow[0:1, :], PS[4][0:1, 0:TI],
                                      brow[0:1, 0:WIN], op=ALU.add)
                    vec.tensor_tensor(r2row[0:1, :], PS[5][0:1, 0:TI],
                                      b2row[0:1, 0:WIN], op=ALU.add)
                    vec.drain()
                    vec.tensor_scalar_mul(murow[0:1, :], murow[0:1, :], 1.0 / E)
                    vec.drain()
                    vec.tensor_tensor(rstdrow[0:1, :], murow[0:1, :], murow[0:1, :],
                                      op=ALU.mult)
                    vec.drain()
                    p.inc(vec.scalar_tensor_tensor(
                        out=r2row[0:1, :], in0=r2row[0:1, :], scalar=1.0 / E,
                        in1=rstdrow[0:1, :], op0=ALU.mult, op1=ALU.subtract), "veI")
                    mk[("stats_rows",)] = p.mark()
                    p.w(scl, "veI")
                    scl.drain()
                    scl.activation(out=rstdrow[0:1, :], in_=r2row[0:1, :],
                                   func=AF.Ln, bias=consts[0:1, 0:1])
                    scl.drain()
                    p.inc(scl.activation(out=rstdrow[0:1, :], in_=rstdrow[0:1, :],
                                         func=AF.Exp, scale=-0.5), "seI")
                    p.w(sync, "veI", "seI")
                    p.flush(sync, "smu")
                    p.inc(sync.dma_start(out=mu_dram[s, tau], in_=murow[0:1, :]),
                          "smu", 16)
                    p.inc(sync.dma_start(out=rstd_dram[s, tau], in_=rstdrow[0:1, :]),
                          "smu", 16)

        def ffn_pass(l, XIN, XOUT):
            load_ffn_weights(l)
            mk = {}
            for s in range(BPC):
                for tau in range(NTI):
                    c0 = tau * TI
                    if tau > 0:
                        p.wat(sync, mk[("ffn_ep", tau - 1)], "veI")
                    else:
                        p.w(sync, "seI", "veI", "peI")
                    p.flush(sync, "lxi")
                    for e in range(EC):
                        p.inc(sync.dma_start(
                            out=x_cur[:, e * TI:(e + 1) * TI],
                            in_=XIN[s, e * 128:(e + 1) * 128, c0:c0 + TI]), "lxi", 16)
                    for e in range(4, EC):
                        if tau > 0:
                            p.inc(sync.dma_start(
                                out=xsh4[:, (e - 4) * TI:(e - 3) * TI],
                                in_=XIN[s, e * 128:(e + 1) * 128, c0 - 1:c0 + TI - 1]),
                                "lxi", 16)
                        else:
                            p.w(vec, "lxi")
                            p.inc(vec.memset(xsh4[:, (e - 4) * TI:(e - 4) * TI + 1],
                                             0.0), "veI")
                            p.w(sync, "veI")
                            p.inc(sync.dma_start(
                                out=xsh4[:, (e - 4) * TI + 1:(e - 3) * TI],
                                in_=XIN[s, e * 128:(e + 1) * 128, c0:c0 + TI - 1]),
                                "lxi", 16)
                    p.flush(sync, "lmu")
                    p.flush(sync, "smu")
                    p.inc(sync.dma_start(out=murow[0:1, :], in_=mu_dram[s, tau]),
                          "lmu", 16)
                    p.inc(sync.dma_start(out=rstdrow[0:1, :], in_=rstd_dram[s, tau]),
                          "lmu", 16)
                    xnb = qrot_bf if tau % 2 == 0 else oa_bf
                    p.w(ten, "lmu", "lw")
                    if tau > 0:
                        p.wat(ten, mk[("ffn_xn", tau - 1)], "veI")
                    else:
                        p.w(ten, "veI", "seI")
                    ten.matmul(PS[4][:, 0:TI], onesrow_f[:], murow[0:1, :],
                               start=True, stop=True)
                    mm = ten.matmul(PS[5][:, 0:TI], onesrow_f[:], rstdrow[0:1, :],
                                    start=True, stop=True)
                    p.inc(mm, "peI")
                    mk[("ffn_bc", tau)] = p.mark()
                    p.wat(vec, mk[("ffn_bc", tau)], "peI")
                    p.w(vec, "lxi")
                    for e in range(EC):
                        src = (x_cur[:, e * TI:(e + 1) * TI] if e < 4 else
                               xsh4[:, (e - 4) * TI:(e - 3) * TI])
                        tl = tln_sb[:, (e % 2) * TI:(e % 2 + 1) * TI]
                        vec.tensor_tensor(tl, src, PS[4][:, 0:TI],
                                          op=ALU.subtract)
                        vec.drain()
                        vec.scalar_tensor_tensor(
                            out=tl, in0=tl,
                            scalar=lng_sb[:, l * EC + e:l * EC + e + 1],
                            in1=PS[5][:, 0:TI], op0=ALU.mult, op1=ALU.mult)
                        vec.drain()
                        p.inc(vec.tensor_scalar_add(
                            xnb[:, e * TI:(e + 1) * TI], tl,
                            lnb_sb[:, l * EC + e:l * EC + e + 1]), "veI")
                    mk[("ffn_xn", tau)] = p.mark()
                    p.wat(ten, mk[("ffn_xn", tau)], "veI")
                    p.w(ten, "seI")
                    pair_snap = {}
                    for ci, (off, szw) in enumerate(zip(FFN_OFF[:-1], FFN_CH)):
                        ab = PS[2 * (ci % 2)]
                        gb = PS[2 * (ci % 2) + 1]
                        if ci >= 2:
                            p.wat(ten, pair_snap[ci - 2], "veI", "seI")
                        for e in range(EC):
                            ten.matmul(ab[0:szw, 0:TI],
                                       W1A[:, e * 5460 + off:e * 5460 + off + szw],
                                       xnb[:, e * TI:(e + 1) * TI],
                                       start=(e == 0), stop=(e == EC - 1))
                        for e in range(EC):
                            mm = ten.matmul(gb[0:szw, 0:TI],
                                            W1A[:, e * 5460 + DI + off:
                                                e * 5460 + DI + off + szw],
                                            xnb[:, e * TI:(e + 1) * TI],
                                            start=(e == 0), stop=(e == EC - 1))
                        p.inc(mm, "peI")
                        gv = (ga_sb if ci % 2 == 0 else tln_sb)[:, 0:TI]
                        p.w(scl, "peI")
                        p.inc(scl.activation(out=gv, in_=gb[0:128, 0:TI],
                                             func=AF.Gelu,
                                             bias=b1g_sb[:, l * NFC + ci:
                                                         l * NFC + ci + 1]), "seI")
                        p.w(vec, "seI")
                        p.inc(vec.scalar_tensor_tensor(
                            out=h_bf[:, ci * TI:(ci + 1) * TI], in0=ab[0:128, 0:TI],
                            scalar=b1a_sb[:, l * NFC + ci:l * NFC + ci + 1],
                            in1=gv, op0=ALU.add, op1=ALU.mult), "veI")
                        pair_snap[ci] = p.mark()
                    p.w(ten, "veI")
                    w2_snap = {}
                    for oc in range(EC):
                        wb = PS[oc % 2]
                        par = oc % 2
                        sxn = "sxa" if par == 0 else "sxb"
                        if oc >= 2:
                            p.wat(ten, w2_snap[oc - 2], "veI")
                        else:
                            p.wat(ten, pair_snap[NFC - 2 + oc], "veI", "seI")
                        for ci in range(NFC):
                            mm = ten.matmul(
                                wb[:, 0:TI],
                                W2A[:, ci * 1024 + oc * 128:ci * 1024 + oc * 128 + 128],
                                h_bf[:, ci * TI:(ci + 1) * TI],
                                start=(ci == 0), stop=(ci == NFC - 1))
                        p.inc(mm, "peI")
                        p.w(vec, "peI")
                        p.w(vec, sxn)
                        p.inc(vec.scalar_tensor_tensor(
                            out=oi_small[:, par * TI:(par + 1) * TI], in0=wb[:, 0:TI],
                            scalar=b2_sb[:, l * EC + oc:l * EC + oc + 1],
                            in1=x_cur[:, oc * TI:(oc + 1) * TI],
                            op0=ALU.add, op1=ALU.add), "veI")
                        p.w(sync, "veI")
                        p.inc(sync.dma_start(
                            out=XOUT[s, oc * 128:(oc + 1) * 128, c0:c0 + TI],
                            in_=oi_small[:, par * TI:(par + 1) * TI]), sxn, 16)
                        w2_snap[oc] = p.mark()
                        if oc == EC - 1:
                            mk[("ffn_ep", tau)] = p.mark()

        for eng in (sync, scl, vec, ten, gp):
            p.w(eng, "lc", "lcv", "sc", "sx", "sv", "seC", "veC", "peC",
                "seA", "veA", "peA", "seB", "veB", "veB2", "peB", "lx", "ld",
                "ldg", "ve", "se", "pe")

        scl.memzero(PT[:])
        p.inc(scl.memzero(PT2[:]), "seI")

        for l in range(n_layers):
            attn_pass(l, xf1, xf2)
            if half and l == n_layers - 1:
                break
            ffn_pass(l, xf2, xf1)

        ictx.close()

    if stage == "FULL":
        fctx = ExitStack()

        def fsb(name, shape, dt):
            return fctx.enter_context(nc.sbuf_tensor(name, list(shape), dt))

        f_sb = arenaA[:, 0 * 8192:1 * 8192].bitcast(F32)
        sF_sb = arenaA[:, 1 * 8192:2 * 8192].bitcast(F32)
        g_sb = arenaA[:, 2 * 8192:3 * 8192].bitcast(F32)
        pF_sb = arenaA[:, 3 * 8192:4 * 8192].bitcast(F32)
        xF_sb = arenaA[:, 4 * 8192:5 * 8192].bitcast(F32)
        yF_sb = fsb("yF_sb", [128, 4096], F32)

        for eng in (sync, scl, vec, ten, gp):
            p.w(eng, "sxa", "sxb", "smu", "seI", "veI", "peI", "lxi", "lqi", "lmu", "lw")

        for s in range(BPC):
            for tau in range(NTO):
                c0 = tau * TT
                p.w(sync, "seF", "veF", "peF", "syF")
                p.flush(sync, "lf")
                for e in range(EC):
                    p.inc(sync.dma_start(
                        out=f_sb[:, e * 512:(e + 1) * 512],
                        in_=xf1[s, e * 128:(e + 1) * 128, c0:c0 + TT]), "lf", 16)
                    p.inc(sync.dma_start(
                        out=sF_sb[:, e * 512:(e + 1) * 512],
                        in_=s_buf[s, e * 128:(e + 1) * 128, c0:c0 + TT]), "lf", 16)
                for j in range(4):
                    p.inc(sync.dma_start(
                        out=xF_sb[:, j * 1024:(j + 1) * 1024],
                        in_=x_in[s, c0 + j * 128:c0 + (j + 1) * 128, :]), "lf", 16)
                p.w(scl, "lf", "veF")
                for e in range(EC):
                    p.inc(scl.activation(out=g_sb[:, e * 512:(e + 1) * 512],
                                         in_=f_sb[:, e * 512:(e + 1) * 512],
                                         func=AF.Silu), "seF")
                p.w(vec, "seF", "lf", "peF")
                for e in range(EC):
                    vec.drain()
                    p.inc(vec.tensor_tensor(pF_sb[:, e * 512:(e + 1) * 512],
                                            g_sb[:, e * 512:(e + 1) * 512],
                                            sF_sb[:, e * 512:(e + 1) * 512],
                                            op=ALU.mult), "veF")
                p.w(ten, "veF")
                for j in range(4):
                    pa = PS[2 * (j % 2)]
                    pb_ = PS[2 * (j % 2) + 1]
                    p.w(ten, "veF2")
                    for e in range(EC):
                        bank = pa if e < 4 else pb_
                        mm = ten.transpose(
                            bank[:, (e % 4) * 128:(e % 4) * 128 + 128],
                            pF_sb[:, e * 512 + j * 128:e * 512 + j * 128 + 128],
                            ident_f[:])
                    p.inc(mm, "peF")
                    p.w(vec, "peF")
                    vec.drain()
                    p.inc(vec.scalar_tensor_tensor(
                        out=yF_sb[:, j * 1024:j * 1024 + 512], in0=pa[:],
                        scalar=ATTN_MIX, in1=xF_sb[:, j * 1024:j * 1024 + 512],
                        op0=ALU.mult, op1=ALU.add), "veF2")
                    vec.drain()
                    p.inc(vec.scalar_tensor_tensor(
                        out=yF_sb[:, j * 1024 + 512:(j + 1) * 1024], in0=pb_[:],
                        scalar=ATTN_MIX, in1=xF_sb[:, j * 1024 + 512:(j + 1) * 1024],
                        op0=ALU.mult, op1=ALU.add), "veF2")
                p.w(sync, "veF2")
                p.flush(sync, "syF")
                for j in range(4):
                    p.inc(sync.dma_start(
                        out=y_out[s, c0 + j * 128:c0 + (j + 1) * 128, :],
                        in_=yF_sb[:, j * 1024:(j + 1) * 1024]), "syF", 16)
        fctx.close()

    p.w(sync, "sc", "sx", "sv", "sxa", "sxb", "smu", "seI", "veI", "peI")
    for name, ext in dbg_outs.items():
        p.inc(sync.dma_start(out=ext, in_=scratch_map[name]), "sdbg", 16)
    if stage != "FULL":
        p.inc(sync.dma_start(out=y_out, in_=x_in), "sy", 16)
        p.w(sync, "sy")
    else:
        p.w(sync, "syF")
    p.w(sync, "sdbg")

    ctx.close()
    return nc


def kernel(**inputs):
    nc = build_program()
    in_maps = [_layout_inputs(inputs, c) for c in range(NCORE)]
    res = run_bass_kernel_spmd(nc, in_maps, core_ids=list(range(NCORE)))
    return np.concatenate([r["y"] for r in res.results], axis=0)


# version 08
# speedup vs baseline: 1.1562x; 1.1562x over previous
import numpy as np
from contextlib import ExitStack

import concourse.bass as bass
import concourse.mybir as mybir
from concourse.bass_utils import run_bass_kernel_spmd

F32 = mybir.dt.float32
BF16 = mybir.dt.bfloat16
AF = mybir.ActivationFunctionType
ALU = mybir.AluOpType
AX = mybir.AxisListType

B = 16
S = 4096
E = 1024
NCORE = 8
BPC = B // NCORE
NH = 16
HD = 64
DEPTH = 6
WIN = 256
DI = 2730
EPS_MP = 1e-4
EPS_LN = 1e-5
QK_SCALE = 8.0
ATTN_MIX = 0.7071067811865476
NEG = -1.0e30
EC = E // 128
FFN_CH = [42] + [128] * 21
FFN_OFF = np.concatenate([[0], np.cumsum(FFN_CH)]).tolist()
NFC = len(FFN_CH)



def _host_constants():
    c = {}
    c["ident"] = np.eye(128, dtype=np.float32)
    rot = np.zeros((128, 128), dtype=np.float32)
    for h0 in (0, 64):
        for d in range(32):
            rot[h0 + d + 32, h0 + d] = -1.0
        for d in range(32, 64):
            rot[h0 + d - 32, h0 + d] = 1.0
    c["rotm"] = rot
    blk2 = np.zeros((128, 2), dtype=np.float32)
    blk2[0:64, 0] = 1.0
    blk2[64:128, 1] = 1.0
    c["blk2"] = blk2
    exp2 = np.zeros((2, 128), dtype=np.float32)
    exp2[0, 0:64] = 1.0
    exp2[1, 64:128] = 1.0
    c["exp2"] = exp2
    c["ones_col"] = np.ones((128, 1), dtype=np.float32)
    c["ones_row"] = np.ones((1, 128), dtype=np.float32)
    i = np.arange(128)[:, None]
    j = np.arange(128)[None, :]
    maskA = np.where(j >= i, 0.0, NEG).astype(np.float32)
    maskB = np.where(j <= i, 0.0, NEG).astype(np.float32)
    c["maskband"] = np.concatenate(
        [maskA, np.zeros((128, 128), np.float32), maskB], axis=1)
    half = np.arange(0, HD, 2, dtype=np.float64)
    inv_freq = 1.0 / (10000.0 ** (half / HD))
    t = np.arange(S, dtype=np.float64)
    freqs = t[:, None] * inv_freq[None, :]
    freqs = np.concatenate([freqs, freqs], axis=1)
    base_scale = (half + 0.4 * HD) / (1.4 * HD)
    power = (t - S // 2) / (WIN // 2)
    scale = base_scale[None, :] ** power[:, None]
    scale = np.concatenate([scale, scale], axis=1)
    qc = (np.cos(freqs) * scale).T.astype(np.float32)
    qs_ = (np.sin(freqs) * scale).T.astype(np.float32)
    kc = (np.cos(freqs) / scale).T.astype(np.float32)
    ks_ = (np.sin(freqs) / scale).T.astype(np.float32)
    c["rot_qc"] = np.concatenate([qc, qc], axis=0)
    c["rot_qs"] = np.concatenate([qs_, qs_], axis=0)
    c["rot_kc"] = np.concatenate([kc, kc], axis=0)
    c["rot_ks"] = np.concatenate([ks_, ks_], axis=0)
    consts = np.zeros((128, 8), dtype=np.float32)
    consts[:, 0] = EPS_LN
    consts[:, 1] = EPS_MP
    consts[:, 2] = 1.0
    c["consts"] = consts
    return c


def _layout_inputs(inp, core):
    m = {}
    m["x"] = np.ascontiguousarray(np.asarray(inp["L0_attn_query"])[2 * core: 2 * core + 2])
    m["wq"] = np.asarray(inp["L0_attn_w_q"])
    m["wk"] = np.asarray(inp["L0_attn_w_k"])
    m["wv"] = np.asarray(inp["L0_attn_w_v"])
    m["wo"] = np.asarray(inp["L0_attn_w_o"])
    m["gain128"] = np.ascontiguousarray(np.broadcast_to(
        np.asarray(inp["L0_attn_gain"], np.float32).reshape(1, 1), (128, 1)))
    m["wqkv"] = np.asarray(inp["L0_ffn_Wqkv"])
    m["woi"] = np.asarray(inp["L0_ffn_Wo"])
    m["w1"] = np.asarray(inp["L0_ffn_W1"])
    m["w2"] = np.asarray(inp["L0_ffn_W2"])
    qs = np.asarray(inp["L0_ffn_q_scale"], np.float32)
    ks = np.asarray(inp["L0_ffn_k_scale"], np.float32)
    m["qs_all"] = np.ascontiguousarray(np.tile(qs, (1, 2)).T)
    m["ks_all"] = np.ascontiguousarray(np.tile(ks, (1, 2)).T)
    lng = np.asarray(inp["L0_ffn_ln_g"], np.float32)
    lnb = np.asarray(inp["L0_ffn_ln_b"], np.float32)
    m["lng"] = np.ascontiguousarray(
        lng.reshape(DEPTH, EC, 128).transpose(2, 0, 1).reshape(128, DEPTH * EC))
    m["lnb"] = np.ascontiguousarray(
        lnb.reshape(DEPTH, EC, 128).transpose(2, 0, 1).reshape(128, DEPTH * EC))
    b1 = np.asarray(inp["L0_ffn_b1"], np.float32)
    b2 = np.asarray(inp["L0_ffn_b2"], np.float32)
    b1a = np.zeros((128, DEPTH * NFC), np.float32)
    b1g = np.zeros((128, DEPTH * NFC), np.float32)
    for l in range(DEPTH):
        for ci, (off, sz) in enumerate(zip(FFN_OFF[:-1], FFN_CH)):
            b1a[0:sz, l * NFC + ci] = b1[l, off:off + sz]
            b1g[0:sz, l * NFC + ci] = b1[l, DI + off:DI + off + sz]
    m["b1a"] = b1a
    m["b1g"] = b1g
    m["b2v"] = np.ascontiguousarray(
        b2.reshape(DEPTH, EC, 128).transpose(2, 0, 1).reshape(128, DEPTH * EC))
    m.update(_host_constants())
    return m



class Prog:
    def __init__(self, nc):
        self.nc = nc
        self.c = {}
        self.sem = {}

    def S(self, name):
        if name not in self.sem:
            self.sem[name] = self.nc.alloc_semaphore(name)
            self.c[name] = 0
        return self.sem[name]

    def inc(self, instr, name, n=1):
        instr.then_inc(self.S(name), n)
        self.c[name] += n
        return instr

    def w(self, eng, *names):
        for name in names:
            if name in self.c and self.c[name] > 0:
                eng.wait_ge(self.S(name), self.c[name])

    def mark(self):
        return dict(self.c)

    def wat(self, eng, snap, *names):
        for name in names:
            v = snap.get(name, 0)
            if v > 0:
                eng.wait_ge(self.S(name), v)

    def flush(self, q, name):
        if name in self.c and self.c[name] > 0:
            q.wait_ge(self.S(name), self.c[name])


def build_program(dbg=(), stage="FULL"):
    nc = bass.Bass("TRN2", target_bir_lowering=False)
    p = Prog(nc)

    def din(name, shape, dt=F32):
        return nc.dram_tensor(name, list(shape), dt, kind="ExternalInput").ap()

    x_in = din("x", [BPC, S, E])
    wq_in = din("wq", [E, E]); wk_in = din("wk", [E, E])
    wv_in = din("wv", [E, E]); wo_in = din("wo", [E, E])
    gain_in = din("gain128", [128, 1])
    wqkv_in = din("wqkv", [DEPTH, E, 3 * E])
    woi_in = din("woi", [DEPTH, E, E])
    w1_in = din("w1", [DEPTH, E, 2 * DI])
    w2_in = din("w2", [DEPTH, DI, E])
    qs_in = din("qs_all", [128, DEPTH]); ks_in = din("ks_all", [128, DEPTH])
    lng_in = din("lng", [128, DEPTH * EC]); lnb_in = din("lnb", [128, DEPTH * EC])
    b1a_in = din("b1a", [128, DEPTH * NFC]); b1g_in = din("b1g", [128, DEPTH * NFC])
    b2_in = din("b2v", [128, DEPTH * EC])
    ident_in = din("ident", [128, 128])
    rotm_in = din("rotm", [128, 128])
    blk2_in = din("blk2", [128, 2]); exp2_in = din("exp2", [2, 128])
    onescol_in = din("ones_col", [128, 1]); onesrow_in = din("ones_row", [1, 128])
    mask_in = din("maskband", [128, 384])
    rqc_in = din("rot_qc", [128, S]); rqs_in = din("rot_qs", [128, S])
    rkc_in = din("rot_kc", [128, S]); rks_in = din("rot_ks", [128, S])
    consts_in = din("consts", [128, 8])

    y_out = nc.dram_tensor("y", [BPC, S, E], F32, kind="ExternalOutput").ap()

    def scr(name, shape, dt=F32):
        return nc.dram_tensor(name, list(shape), dt, kind="Internal").ap()

    xn0 = scr("xn0", [BPC, E, S])
    xf1 = scr("xf1", [BPC, E, S])
    xf2 = scr("xf2", [BPC, E, S])
    s_buf = scr("s_buf", [BPC, E, S])
    v_buf = scr("v_buf", [BPC, E, S], BF16)
    mu_dram = scr("mu_dram", [BPC, 16, WIN])
    rstd_dram = scr("rstd_dram", [BPC, 16, WIN])
    scratch_map = {"xn": xn0, "s": s_buf, "v": v_buf, "xf1": xf1, "xf2": xf2}

    dbg_outs = {}
    for name in dbg:
        src = scratch_map[name]
        dbg_outs[name] = nc.dram_tensor(
            "dbg_" + name, list(src.shape), src.dtype, kind="ExternalOutput").ap()

    ctx = ExitStack()

    def sb(name, shape, dt):
        return ctx.enter_context(nc.sbuf_tensor(name, list(shape), dt))

    def psum(name, shape, dt=F32):
        return ctx.enter_context(nc.psum_tensor(name, list(shape), dt))

    ident_bf = sb("ident_bf", [128, 128], BF16)
    ident_f = sb("ident_f", [128, 128], F32)
    rotm_bf = sb("rotm_bf", [128, 128], BF16)
    blk2_bf = sb("blk2_bf", [128, 2], BF16)
    exp2_f = sb("exp2_f", [2, 128], F32)
    onescol_bf = sb("onescol_bf", [128, 1], BF16)
    onesrow_f = sb("onesrow_f", [1, 128], F32)
    maskband = sb("maskband_sb", [128, 384], F32)
    consts = sb("consts_sb", [128, 8], F32)
    gain_sb = sb("gain_sb", [128, 1], F32)
    qs_sb = sb("qs_sb", [128, DEPTH], F32)
    ks_sb = sb("ks_sb", [128, DEPTH], F32)
    lng_sb = sb("lng_sb", [128, DEPTH * EC], F32)
    lnb_sb = sb("lnb_sb", [128, DEPTH * EC], F32)
    b1a_sb = sb("b1a_sb", [128, DEPTH * NFC], F32)
    b1g_sb = sb("b1g_sb", [128, DEPTH * NFC], F32)
    b2_sb = sb("b2_sb", [128, DEPTH * EC], F32)

    arenaA = sb("arenaA", [128, 8 * 5460], BF16)
    arenaB = sb("arenaB", [128, NFC * 1024], BF16)

    WQT = arenaA[:, 0 * 8192: 1 * 8192]
    WKT = arenaA[:, 1 * 8192: 2 * 8192]
    WVT = arenaA[:, 2 * 8192: 3 * 8192]
    WOT = arenaA[:, 3 * 8192: 4 * 8192]

    PS = [psum(f"ps{i}", [128, 512], F32) for i in range(6)]
    PT = psum("pt_bf", [128, 1024], BF16)
    PT2 = psum("pt2_bf", [128, 1024], BF16)

    sync = nc.sync
    scl = nc.scalar
    vec = nc.vector
    ten = nc.tensor
    gp = nc.gpsimd

    for src, dst in [(ident_in, ident_f), (exp2_in, exp2_f), (onesrow_in, onesrow_f),
                     (mask_in, maskband), (consts_in, consts), (gain_in, gain_sb),
                     (qs_in, qs_sb), (ks_in, ks_sb), (lng_in, lng_sb),
                     (lnb_in, lnb_sb), (b1a_in, b1a_sb), (b1g_in, b1g_sb),
                     (b2_in, b2_sb)]:
        p.inc(sync.dma_start(out=dst[:], in_=src), "ld", 16)
    for src, dst in [(ident_in, ident_bf), (rotm_in, rotm_bf), (blk2_in, blk2_bf),
                     (onescol_in, onescol_bf)]:
        p.inc(gp.dma_start(out=dst[:], in_=src), "ldg", 16)

    p0ctx = ExitStack()
    w_stage = p0ctx.enter_context(nc.sbuf_tensor("w_stage", [128, 1024], F32))
    w_rows = p0ctx.enter_context(nc.sbuf_tensor("w_rows", [128, 4], F32))
    w_dump = p0ctx.enter_context(nc.sbuf_tensor("w_dump", [128, 1024], F32))
    wn_bf = p0ctx.enter_context(nc.sbuf_tensor("wn_bf", [128, 1024], BF16))

    p.w(vec, "ld", "ldg")
    p.w(ten, "ld", "ldg")
    for w_src, w_dst in [(wq_in, WQT), (wk_in, WKT), (wv_in, WVT), (wo_in, WOT)]:
        for oc in range(EC):
            p.w(sync, "se", "pe")
            p.flush(sync, "ld")
            p.inc(sync.dma_start(out=w_stage[:],
                                 in_=w_src[oc * 128:(oc + 1) * 128, :]), "ld", 16)
            p.w(scl, "ld", "ldg")
            scl.activation(out=w_dump[:], in_=w_stage[:], func=AF.Square,
                           accum_out=w_rows[:, 0:1])
            scl.drain()
            scl.activation(out=w_rows[:, 1:2], in_=w_rows[:, 0:1], func=AF.Ln)
            scl.drain()
            p.inc(scl.activation(out=w_rows[:, 1:2], in_=w_rows[:, 1:2],
                                 func=AF.Exp, scale=0.5), "se")
            p.w(vec, "se")
            vec.tensor_scalar(out=w_rows[:, 2:3], in0=w_rows[:, 1:2],
                              scalar1=1.0 / 32.0, scalar2=EPS_MP,
                              op0=ALU.mult, op1=ALU.add)
            vec.drain()
            vec.reciprocal(w_rows[:, 2:3], w_rows[:, 2:3])
            vec.drain()
            p.inc(vec.tensor_scalar(out=w_rows[:, 3:4], in0=w_rows[:, 2:3],
                                    scalar1=gain_sb[:, 0:1], scalar2=1.0 / 32.0,
                                    op0=ALU.mult, op1=ALU.mult), "ve")
            p.w(scl, "ve")
            scl.drain()
            p.inc(scl.activation(out=wn_bf[:], in_=w_stage[:], func=AF.Copy,
                                 scale=w_rows[:, 3:4]), "se")
            p.w(ten, "se", "ve")
            for e8 in range(EC):
                mm = ten.transpose(PT[:, e8 * 128:(e8 + 1) * 128],
                                   wn_bf[:, e8 * 128:(e8 + 1) * 128], ident_bf[:])
            p.inc(mm, "pe")
            p.w(vec, "pe")
            p.inc(vec.tensor_copy(
                w_dst.rearrange("p (e o) -> p e o", o=1024)
                [:, :, oc * 128:(oc + 1) * 128],
                PT[:].rearrange("p (e o) -> p e o", o=128)), "ve")

    p0ctx.close()

    TT = 512
    NTO = S // TT
    outctx = ExitStack()

    def osb(name, shape, dt):
        return outctx.enter_context(nc.sbuf_tensor(name, list(shape), dt))

    xt_sb = osb("xt_sb", [128, 4096], F32)
    sqs_sb = osb("sqs_sb", [128, 1024], F32)
    ln_sc = osb("ln_sc", [128, 8], F32)
    xn_tm = osb("xn_tm", [128, 4096], F32)
    xn_fm = osb("xn_fm", [128, 8 * 512], F32)
    xn_bf = osb("xn_bf", [128, 8 * 512], BF16)
    qn_sb = osb("qn_sb", [128, 1024], BF16)
    kn_sb = osb("kn_sb", [128, 1024], BF16)
    nrm_sc = osb("nrm_sc", [128, 48], F32)
    v_sb = osb("v_sb", [128, 512], BF16)
    attnb = osb("attnb", [128, 512], BF16)
    attnT = osb("attnT", [128, 512], BF16)
    sm_sc = osb("sm_sc", [128, 4], F32)
    o_bf = arenaB[:, 0:4096]
    s_out = osb("s_out", [128, 512], F32)
    scores_acc = osb("scores_acc", [128, 512], F32)

    SCORES = PS[5]

    def phase_A(s):
        p.w(sync, "ld", "se", "ve", "pe")
        p.w(ten, "ve", "se", "ld", "ldg")
        p.w(scl, "ve", "ld")
        p.w(vec, "se", "ld")
        for tau in range(NTO):
            xt = xt_sb[:, 0:4096]
            xnt = xn_tm[:, 0:4096]
            p.w(sync, "seA", "veA", "peA")
            p.flush(sync, "lx")
            for j in range(4):
                p.inc(sync.dma_start(
                    out=xt[:, j * 1024:(j + 1) * 1024],
                    in_=x_in[s, tau * TT + j * 128: tau * TT + (j + 1) * 128, :]),
                    "lx", 16)
            p.w(scl, "lx")
            p.w(vec, "lx")
            for j in range(4):
                xj = xt[:, j * 1024:(j + 1) * 1024]
                p.inc(scl.activation(out=sqs_sb[:], in_=xj, func=AF.Square,
                                     accum_out=ln_sc[:, 0:1]), "seA")
                p.w(vec, "seA")
                vec.reduce_sum(ln_sc[:, 1:2], xj, axis=AX.X)
                vec.drain()
                vec.tensor_scalar_mul(ln_sc[:, 2:3], ln_sc[:, 1:2], 1.0 / E)
                vec.drain()
                vec.tensor_tensor(ln_sc[:, 3:4], ln_sc[:, 2:3], ln_sc[:, 2:3],
                                  op=ALU.mult)
                vec.drain()
                p.inc(vec.scalar_tensor_tensor(
                    out=ln_sc[:, 4:5], in0=ln_sc[:, 0:1], scalar=1.0 / E,
                    in1=ln_sc[:, 3:4], op0=ALU.mult, op1=ALU.subtract), "veA")
                p.w(scl, "veA")
                scl.activation(out=ln_sc[:, 5:6], in_=ln_sc[:, 4:5], func=AF.Ln,
                               bias=consts[:, 0:1])
                scl.drain()
                p.inc(scl.activation(out=ln_sc[:, 5:6], in_=ln_sc[:, 5:6],
                                     func=AF.Exp, scale=-0.5), "seA")
                p.w(vec, "seA")
                vec.drain()
                p.inc(vec.scalar_tensor_tensor(
                    out=ln_sc[:, 6:7], in0=ln_sc[:, 2:3], scalar=-1.0,
                    in1=ln_sc[:, 5:6], op0=ALU.mult, op1=ALU.mult), "veA")
                vec.drain()
                p.inc(vec.tensor_scalar(out=xnt[:, j * 1024:(j + 1) * 1024],
                                        in0=xj, scalar1=ln_sc[:, 5:6],
                                        scalar2=ln_sc[:, 6:7],
                                        op0=ALU.mult, op1=ALU.add), "veA")
            if stage == "A1":
                continue
            p.w(ten, "seA")
            for ec in range(EC):
                bank = PS[ec % 2]
                if stage != "A2noT":
                    p.w(ten, "seA", "veA")
                    for j in range(4):
                        mm = ten.transpose(bank[:, j * 128:(j + 1) * 128],
                                           xnt[:, j * 1024 + ec * 128: j * 1024 + ec * 128 + 128],
                                           ident_f[:])
                    p.inc(mm, "peA")
                    p.w(scl, "peA", "sx")
                    p.w(vec, "peA")
                    p.inc(scl.activation(out=xn_fm[:, ec * 512:(ec + 1) * 512],
                                         in_=bank[:], func=AF.Copy), "seA")
                    p.w(vec, "seA")
                    p.inc(vec.tensor_copy(xn_bf[:, ec * 512:(ec + 1) * 512],
                                          xn_fm[:, ec * 512:(ec + 1) * 512]), "veA")
                else:
                    p.w(scl, "sx")
                    p.inc(scl.activation(out=xn_fm[:, ec * 512:(ec + 1) * 512],
                                         in_=xnt[:, ec * 512:(ec + 1) * 512],
                                         func=AF.Copy), "seA")
                    p.inc(vec.tensor_copy(xn_bf[:, ec * 512:(ec + 1) * 512],
                                          xnt[:, ec * 512:(ec + 1) * 512]), "veA")
            if stage in ("A2noS", "A2noV"):
                continue
            p.w(sync, "seA")
            p.flush(sync, "sx")
            for ec in range(EC):
                p.inc(sync.dma_start(
                    out=xn0[s, ec * 128:(ec + 1) * 128, tau * TT:(tau + 1) * TT],
                    in_=xn_fm[:, ec * 512:(ec + 1) * 512]), "sx", 16)
                p.inc(sync.dma_start(
                    out=xf1[s, ec * 128:(ec + 1) * 128, tau * TT:(tau + 1) * TT],
                    in_=xn_fm[:, ec * 512:(ec + 1) * 512]), "sx", 16)
            if stage in ("A2", "A2noT", "A2noS", "A2noV"):
                continue
            for tc in range(4):
                qb, qb2 = PS[0], PS[1]
                kb, kb2 = PS[2], PS[3]
                lhs = lambda ecc: xn_bf[:, ecc * 512 + tc * 128: ecc * 512 + tc * 128 + 128]
                p.w(ten, "veA", "seA")
                for ecc in range(EC):
                    ten.matmul(qb[:], lhs(ecc), WQT[:, ecc * 1024:ecc * 1024 + 512],
                               start=(ecc == 0), stop=(ecc == EC - 1))
                for ecc in range(EC):
                    ten.matmul(qb2[:], lhs(ecc), WQT[:, ecc * 1024 + 512:(ecc + 1) * 1024],
                               start=(ecc == 0), stop=(ecc == EC - 1))
                for ecc in range(EC):
                    ten.matmul(kb[:], lhs(ecc), WKT[:, ecc * 1024:ecc * 1024 + 512],
                               start=(ecc == 0), stop=(ecc == EC - 1))
                for ecc in range(EC):
                    mm = ten.matmul(kb2[:], lhs(ecc), WKT[:, ecc * 1024 + 512:(ecc + 1) * 1024],
                                    start=(ecc == 0), stop=(ecc == EC - 1))
                p.inc(mm, "peA")
                p.w(scl, "peA")
                p.w(vec, "peA")
                for proj, (b0, b1k, dst) in enumerate(
                        [(qb, qb2, qn_sb), (kb, kb2, kn_sb)]):
                    scl.activation(out=sqs_sb[:, 0:512], in_=b0[:], func=AF.Square)
                    p.inc(scl.activation(out=sqs_sb[:, 512:1024], in_=b1k[:],
                                         func=AF.Square), "seA")
                    p.w(vec, "seA")
                    p.inc(vec.reduce_sum(
                        nrm_sc[:, 0:16],
                        sqs_sb[:].rearrange("p (h d) -> p h d", d=64), axis=AX.X),
                        "veA")
                    p.w(scl, "veA")
                    scl.drain()
                    scl.activation(out=nrm_sc[:, 16:32], in_=nrm_sc[:, 0:16],
                                   func=AF.Ln)
                    scl.drain()
                    p.inc(scl.activation(out=nrm_sc[:, 16:32], in_=nrm_sc[:, 16:32],
                                         func=AF.Exp, scale=0.5), "seA")
                    p.w(vec, "seA")
                    vec.drain()
                    vec.tensor_scalar(out=nrm_sc[:, 32:48], in0=nrm_sc[:, 16:32],
                                      scalar1=1.0 / 8.0, scalar2=EPS_MP,
                                      op0=ALU.mult, op1=ALU.add)
                    vec.drain()
                    vec.reciprocal(nrm_sc[:, 32:48], nrm_sc[:, 32:48])
                    vec.drain()
                    if proj == 1:
                        vec.tensor_scalar_mul(nrm_sc[:, 32:48], nrm_sc[:, 32:48],
                                              1.0 / 64.0)
                        vec.drain()
                    rb = nrm_sc[:, 32:48].rearrange("p (h o) -> p h o", o=1)\
                        .broadcast_to([128, 16, 64])
                    dv = dst[:].rearrange("p (h d) -> p h d", d=64)
                    vec.tensor_tensor(dv[:, 0:8, :],
                                      b0[:].rearrange("p (h d) -> p h d", d=64),
                                      rb[:, 0:8, :], op=ALU.mult)
                    p.inc(vec.tensor_tensor(dv[:, 8:16, :],
                                            b1k[:].rearrange("p (h d) -> p h d", d=64),
                                            rb[:, 8:16, :], op=ALU.mult), "veA")
                if stage == "A3":
                    continue
                p.w(ten, "veA")
                for h in range(NH):
                    mm = ten.matmul(
                        SCORES[(h % 2) * 64:(h % 2) * 64 + 64,
                               (h // 2) * 64:(h // 2) * 64 + 64],
                        qn_sb[:, h * 64:(h + 1) * 64],
                        kn_sb[:, h * 64:(h + 1) * 64],
                        start=True, stop=True)
                p.inc(mm, "peA")
                p.w(vec, "peA")
                vec.drain()
                if tau == 0 and tc == 0:
                    p.inc(vec.tensor_copy(scores_acc[:], SCORES[:]), "veA")
                else:
                    p.inc(vec.tensor_tensor(scores_acc[:], scores_acc[:],
                                            SCORES[:], op=ALU.add), "veA")
            if stage in ("A3", "A4"):
                continue
            for oc in range(EC):
                vb = PS[4]
                p.w(ten, "seA")
                for ecc in range(EC):
                    mm = ten.matmul(vb[:],
                                    WVT[:, ecc * 1024 + oc * 128: ecc * 1024 + oc * 128 + 128],
                                    xn_bf[:, ecc * 512:(ecc + 1) * 512],
                                    start=(ecc == 0), stop=(ecc == EC - 1))
                p.inc(mm, "peA")
                p.w(scl, "peA", "sv")
                p.inc(scl.activation(out=v_sb[:], in_=vb[:], func=AF.Copy), "seA")
                p.w(gp, "seA")
                p.flush(gp, "sv")
                p.inc(gp.dma_start(
                    out=v_buf[s, oc * 128:(oc + 1) * 128, tau * TT:(tau + 1) * TT],
                    in_=v_sb[:]), "sv", 16)

    def phase_B(s):
        p.w(vec, "peA")
        p.w(scl, "peA", "veA")
        vec.drain()
        for h in range(NH):
            pb = (h % 2) * 64
            co = (h // 2) * 64
            sc_v = scores_acc[pb:pb + 64, co:co + 64]
            p.inc(vec.reduce_max(sm_sc[pb:pb + 64, 0:1], sc_v, axis=AX.X,
                                 negate=True), "veB")
            p.w(scl, "veB")
            p.inc(scl.activation(out=sqs_sb[pb:pb + 64, co:co + 64], in_=sc_v,
                                 func=AF.Exp, bias=sm_sc[pb:pb + 64, 0:1],
                                 accum_out=sm_sc[pb:pb + 64, 1:2]), "seB")
            p.w(vec, "seB")
            p.inc(vec.reciprocal(sm_sc[pb:pb + 64, 2:3], sm_sc[pb:pb + 64, 1:2]),
                  "veB")
            p.w(scl, "veB")
            scl.drain()
            p.inc(scl.activation(out=attnb[pb:pb + 64, co:co + 64],
                                 in_=sqs_sb[pb:pb + 64, co:co + 64], func=AF.Copy,
                                 scale=sm_sc[pb:pb + 64, 2:3]), "seB")
        p.w(ten, "seB", "veB2")
        for h in range(NH):
            pb = (h % 2) * 64
            co = (h // 2) * 64
            mm = ten.transpose(PT[pb:pb + 64, co:co + 64],
                               attnb[pb:pb + 64, co:co + 64],
                               ident_bf[pb:pb + 64, pb:pb + 64])
        p.inc(mm, "peB")
        p.w(vec, "peB")
        p.inc(vec.tensor_copy(attnT[:], PT[:, 0:512]), "veB2")

    def phase_C(s):
        p.w(gp, "sv")
        p.w(sync, "sx")
        for tau in range(NTO):
            p.w(sync, "veC", "peC", "veA", "seA", "peA")
            p.w(gp, "peC", "veA", "peA")
            p.flush(sync, "lc")
            p.flush(gp, "lcv")
            for ec in range(EC):
                p.inc(sync.dma_start(
                    out=xn_fm[:, ec * 512:(ec + 1) * 512],
                    in_=xn0[s, ec * 128:(ec + 1) * 128, tau * TT:(tau + 1) * TT]),
                    "lc", 16)
                p.inc(gp.dma_start(
                    out=xn_bf[:, ec * 512:(ec + 1) * 512],
                    in_=v_buf[s, ec * 128:(ec + 1) * 128, tau * TT:(tau + 1) * TT]),
                    "lcv", 16)
            p.w(ten, "lcv", "veB2")
            for hc in range(EC):
                ob = PS[hc % 2]
                p.w(ten, "seC")
                for sub in range(2):
                    h = hc * 2 + sub
                    mm = ten.matmul(ob[sub * 64:sub * 64 + 64, :],
                                    attnT[(h % 2) * 64:(h % 2) * 64 + 64,
                                          (h // 2) * 64:(h // 2) * 64 + 64],
                                    xn_bf[:, hc * 512:(hc + 1) * 512]
                                    [(h % 2) * 64:(h % 2) * 64 + 64, :],
                                    start=True, stop=True)
                p.inc(mm, "peC")
                p.w(scl, "peC")
                p.inc(scl.activation(out=o_bf[:, hc * 512:(hc + 1) * 512],
                                     in_=ob[:], func=AF.Copy), "seC")
            p.w(ten, "seC", "lc")
            for oc in range(EC):
                wb = PS[2 + oc % 2]
                p.w(ten, "veC")
                for ecc in range(EC):
                    mm = ten.matmul(wb[:],
                                    WOT[:, ecc * 1024 + oc * 128: ecc * 1024 + oc * 128 + 128],
                                    o_bf[:, ecc * 512:(ecc + 1) * 512],
                                    start=(ecc == 0), stop=(ecc == EC - 1))
                p.inc(mm, "peC")
                p.w(vec, "peC", "sc")
                vec.drain()
                p.inc(vec.tensor_tensor(s_out[:], wb[:],
                                        xn_fm[:, oc * 512:(oc + 1) * 512],
                                        op=ALU.add), "veC")
                p.w(sync, "veC")
                p.flush(sync, "sc")
                p.inc(sync.dma_start(
                    out=s_buf[s, oc * 128:(oc + 1) * 128, tau * TT:(tau + 1) * TT],
                    in_=s_out[:]), "sc", 16)

    ostage = stage if (stage.startswith("A") or stage == "P0") else "ABC"
    for s in range(BPC):
        if "A" in ostage:
            phase_A(s)
        if "B" in ostage:
            phase_B(s)
        if "C" in ostage:
            phase_C(s)
        if ostage == "P0":
            break

    outctx.close()

    if stage.startswith("L") or stage == "FULL":
        if stage == "FULL":
            n_layers, half = DEPTH, False
        elif stage.endswith("A"):
            n_layers, half = int(stage[1:-1]), True
        else:
            n_layers, half = int(stage[1:]), False

        ictx = ExitStack()

        def isb(name, shape, dt):
            return ictx.enter_context(nc.sbuf_tensor(name, list(shape), dt))

        TI = 256
        NTI = S // TI
        x_cur = isb("x_cur", [128, 8 * TI], F32)
        xq_bf = isb("xq_bf", [128, 8 * TI], BF16)
        xsh4 = isb("xsh4", [128, 4 * TI], F32)
        q0_sb = isb("q0_sb", [128, 2 * TI], F32)
        sqi_bf = isb("sqi_bf", [128, 2 * TI], BF16)
        nrmi = isb("nrmi", [128, 16], F32)
        q1_bf = isb("q1_bf", [128, 2 * TI], BF16)
        scrX = isb("scrX", [128, 4 * TI], F32)
        t1_sb = scrX[:, 0:2 * TI]
        t2_sb = scrX[:, 2 * TI:4 * TI]
        ptB = scrX.bitcast(BF16)
        rnrow = isb("rnrow", [2, 2 * TI], F32)
        qrot_bf = isb("qrot_bf", [128, 8 * TI], BF16)
        kv_ring = isb("kv_ring", [128, 4 * 8 * TI], BF16)
        k_ring = kv_ring[:, 0:2 * 8 * TI]
        v_ring = kv_ring[:, 2 * 8 * TI:4 * 8 * TI]
        rot4 = isb("rot4", [128, 4 * TI], F32)
        zm_sb = isb("zm_sb", [128, 2 * 2 * 2 * 384], BF16)
        smx = isb("smx", [128, 16], F32)
        pt_sb = isb("pt_sb", [128, 2048], BF16)
        oa_bf = isb("oa_bf", [128, 8 * TI], BF16)
        oi_small = isb("oi_small", [128, 2 * TI], F32)
        xst_bf = qrot_bf
        brow = isb("brow", [1, WIN + 1], F32)
        b2row = isb("b2row", [1, WIN + 1], F32)
        r2row = isb("r2row", [1, WIN], F32)
        murow = isb("murow", [1, WIN], F32)
        rstdrow = isb("rstdrow", [1, WIN], F32)
        xn_bfi = qrot_bf
        ga_sb = t1_sb
        tln_sb = t2_sb
        h_bf = kv_ring[:, 0:NFC * TI]

        WQKV = arenaA[:, 0:8 * 3072]
        WOI = arenaA[:, 8 * 3072:8 * 3072 + 8 * 1024]
        W1A = arenaA
        W2A = arenaB

        def load_attn_weights(l):
            p.w(gp, "peI")
            p.flush(gp, "lw")
            for e in range(EC):
                p.inc(gp.dma_start(out=WQKV[:, e * 3072:(e + 1) * 3072],
                                   in_=wqkv_in[l, e * 128:(e + 1) * 128, :]), "lw", 16)
            for e in range(EC):
                p.inc(gp.dma_start(out=WOI[:, e * 1024:(e + 1) * 1024],
                                   in_=woi_in[l, e * 128:(e + 1) * 128, :]), "lw", 16)

        def load_ffn_weights(l):
            p.w(gp, "peI")
            p.flush(gp, "lw")
            for e in range(EC):
                p.inc(gp.dma_start(out=W1A[:, e * 5460:(e + 1) * 5460],
                                   in_=w1_in[l, e * 128:(e + 1) * 128, :]), "lw", 16)
            p.w(vec, "lw", "peI")
            p.inc(vec.memset(W2A[:, 0:1024], 0.0), "veI")
            p.w(gp, "veI")
            for ci, (off, szw) in enumerate(zip(FFN_OFF[:-1], FFN_CH)):
                p.inc(gp.dma_start(out=W2A[0:szw, ci * 1024:(ci + 1) * 1024],
                                   in_=w2_in[l, off:off + szw, :]), "lw", 16)

        def attn_pass(l, XIN, XOUT):
            load_attn_weights(l)
            for s in range(BPC):
                p.w(vec, "smu")
                vec.memset(brow[0:1, 0:1], 0.0)
                vec.drain()
                p.inc(vec.memset(b2row[0:1, 0:1], 0.0), "veI")
                for tau in range(NTI):
                    slot = tau % 2
                    pslot = 1 - slot
                    c0 = tau * TI
                    if tau > 0:
                        p.wat(sync, mk[("wo_done",)], "veI")
                        p.wat(sync, mk[("rot_done", 15)], "veI")
                    else:
                        p.w(sync, "seI", "veI", "peI")
                    p.flush(sync, "lxi")
                    for e in range(EC):
                        p.inc(sync.dma_start(
                            out=x_cur[:, e * TI:(e + 1) * TI],
                            in_=XIN[s, e * 128:(e + 1) * 128, c0:c0 + TI]), "lxi", 16)
                    for t4 in range(4):
                        p.inc(sync.dma_start(
                            out=rot4[:, t4 * TI:(t4 + 1) * TI],
                            in_=[rqc_in, rqs_in, rkc_in, rks_in][t4][:, c0:c0 + TI]),
                            "lxi", 16)
                    if tau > 0:
                        p.wat(gp, mk[("mm", 15)], "peI")
                    else:
                        p.w(gp, "peI", "seI")
                    p.flush(gp, "lqi")
                    for e in range(EC):
                        if e < 4:
                            p.inc(gp.dma_start(
                                out=xq_bf[:, e * TI:(e + 1) * TI],
                                in_=XIN[s, e * 128:(e + 1) * 128, c0:c0 + TI]),
                                "lqi", 16)
                        elif tau > 0:
                            p.inc(gp.dma_start(
                                out=xq_bf[:, e * TI:(e + 1) * TI],
                                in_=XIN[s, e * 128:(e + 1) * 128, c0 - 1:c0 + TI - 1]),
                                "lqi", 16)
                        else:
                            p.w(vec, "lqi")
                            p.inc(vec.memset(xq_bf[:, e * TI:e * TI + 1], 0.0), "veI")
                            p.w(gp, "veI")
                            p.inc(gp.dma_start(
                                out=xq_bf[:, e * TI + 1:(e + 1) * TI],
                                in_=XIN[s, e * 128:(e + 1) * 128, c0:c0 + TI - 1]),
                                "lqi", 16)
                    p.w(ten, "lqi", "lw", "seI", "veI")
                    for tc in range(2):
                        vb0, vb1 = PS[2 * tc], PS[2 * tc + 1]
                        for e in range(EC):
                            ten.matmul(vb0[:],
                                       xq_bf[:, e * TI + tc * 128: e * TI + tc * 128 + 128],
                                       WQKV[:, e * 3072 + 2048:e * 3072 + 2560],
                                       start=(e == 0), stop=(e == EC - 1))
                        for e in range(EC):
                            mm = ten.matmul(vb1[:],
                                            xq_bf[:, e * TI + tc * 128: e * TI + tc * 128 + 128],
                                            WQKV[:, e * 3072 + 2560:e * 3072 + 3072],
                                            start=(e == 0), stop=(e == EC - 1))
                        p.inc(mm, "peI")
                        p.w(scl, "peI")
                        scl.activation(out=v_ring[:, slot * 2048 + tc * 1024:
                                                  slot * 2048 + tc * 1024 + 512],
                                       in_=vb0[:], func=AF.Copy)
                        p.inc(scl.activation(out=v_ring[:, slot * 2048 + tc * 1024 + 512:
                                                        slot * 2048 + tc * 1024 + 1024],
                                             in_=vb1[:], func=AF.Copy), "seI")
                    vsnap = p.mark()
                    NCH = 16
                    mk = {}

                    def qkv_dst(c):
                        kind, cc = divmod(c, 8)
                        if kind == 0:
                            return qrot_bf[:, cc * TI:(cc + 1) * TI]
                        return k_ring[:, slot * 2048 + cc * TI:slot * 2048 + (cc + 1) * TI]

                    base_snap = p.mark()
                    for t in range(NCH + 4):
                        c = t - 3
                        if 0 <= c < NCH:
                            par = c % 2
                            rb = PS[2 + par]
                            p.wat(ten, mk[("q1", c)], "veI")
                            if c >= 2:
                                p.wat(ten, mk[("rot_done", c - 2)], "veI")
                            mm = ten.matmul(rb[:, 0:TI], rotm_bf[:],
                                            q1_bf[:, par * TI:(par + 1) * TI],
                                            start=True, stop=True)
                            p.inc(mm, "peI")
                            mk[("rot", c)] = p.mark()
                            ro = 0 if c < 8 else 2
                            p.wat(vec, mk[("rot", c)], "peI", "lxi")
                            vec.tensor_tensor(t1_sb[:, par * TI:(par + 1) * TI],
                                              q1_bf[:, par * TI:(par + 1) * TI],
                                              rot4[:, ro * TI:(ro + 1) * TI],
                                              op=ALU.mult)
                            vec.tensor_tensor(t2_sb[:, par * TI:(par + 1) * TI],
                                              rb[:, 0:TI],
                                              rot4[:, (ro + 1) * TI:(ro + 2) * TI],
                                              op=ALU.mult)
                            vec.drain()
                            p.inc(vec.tensor_tensor(qkv_dst(c),
                                                    t1_sb[:, par * TI:(par + 1) * TI],
                                                    t2_sb[:, par * TI:(par + 1) * TI],
                                                    op=ALU.add), "veI")
                            mk[("rot_done", c)] = p.mark()
                        c = t - 2
                        if 0 <= c < NCH:
                            par = c % 2
                            p.wat(ten, mk[("min", c)], "veI", "seI")
                            if c >= 2:
                                p.wat(ten, mk[("q1", c - 2)], "veI")
                            mm = ten.matmul(PS[5][:, par * TI:(par + 1) * TI],
                                            exp2_f[:],
                                            rnrow[0:2, par * TI:(par + 1) * TI],
                                            start=True, stop=True)
                            p.inc(mm, "peI")
                            mk[("exp", c)] = p.mark()
                            p.wat(vec, mk[("exp", c)], "peI")
                            p.inc(vec.tensor_tensor(
                                q1_bf[:, par * TI:(par + 1) * TI],
                                q0_sb[:, par * TI:(par + 1) * TI],
                                PS[5][:, par * TI:(par + 1) * TI], op=ALU.mult),
                                "veI")
                            mk[("q1", c)] = p.mark()
                        c = t - 1
                        if 0 <= c < NCH:
                            par = c % 2
                            p.wat(ten, mk[("evac", c)], "seI", "veI")
                            if c >= 2:
                                p.wat(ten, mk[("min", c - 2)], "veI")
                            elif ("stats_rows",) in mk:
                                p.wat(ten, mk[("stats_rows",)], "veI")
                            mm = ten.matmul(PS[4][0:2, par * TI:(par + 1) * TI],
                                            blk2_bf[:],
                                            sqi_bf[:, par * TI:(par + 1) * TI],
                                            start=True, stop=True)
                            p.inc(mm, "peI")
                            mk[("ssq", c)] = p.mark()
                            p.wat(scl, mk[("ssq", c)], "peI")
                            scl.drain()
                            scl.activation(out=PS[4][0:2, par * TI:(par + 1) * TI],
                                           in_=PS[4][0:2, par * TI:(par + 1) * TI],
                                           func=AF.Ln)
                            scl.drain()
                            p.inc(scl.activation(out=PS[4][0:2, par * TI:(par + 1) * TI],
                                                 in_=PS[4][0:2, par * TI:(par + 1) * TI],
                                                 func=AF.Exp, scale=-0.5), "seI")
                            mk[("rn", c)] = p.mark()
                            p.wat(vec, mk[("rn", c)], "seI")
                            p.inc(vec.tensor_scalar_min(
                                rnrow[0:2, par * TI:(par + 1) * TI],
                                PS[4][0:2, par * TI:(par + 1) * TI], 1e12), "veI")
                            mk[("min", c)] = p.mark()
                        c = t
                        if c < NCH:
                            kind, cc = divmod(c, 8)
                            par = c % 2
                            qb = PS[par]
                            if c >= 2:
                                p.wat(ten, mk[("evac", c - 2)], "seI", "veI")
                            else:
                                p.wat(ten, vsnap, "seI")
                            for e in range(EC):
                                mm = ten.matmul(
                                    qb[:, 0:TI],
                                    WQKV[:, e * 3072 + kind * 1024 + cc * 128:
                                         e * 3072 + kind * 1024 + cc * 128 + 128],
                                    xq_bf[:, e * TI:(e + 1) * TI],
                                    start=(e == 0), stop=(e == EC - 1))
                            p.inc(mm, "peI")
                            mk[("mm", c)] = p.mark()
                            kindc = c // 8
                            scv = (qs_sb if kindc == 0 else ks_sb)[:, l:l + 1]
                            p.wat(scl, mk[("mm", c)], "peI")
                            if c >= 2:
                                p.wat(scl, mk[("q1", c - 2)], "veI")
                            scl.activation(out=q0_sb[:, par * TI:(par + 1) * TI],
                                           in_=qb[:, 0:TI], func=AF.Copy, scale=scv)
                            p.inc(scl.activation(out=sqi_bf[:, par * TI:(par + 1) * TI],
                                                 in_=qb[:, 0:TI], func=AF.Square),
                                  "seI")
                            mk[("evac", c)] = p.mark()
                    if tau > 0:
                        live = {0: [(0, 0), (1, 1), (2, 2)],
                                1: [(1, 0), (2, 1), (3, 2)]}
                    else:
                        live = {0: [(2, 2)], 1: [(2, 1), (3, 2)]}

                    def kv(h, which):
                        sl = pslot if which < 2 else slot
                        jj = which % 2
                        cch = h // 2
                        sub = h % 2
                        return k_ring[sub * 64:sub * 64 + 64,
                                      sl * 2048 + cch * TI + jj * 128:
                                      sl * 2048 + cch * TI + jj * 128 + 128]

                    def vv(h, which):
                        sl = pslot if which < 2 else slot
                        jj = which % 2
                        return v_ring[:, sl * 2048 + jj * 1024 + h * 64:
                                      sl * 2048 + jj * 1024 + h * 64 + 64]

                    def emit_sim(hp):
                        for sub in range(2):
                            h = 2 * hp + sub
                            cch = h // 2
                            for ic in (0, 1):
                                sim = PS[2 * sub + ic]
                                qv = qrot_bf[(h % 2) * 64:(h % 2) * 64 + 64,
                                             cch * TI + ic * 128:cch * TI + (ic + 1) * 128]
                                for which, simcol in live[ic]:
                                    mm = ten.matmul(
                                        sim[:, simcol * 128:(simcol + 1) * 128],
                                        qv, kv(h, which), start=True, stop=True)
                        p.inc(mm, "peI")
                        mk[("sim", hp)] = p.mark()

                    def emit_zm(hp):
                        par = hp % 2
                        p.wat(vec, mk[("sim", hp)], "peI")
                        if hp >= 2:
                            p.wat(vec, mk[("T", hp - 2)], "peI")
                            p.wat(vec, mk[("pn", hp - 2)], "seI")
                        for sub in range(2):
                            for ic in (0, 1):
                                lo = live[ic][0][1] * 128
                                p.inc(vec.scalar_tensor_tensor(
                                    out=zm_sb[:, par * 1536 + sub * 768 + ic * 384 + lo:
                                              par * 1536 + sub * 768 + ic * 384 + 384],
                                    in0=PS[2 * sub + ic][:, lo:384], scalar=QK_SCALE,
                                    in1=maskband[:, lo:384],
                                    op0=ALU.mult, op1=ALU.add), "veI")
                        mk[("zm", hp)] = p.mark()
                        vec.drain()
                        for sub in range(2):
                            for ic in (0, 1):
                                lo = live[ic][0][1] * 128
                                p.inc(vec.reduce_max(
                                    smx[:, par * 8 + sub * 2 + ic:
                                        par * 8 + sub * 2 + ic + 1],
                                    zm_sb[:, par * 1536 + sub * 768 + ic * 384 + lo:
                                          par * 1536 + sub * 768 + ic * 384 + 384],
                                    axis=AX.X, negate=True), "veI")
                        mk[("rm", hp)] = p.mark()

                    def emit_softmax(hp):
                        par = hp % 2
                        p.wat(scl, mk[("rm", hp)], "veI")
                        for sub in range(2):
                            for ic in (0, 1):
                                lo = live[ic][0][1] * 128
                                co = par * 1536 + sub * 768 + ic * 384
                                p.inc(scl.activation(
                                    out=zm_sb[:, co + lo:co + 384],
                                    in_=zm_sb[:, co + lo:co + 384],
                                    func=AF.Exp,
                                    bias=smx[:, par * 8 + sub * 2 + ic:
                                             par * 8 + sub * 2 + ic + 1],
                                    accum_out=smx[:, par * 8 + 4 + sub * 2 + ic:
                                                  par * 8 + 5 + sub * 2 + ic]), "seI")
                        mk[("exp", "a", hp)] = p.mark()
                        p.wat(vec, mk[("exp", "a", hp)], "seI")
                        vec.drain()
                        p.inc(vec.reciprocal(smx[:, par * 8 + 12 - 8:par * 8 + 16 - 8]
                                             if False else
                                             smx[:, par * 8 + 4:par * 8 + 8],
                                             smx[:, par * 8 + 4:par * 8 + 8]), "veI")
                        mk[("rcp", hp)] = p.mark()
                        p.wat(scl, mk[("rcp", hp)], "veI")
                        scl.drain()
                        for sub in range(2):
                            for ic in (0, 1):
                                lo = live[ic][0][1] * 128
                                co = par * 1536 + sub * 768 + ic * 384
                                p.inc(scl.activation(
                                    out=zm_sb[:, co + lo:co + 384],
                                    in_=zm_sb[:, co + lo:co + 384],
                                    func=AF.Copy,
                                    scale=smx[:, par * 8 + 4 + sub * 2 + ic:
                                              par * 8 + 5 + sub * 2 + ic]), "seI")
                        mk[("pn", hp)] = p.mark()

                    def emit_T(hp):
                        par = hp % 2
                        ptx = pt_sb if par == 0 else ptB
                        p.wat(ten, mk[("pn", hp)], "seI")
                        if hp >= 1:
                            p.wat(ten, mk[("pte", hp - 1)], "seI", "veI")
                        for sub in range(2):
                            PTx = PT if sub == 0 else PT2
                            for ic in (0, 1):
                                co = par * 1536 + sub * 768 + ic * 384
                                for which, simcol in live[ic]:
                                    mm = ten.transpose(
                                        PTx[:, which * 256 + ic * 128:
                                            which * 256 + (ic + 1) * 128],
                                        zm_sb[:, co + simcol * 128:
                                              co + (simcol + 1) * 128],
                                        ident_bf[:])
                        p.inc(mm, "peI")
                        mk[("T", hp)] = p.mark()
                        p.wat(scl, mk[("T", hp)], "peI")
                        if hp >= 2:
                            p.wat(scl, mk[("av", hp - 2)], "peI")
                        p.inc(scl.activation(out=ptx[:, 0:1024], in_=PT[:],
                                             func=AF.Copy), "seI")
                        p.wat(vec, mk[("T", hp)], "peI")
                        if hp >= 2:
                            p.wat(vec, mk[("av", hp - 2)], "peI")
                        p.inc(vec.tensor_copy(ptx[:, 1024:2048], PT2[:]), "veI")
                        mk[("pte", hp)] = p.mark()

                    def emit_av(hp):
                        ptx = pt_sb if hp % 2 == 0 else ptB
                        p.wat(ten, mk[("pte", hp)], "seI", "veI")
                        if hp >= 1:
                            p.wat(ten, mk[("oe", hp - 1)], "seI", "veI")
                        if tau > 0:
                            order = [(1, 0, 256, True), (0, 0, 128, False),
                                     (2, 0, 256, False), (3, 128, 256, False)]
                        else:
                            order = [(2, 0, 256, True), (3, 128, 256, False)]
                        for sub in range(2):
                            h = 2 * hp + sub
                            ob = PS[4 + sub]
                            for oi_, (which, i0, i1, st_) in enumerate(order):
                                mm = ten.matmul(
                                    ob[0:64, i0:i1],
                                    vv(h, which),
                                    ptx[:, sub * 1024 + which * 256 + i0:
                                        sub * 1024 + which * 256 + i1],
                                    start=st_, stop=(oi_ == len(order) - 1))
                        p.inc(mm, "peI")
                        mk[("av", hp)] = p.mark()
                        h0 = 2 * hp
                        p.wat(scl, mk[("av", hp)], "peI")
                        p.inc(scl.activation(
                            out=oa_bf[0:64, (h0 // 2) * TI:(h0 // 2 + 1) * TI],
                            in_=PS[4][0:64, 0:TI], func=AF.Copy), "seI")
                        p.wat(vec, mk[("av", hp)], "peI")
                        p.inc(vec.tensor_copy(
                            oa_bf[64:128, (h0 // 2) * TI:(h0 // 2 + 1) * TI],
                            PS[5][0:64, 0:TI]), "veI")
                        mk[("oe", hp)] = p.mark()

                    for hp in range(11):
                        if hp < 8:
                            if hp >= 1:
                                p.wat(ten, mk[("zm", hp - 1)], "veI")
                            if hp == 0:
                                p.wat(ten, mk[("rot_done", 14)], "veI")
                                p.wat(ten, mk[("rot_done", 15)], "veI", "seI")
                            emit_sim(hp)
                        if 2 <= hp < 10:
                            emit_T(hp - 2)
                        if hp < 8:
                            emit_zm(hp)
                            emit_softmax(hp)
                        if 3 <= hp:
                            emit_av(hp - 3)
                    p.wat(ten, mk[("oe", 7)], "seI", "veI")
                    p.wat(ten, mk[("zm", 7)], "veI")
                    wo_snap = {}
                    for oc in range(EC):
                        wb = PS[oc % 2]
                        if oc >= 2:
                            p.wat(ten, wo_snap[oc - 2], "veI", "seI")
                        for e in range(EC):
                            mm = ten.matmul(
                                wb[:, 0:TI],
                                WOI[:, e * 1024 + oc * 128:e * 1024 + oc * 128 + 128],
                                oa_bf[:, e * TI:(e + 1) * TI],
                                start=(e == 0), stop=(e == EC - 1))
                        p.inc(mm, "peI")
                        par = oc % 2
                        sxn = "sxa" if par == 0 else "sxb"
                        p.w(vec, "lxi")
                        p.wat(vec, mk[("wo", oc)] if ("wo", oc) in mk else p.mark(),
                              "peI")
                        p.w(vec, sxn)
                        p.inc(vec.tensor_tensor(oi_small[:, par * TI:(par + 1) * TI],
                                                wb[:, 0:TI],
                                                x_cur[:, oc * TI:(oc + 1) * TI],
                                                op=ALU.add), "veI")
                        p.w(scl, "veI")
                        p.inc(scl.activation(out=xst_bf[:, oc * TI:(oc + 1) * TI],
                                             in_=oi_small[:, par * TI:(par + 1) * TI],
                                             func=AF.Copy), "seI")
                        p.w(sync, "seI")
                        p.inc(sync.dma_start(
                            out=XOUT[s, oc * 128:(oc + 1) * 128, c0:c0 + TI],
                            in_=oi_small[:, par * TI:(par + 1) * TI]), sxn, 16)
                        wo_snap[oc] = p.mark()
                    mk[("wo_done",)] = p.mark()
                    p.w(scl, "peI")
                    scl.drain()
                    for e in range(EC):
                        mmq = scl.activation(out=oa_bf[:, e * TI:(e + 1) * TI],
                                             in_=xst_bf[:, e * TI:(e + 1) * TI],
                                             func=AF.Square)
                    p.inc(mmq, "seI")
                    p.w(ten, "seI", "veI")
                    for e in range(4):
                        ten.matmul(PS[4][0:1, 0:TI], onescol_bf[:],
                                   xst_bf[:, e * TI:(e + 1) * TI],
                                   start=(e == 0), stop=(e == 3))
                    for e in range(4, EC):
                        ten.matmul(PS[4][0:1, TI:2 * TI], onescol_bf[:],
                                   xst_bf[:, e * TI:(e + 1) * TI],
                                   start=(e == 4), stop=(e == 7))
                    for e in range(4):
                        ten.matmul(PS[5][0:1, 0:TI], onescol_bf[:],
                                   oa_bf[:, e * TI:(e + 1) * TI],
                                   start=(e == 0), stop=(e == 3))
                    for e in range(4, EC):
                        mm = ten.matmul(PS[5][0:1, TI:2 * TI], onescol_bf[:],
                                        oa_bf[:, e * TI:(e + 1) * TI],
                                        start=(e == 4), stop=(e == 7))
                    p.inc(mm, "peI")
                    mk[("stats_mm",)] = p.mark()
                    p.w(vec, "peI", "smu")
                    vec.drain()
                    if tau > 0:
                        vec.tensor_copy(brow[0:1, 0:1], brow[0:1, WIN:WIN + 1])
                        vec.drain()
                        vec.tensor_copy(b2row[0:1, 0:1], b2row[0:1, WIN:WIN + 1])
                        vec.drain()
                    vec.tensor_copy(brow[0:1, 1:WIN + 1], PS[4][0:1, TI:2 * TI])
                    vec.tensor_copy(b2row[0:1, 1:WIN + 1], PS[5][0:1, TI:2 * TI])
                    vec.drain()
                    vec.tensor_tensor(murow[0:1, :], PS[4][0:1, 0:TI],
                                      brow[0:1, 0:WIN], op=ALU.add)
                    vec.tensor_tensor(r2row[0:1, :], PS[5][0:1, 0:TI],
                                      b2row[0:1, 0:WIN], op=ALU.add)
                    vec.drain()
                    vec.tensor_scalar_mul(murow[0:1, :], murow[0:1, :], 1.0 / E)
                    vec.drain()
                    vec.tensor_tensor(rstdrow[0:1, :], murow[0:1, :], murow[0:1, :],
                                      op=ALU.mult)
                    vec.drain()
                    p.inc(vec.scalar_tensor_tensor(
                        out=r2row[0:1, :], in0=r2row[0:1, :], scalar=1.0 / E,
                        in1=rstdrow[0:1, :], op0=ALU.mult, op1=ALU.subtract), "veI")
                    mk[("stats_rows",)] = p.mark()
                    p.w(scl, "veI")
                    scl.drain()
                    scl.activation(out=rstdrow[0:1, :], in_=r2row[0:1, :],
                                   func=AF.Ln, bias=consts[0:1, 0:1])
                    scl.drain()
                    p.inc(scl.activation(out=rstdrow[0:1, :], in_=rstdrow[0:1, :],
                                         func=AF.Exp, scale=-0.5), "seI")
                    p.w(sync, "veI", "seI")
                    p.flush(sync, "smu")
                    p.inc(sync.dma_start(out=mu_dram[s, tau], in_=murow[0:1, :]),
                          "smu", 16)
                    p.inc(sync.dma_start(out=rstd_dram[s, tau], in_=rstdrow[0:1, :]),
                          "smu", 16)

        def ffn_pass(l, XIN, XOUT):
            load_ffn_weights(l)
            mk = {}
            for s in range(BPC):
                for tau in range(NTI):
                    c0 = tau * TI
                    if tau > 0:
                        p.wat(sync, mk[("ffn_ep", tau - 1)], "veI")
                    else:
                        p.w(sync, "seI", "veI", "peI")
                    p.flush(sync, "lxi")
                    for e in range(EC):
                        p.inc(sync.dma_start(
                            out=x_cur[:, e * TI:(e + 1) * TI],
                            in_=XIN[s, e * 128:(e + 1) * 128, c0:c0 + TI]), "lxi", 16)
                    for e in range(4, EC):
                        if tau > 0:
                            p.inc(sync.dma_start(
                                out=xsh4[:, (e - 4) * TI:(e - 3) * TI],
                                in_=XIN[s, e * 128:(e + 1) * 128, c0 - 1:c0 + TI - 1]),
                                "lxi", 16)
                        else:
                            p.w(vec, "lxi")
                            p.inc(vec.memset(xsh4[:, (e - 4) * TI:(e - 4) * TI + 1],
                                             0.0), "veI")
                            p.w(sync, "veI")
                            p.inc(sync.dma_start(
                                out=xsh4[:, (e - 4) * TI + 1:(e - 3) * TI],
                                in_=XIN[s, e * 128:(e + 1) * 128, c0:c0 + TI - 1]),
                                "lxi", 16)
                    p.flush(sync, "lmu")
                    p.flush(sync, "smu")
                    p.inc(sync.dma_start(out=murow[0:1, :], in_=mu_dram[s, tau]),
                          "lmu", 16)
                    p.inc(sync.dma_start(out=rstdrow[0:1, :], in_=rstd_dram[s, tau]),
                          "lmu", 16)
                    xnb = qrot_bf if tau % 2 == 0 else oa_bf
                    p.w(ten, "lmu", "lw")
                    if tau > 0:
                        p.wat(ten, mk[("ffn_xn", tau - 1)], "veI")
                    else:
                        p.w(ten, "veI", "seI")
                    ten.matmul(PS[4][:, 0:TI], onesrow_f[:], murow[0:1, :],
                               start=True, stop=True)
                    mm = ten.matmul(PS[5][:, 0:TI], onesrow_f[:], rstdrow[0:1, :],
                                    start=True, stop=True)
                    p.inc(mm, "peI")
                    mk[("ffn_bc", tau)] = p.mark()
                    p.wat(vec, mk[("ffn_bc", tau)], "peI")
                    p.w(vec, "lxi")
                    for e in range(EC):
                        src = (x_cur[:, e * TI:(e + 1) * TI] if e < 4 else
                               xsh4[:, (e - 4) * TI:(e - 3) * TI])
                        tl = tln_sb[:, (e % 2) * TI:(e % 2 + 1) * TI]
                        vec.tensor_tensor(tl, src, PS[4][:, 0:TI],
                                          op=ALU.subtract)
                        vec.drain()
                        vec.scalar_tensor_tensor(
                            out=tl, in0=tl,
                            scalar=lng_sb[:, l * EC + e:l * EC + e + 1],
                            in1=PS[5][:, 0:TI], op0=ALU.mult, op1=ALU.mult)
                        vec.drain()
                        p.inc(vec.tensor_scalar_add(
                            xnb[:, e * TI:(e + 1) * TI], tl,
                            lnb_sb[:, l * EC + e:l * EC + e + 1]), "veI")
                    mk[("ffn_xn", tau)] = p.mark()
                    p.wat(ten, mk[("ffn_xn", tau)], "veI")
                    p.w(ten, "seI")
                    pair_snap = {}
                    for ci, (off, szw) in enumerate(zip(FFN_OFF[:-1], FFN_CH)):
                        ab = PS[2 * (ci % 2)]
                        gb = PS[2 * (ci % 2) + 1]
                        if ci >= 2:
                            p.wat(ten, pair_snap[ci - 2], "veI", "seI")
                        for e in range(EC):
                            ten.matmul(ab[0:szw, 0:TI],
                                       W1A[:, e * 5460 + off:e * 5460 + off + szw],
                                       xnb[:, e * TI:(e + 1) * TI],
                                       start=(e == 0), stop=(e == EC - 1))
                        for e in range(EC):
                            mm = ten.matmul(gb[0:szw, 0:TI],
                                            W1A[:, e * 5460 + DI + off:
                                                e * 5460 + DI + off + szw],
                                            xnb[:, e * TI:(e + 1) * TI],
                                            start=(e == 0), stop=(e == EC - 1))
                        p.inc(mm, "peI")
                        gv = (ga_sb if ci % 2 == 0 else tln_sb)[:, 0:TI]
                        p.w(scl, "peI")
                        p.inc(scl.activation(out=gv, in_=gb[0:128, 0:TI],
                                             func=AF.Gelu,
                                             bias=b1g_sb[:, l * NFC + ci:
                                                         l * NFC + ci + 1]), "seI")
                        p.w(vec, "seI")
                        p.inc(vec.scalar_tensor_tensor(
                            out=h_bf[:, ci * TI:(ci + 1) * TI], in0=ab[0:128, 0:TI],
                            scalar=b1a_sb[:, l * NFC + ci:l * NFC + ci + 1],
                            in1=gv, op0=ALU.add, op1=ALU.mult), "veI")
                        pair_snap[ci] = p.mark()
                    p.w(ten, "veI")
                    w2_snap = {}
                    for oc in range(EC):
                        wb = PS[oc % 2]
                        par = oc % 2
                        sxn = "sxa" if par == 0 else "sxb"
                        if oc >= 2:
                            p.wat(ten, w2_snap[oc - 2], "veI")
                        else:
                            p.wat(ten, pair_snap[NFC - 2 + oc], "veI", "seI")
                        for ci in range(NFC):
                            mm = ten.matmul(
                                wb[:, 0:TI],
                                W2A[:, ci * 1024 + oc * 128:ci * 1024 + oc * 128 + 128],
                                h_bf[:, ci * TI:(ci + 1) * TI],
                                start=(ci == 0), stop=(ci == NFC - 1))
                        p.inc(mm, "peI")
                        p.w(vec, "peI")
                        p.w(vec, sxn)
                        p.inc(vec.scalar_tensor_tensor(
                            out=oi_small[:, par * TI:(par + 1) * TI], in0=wb[:, 0:TI],
                            scalar=b2_sb[:, l * EC + oc:l * EC + oc + 1],
                            in1=x_cur[:, oc * TI:(oc + 1) * TI],
                            op0=ALU.add, op1=ALU.add), "veI")
                        p.w(sync, "veI")
                        p.inc(sync.dma_start(
                            out=XOUT[s, oc * 128:(oc + 1) * 128, c0:c0 + TI],
                            in_=oi_small[:, par * TI:(par + 1) * TI]), sxn, 16)
                        w2_snap[oc] = p.mark()
                        if oc == EC - 1:
                            mk[("ffn_ep", tau)] = p.mark()

        for eng in (sync, scl, vec, ten, gp):
            p.w(eng, "lc", "lcv", "sc", "sx", "sv", "seC", "veC", "peC",
                "seA", "veA", "peA", "seB", "veB", "veB2", "peB", "lx", "ld",
                "ldg", "ve", "se", "pe")

        scl.memzero(PT[:])
        p.inc(scl.memzero(PT2[:]), "seI")

        for l in range(n_layers):
            attn_pass(l, xf1, xf2)
            if half and l == n_layers - 1:
                break
            ffn_pass(l, xf2, xf1)

        ictx.close()

    if stage == "FULL":
        fctx = ExitStack()

        def fsb(name, shape, dt):
            return fctx.enter_context(nc.sbuf_tensor(name, list(shape), dt))

        f_sb = arenaA[:, 0 * 8192:1 * 8192].bitcast(F32)
        sF_sb = arenaA[:, 1 * 8192:2 * 8192].bitcast(F32)
        g_sb = arenaA[:, 2 * 8192:3 * 8192].bitcast(F32)
        pF_sb = arenaA[:, 3 * 8192:4 * 8192].bitcast(F32)
        xF_sb = arenaA[:, 4 * 8192:5 * 8192].bitcast(F32)
        yF_sb = fsb("yF_sb", [128, 4096], F32)

        for eng in (sync, scl, vec, ten, gp):
            p.w(eng, "sxa", "sxb", "smu", "seI", "veI", "peI", "lxi", "lqi", "lmu", "lw")

        for s in range(BPC):
            for tau in range(NTO):
                c0 = tau * TT
                p.w(sync, "seF", "veF", "peF", "syF")
                p.flush(sync, "lf")
                for e in range(EC):
                    p.inc(sync.dma_start(
                        out=f_sb[:, e * 512:(e + 1) * 512],
                        in_=xf1[s, e * 128:(e + 1) * 128, c0:c0 + TT]), "lf", 16)
                    p.inc(sync.dma_start(
                        out=sF_sb[:, e * 512:(e + 1) * 512],
                        in_=s_buf[s, e * 128:(e + 1) * 128, c0:c0 + TT]), "lf", 16)
                for j in range(4):
                    p.inc(sync.dma_start(
                        out=xF_sb[:, j * 1024:(j + 1) * 1024],
                        in_=x_in[s, c0 + j * 128:c0 + (j + 1) * 128, :]), "lf", 16)
                p.w(scl, "lf", "veF")
                for e in range(EC):
                    p.inc(scl.activation(out=g_sb[:, e * 512:(e + 1) * 512],
                                         in_=f_sb[:, e * 512:(e + 1) * 512],
                                         func=AF.Silu), "seF")
                p.w(vec, "seF", "lf", "peF")
                for e in range(EC):
                    vec.drain()
                    p.inc(vec.tensor_tensor(pF_sb[:, e * 512:(e + 1) * 512],
                                            g_sb[:, e * 512:(e + 1) * 512],
                                            sF_sb[:, e * 512:(e + 1) * 512],
                                            op=ALU.mult), "veF")
                p.w(ten, "veF")
                for j in range(4):
                    pa = PS[2 * (j % 2)]
                    pb_ = PS[2 * (j % 2) + 1]
                    p.w(ten, "veF2")
                    for e in range(EC):
                        bank = pa if e < 4 else pb_
                        mm = ten.transpose(
                            bank[:, (e % 4) * 128:(e % 4) * 128 + 128],
                            pF_sb[:, e * 512 + j * 128:e * 512 + j * 128 + 128],
                            ident_f[:])
                    p.inc(mm, "peF")
                    p.w(vec, "peF")
                    vec.drain()
                    p.inc(vec.scalar_tensor_tensor(
                        out=yF_sb[:, j * 1024:j * 1024 + 512], in0=pa[:],
                        scalar=ATTN_MIX, in1=xF_sb[:, j * 1024:j * 1024 + 512],
                        op0=ALU.mult, op1=ALU.add), "veF2")
                    vec.drain()
                    p.inc(vec.scalar_tensor_tensor(
                        out=yF_sb[:, j * 1024 + 512:(j + 1) * 1024], in0=pb_[:],
                        scalar=ATTN_MIX, in1=xF_sb[:, j * 1024 + 512:(j + 1) * 1024],
                        op0=ALU.mult, op1=ALU.add), "veF2")
                p.w(sync, "veF2")
                p.flush(sync, "syF")
                for j in range(4):
                    p.inc(sync.dma_start(
                        out=y_out[s, c0 + j * 128:c0 + (j + 1) * 128, :],
                        in_=yF_sb[:, j * 1024:(j + 1) * 1024]), "syF", 16)
        fctx.close()

    p.w(sync, "sc", "sx", "sv", "sxa", "sxb", "smu", "seI", "veI", "peI")
    for name, ext in dbg_outs.items():
        p.inc(sync.dma_start(out=ext, in_=scratch_map[name]), "sdbg", 16)
    if stage != "FULL":
        p.inc(sync.dma_start(out=y_out, in_=x_in), "sy", 16)
        p.w(sync, "sy")
    else:
        p.w(sync, "syF")
    p.w(sync, "sdbg")

    ctx.close()
    return nc


def kernel(**inputs):
    nc = build_program()
    in_maps = [_layout_inputs(inputs, c) for c in range(NCORE)]
    res = run_bass_kernel_spmd(nc, in_maps, core_ids=list(range(NCORE)))
    return np.concatenate([r["y"] for r in res.results], axis=0)


# version 09
# speedup vs baseline: 1.1702x; 1.0121x over previous
import numpy as np
from contextlib import ExitStack

import concourse.bass as bass
import concourse.mybir as mybir
from concourse.bass_utils import run_bass_kernel_spmd

F32 = mybir.dt.float32
BF16 = mybir.dt.bfloat16
AF = mybir.ActivationFunctionType
ALU = mybir.AluOpType
AX = mybir.AxisListType

B = 16
S = 4096
E = 1024
NCORE = 8
BPC = B // NCORE
NH = 16
HD = 64
DEPTH = 6
WIN = 256
DI = 2730
EPS_MP = 1e-4
EPS_LN = 1e-5
QK_SCALE = 8.0
ATTN_MIX = 0.7071067811865476
NEG = -1.0e30
EC = E // 128
FFN_CH = [42] + [128] * 21
FFN_OFF = np.concatenate([[0], np.cumsum(FFN_CH)]).tolist()
NFC = len(FFN_CH)



def _host_constants():
    c = {}
    c["ident"] = np.eye(128, dtype=np.float32)
    rot = np.zeros((128, 128), dtype=np.float32)
    for h0 in (0, 64):
        for d in range(32):
            rot[h0 + d + 32, h0 + d] = -1.0
        for d in range(32, 64):
            rot[h0 + d - 32, h0 + d] = 1.0
    c["rotm"] = rot
    blk2 = np.zeros((128, 2), dtype=np.float32)
    blk2[0:64, 0] = 1.0
    blk2[64:128, 1] = 1.0
    c["blk2"] = blk2
    exp2 = np.zeros((2, 128), dtype=np.float32)
    exp2[0, 0:64] = 1.0
    exp2[1, 64:128] = 1.0
    c["exp2"] = exp2
    c["ones_col"] = np.ones((128, 1), dtype=np.float32)
    c["ones_row"] = np.ones((1, 128), dtype=np.float32)
    i = np.arange(128)[:, None]
    j = np.arange(128)[None, :]
    maskA = np.where(j >= i, 0.0, NEG).astype(np.float32)
    maskB = np.where(j <= i, 0.0, NEG).astype(np.float32)
    c["maskband"] = np.concatenate(
        [maskA, np.zeros((128, 128), np.float32), maskB], axis=1)
    half = np.arange(0, HD, 2, dtype=np.float64)
    inv_freq = 1.0 / (10000.0 ** (half / HD))
    t = np.arange(S, dtype=np.float64)
    freqs = t[:, None] * inv_freq[None, :]
    freqs = np.concatenate([freqs, freqs], axis=1)
    base_scale = (half + 0.4 * HD) / (1.4 * HD)
    power = (t - S // 2) / (WIN // 2)
    scale = base_scale[None, :] ** power[:, None]
    scale = np.concatenate([scale, scale], axis=1)
    qc = (np.cos(freqs) * scale).T.astype(np.float32)
    qs_ = (np.sin(freqs) * scale).T.astype(np.float32)
    kc = (np.cos(freqs) / scale).T.astype(np.float32)
    ks_ = (np.sin(freqs) / scale).T.astype(np.float32)
    c["rot_qc"] = np.concatenate([qc, qc], axis=0)
    c["rot_qs"] = np.concatenate([qs_, qs_], axis=0)
    c["rot_kc"] = np.concatenate([kc, kc], axis=0)
    c["rot_ks"] = np.concatenate([ks_, ks_], axis=0)
    consts = np.zeros((128, 8), dtype=np.float32)
    consts[:, 0] = EPS_LN
    consts[:, 1] = EPS_MP
    consts[:, 2] = 1.0
    c["consts"] = consts
    return c


def _layout_inputs(inp, core):
    m = {}
    m["x"] = np.ascontiguousarray(np.asarray(inp["L0_attn_query"])[2 * core: 2 * core + 2])
    m["wq"] = np.asarray(inp["L0_attn_w_q"])
    m["wk"] = np.asarray(inp["L0_attn_w_k"])
    m["wv"] = np.asarray(inp["L0_attn_w_v"])
    m["wo"] = np.asarray(inp["L0_attn_w_o"])
    m["gain128"] = np.ascontiguousarray(np.broadcast_to(
        np.asarray(inp["L0_attn_gain"], np.float32).reshape(1, 1), (128, 1)))
    m["wqkv"] = np.asarray(inp["L0_ffn_Wqkv"])
    m["woi"] = np.asarray(inp["L0_ffn_Wo"])
    m["w1"] = np.asarray(inp["L0_ffn_W1"])
    m["w2"] = np.asarray(inp["L0_ffn_W2"])
    qs = np.asarray(inp["L0_ffn_q_scale"], np.float32)
    ks = np.asarray(inp["L0_ffn_k_scale"], np.float32)
    m["qs_all"] = np.ascontiguousarray(np.tile(qs, (1, 2)).T)
    m["ks_all"] = np.ascontiguousarray(np.tile(ks, (1, 2)).T)
    lng = np.asarray(inp["L0_ffn_ln_g"], np.float32)
    lnb = np.asarray(inp["L0_ffn_ln_b"], np.float32)
    m["lng"] = np.ascontiguousarray(
        lng.reshape(DEPTH, EC, 128).transpose(2, 0, 1).reshape(128, DEPTH * EC))
    m["lnb"] = np.ascontiguousarray(
        lnb.reshape(DEPTH, EC, 128).transpose(2, 0, 1).reshape(128, DEPTH * EC))
    b1 = np.asarray(inp["L0_ffn_b1"], np.float32)
    b2 = np.asarray(inp["L0_ffn_b2"], np.float32)
    b1a = np.zeros((128, DEPTH * NFC), np.float32)
    b1g = np.zeros((128, DEPTH * NFC), np.float32)
    for l in range(DEPTH):
        for ci, (off, sz) in enumerate(zip(FFN_OFF[:-1], FFN_CH)):
            b1a[0:sz, l * NFC + ci] = b1[l, off:off + sz]
            b1g[0:sz, l * NFC + ci] = b1[l, DI + off:DI + off + sz]
    m["b1a"] = b1a
    m["b1g"] = b1g
    m["b2v"] = np.ascontiguousarray(
        b2.reshape(DEPTH, EC, 128).transpose(2, 0, 1).reshape(128, DEPTH * EC))
    m.update(_host_constants())
    return m



class Prog:
    def __init__(self, nc):
        self.nc = nc
        self.c = {}
        self.sem = {}

    def S(self, name):
        if name not in self.sem:
            self.sem[name] = self.nc.alloc_semaphore(name)
            self.c[name] = 0
        return self.sem[name]

    def inc(self, instr, name, n=1):
        instr.then_inc(self.S(name), n)
        self.c[name] += n
        return instr

    def w(self, eng, *names):
        for name in names:
            if name in self.c and self.c[name] > 0:
                eng.wait_ge(self.S(name), self.c[name])

    def mark(self):
        return dict(self.c)

    def wat(self, eng, snap, *names):
        for name in names:
            v = snap.get(name, 0)
            if v > 0:
                eng.wait_ge(self.S(name), v)

    def flush(self, q, name):
        if name in self.c and self.c[name] > 0:
            q.wait_ge(self.S(name), self.c[name])


def build_program(dbg=(), stage="FULL"):
    nc = bass.Bass("TRN2", target_bir_lowering=False)
    p = Prog(nc)

    def din(name, shape, dt=F32):
        return nc.dram_tensor(name, list(shape), dt, kind="ExternalInput").ap()

    x_in = din("x", [BPC, S, E])
    wq_in = din("wq", [E, E]); wk_in = din("wk", [E, E])
    wv_in = din("wv", [E, E]); wo_in = din("wo", [E, E])
    gain_in = din("gain128", [128, 1])
    wqkv_in = din("wqkv", [DEPTH, E, 3 * E])
    woi_in = din("woi", [DEPTH, E, E])
    w1_in = din("w1", [DEPTH, E, 2 * DI])
    w2_in = din("w2", [DEPTH, DI, E])
    qs_in = din("qs_all", [128, DEPTH]); ks_in = din("ks_all", [128, DEPTH])
    lng_in = din("lng", [128, DEPTH * EC]); lnb_in = din("lnb", [128, DEPTH * EC])
    b1a_in = din("b1a", [128, DEPTH * NFC]); b1g_in = din("b1g", [128, DEPTH * NFC])
    b2_in = din("b2v", [128, DEPTH * EC])
    ident_in = din("ident", [128, 128])
    rotm_in = din("rotm", [128, 128])
    blk2_in = din("blk2", [128, 2]); exp2_in = din("exp2", [2, 128])
    onescol_in = din("ones_col", [128, 1]); onesrow_in = din("ones_row", [1, 128])
    mask_in = din("maskband", [128, 384])
    rqc_in = din("rot_qc", [128, S]); rqs_in = din("rot_qs", [128, S])
    rkc_in = din("rot_kc", [128, S]); rks_in = din("rot_ks", [128, S])
    consts_in = din("consts", [128, 8])

    y_out = nc.dram_tensor("y", [BPC, S, E], F32, kind="ExternalOutput").ap()

    def scr(name, shape, dt=F32):
        return nc.dram_tensor(name, list(shape), dt, kind="Internal").ap()

    xn0 = scr("xn0", [BPC, E, S])
    xf1 = scr("xf1", [BPC, E, S])
    xf2 = scr("xf2", [BPC, E, S])
    s_buf = scr("s_buf", [BPC, E, S])
    v_buf = scr("v_buf", [BPC, E, S], BF16)
    mu_dram = scr("mu_dram", [BPC, 16, WIN])
    rstd_dram = scr("rstd_dram", [BPC, 16, WIN])
    scratch_map = {"xn": xn0, "s": s_buf, "v": v_buf, "xf1": xf1, "xf2": xf2}

    dbg_outs = {}
    for name in dbg:
        src = scratch_map[name]
        dbg_outs[name] = nc.dram_tensor(
            "dbg_" + name, list(src.shape), src.dtype, kind="ExternalOutput").ap()

    ctx = ExitStack()

    def sb(name, shape, dt):
        return ctx.enter_context(nc.sbuf_tensor(name, list(shape), dt))

    def psum(name, shape, dt=F32):
        return ctx.enter_context(nc.psum_tensor(name, list(shape), dt))

    ident_bf = sb("ident_bf", [128, 128], BF16)
    ident_f = sb("ident_f", [128, 128], F32)
    rotm_bf = sb("rotm_bf", [128, 128], BF16)
    blk2_bf = sb("blk2_bf", [128, 2], BF16)
    exp2_f = sb("exp2_f", [2, 128], F32)
    onescol_bf = sb("onescol_bf", [128, 1], BF16)
    onesrow_f = sb("onesrow_f", [1, 128], F32)
    maskband = sb("maskband_sb", [128, 384], F32)
    consts = sb("consts_sb", [128, 8], F32)
    gain_sb = sb("gain_sb", [128, 1], F32)
    qs_sb = sb("qs_sb", [128, DEPTH], F32)
    ks_sb = sb("ks_sb", [128, DEPTH], F32)
    lng_sb = sb("lng_sb", [128, DEPTH * EC], F32)
    lnb_sb = sb("lnb_sb", [128, DEPTH * EC], F32)
    b1a_sb = sb("b1a_sb", [128, DEPTH * NFC], F32)
    b1g_sb = sb("b1g_sb", [128, DEPTH * NFC], F32)
    b2_sb = sb("b2_sb", [128, DEPTH * EC], F32)

    arenaA = sb("arenaA", [128, 8 * 5460], BF16)
    arenaB = sb("arenaB", [128, NFC * 1024], BF16)

    WQT = arenaA[:, 0 * 8192: 1 * 8192]
    WKT = arenaA[:, 1 * 8192: 2 * 8192]
    WVT = arenaA[:, 2 * 8192: 3 * 8192]
    WOT = arenaA[:, 3 * 8192: 4 * 8192]

    PS = [psum(f"ps{i}", [128, 512], F32) for i in range(6)]
    PT = psum("pt_bf", [128, 1024], BF16)
    PT2 = psum("pt2_bf", [128, 1024], BF16)

    sync = nc.sync
    scl = nc.scalar
    vec = nc.vector
    ten = nc.tensor
    gp = nc.gpsimd

    for src, dst in [(ident_in, ident_f), (exp2_in, exp2_f), (onesrow_in, onesrow_f),
                     (mask_in, maskband), (consts_in, consts), (gain_in, gain_sb),
                     (qs_in, qs_sb), (ks_in, ks_sb), (lng_in, lng_sb),
                     (lnb_in, lnb_sb), (b1a_in, b1a_sb), (b1g_in, b1g_sb),
                     (b2_in, b2_sb)]:
        p.inc(sync.dma_start(out=dst[:], in_=src), "ld", 16)
    for src, dst in [(ident_in, ident_bf), (rotm_in, rotm_bf), (blk2_in, blk2_bf),
                     (onescol_in, onescol_bf)]:
        p.inc(gp.dma_start(out=dst[:], in_=src), "ldg", 16)

    p0ctx = ExitStack()
    w_stage = p0ctx.enter_context(nc.sbuf_tensor("w_stage", [128, 1024], F32))
    w_rows = p0ctx.enter_context(nc.sbuf_tensor("w_rows", [128, 4], F32))
    w_dump = p0ctx.enter_context(nc.sbuf_tensor("w_dump", [128, 1024], F32))
    wn_bf = p0ctx.enter_context(nc.sbuf_tensor("wn_bf", [128, 1024], BF16))

    p.w(vec, "ld", "ldg")
    p.w(ten, "ld", "ldg")
    for w_src, w_dst in [(wq_in, WQT), (wk_in, WKT), (wv_in, WVT), (wo_in, WOT)]:
        for oc in range(EC):
            p.w(sync, "se", "pe")
            p.flush(sync, "ld")
            p.inc(sync.dma_start(out=w_stage[:],
                                 in_=w_src[oc * 128:(oc + 1) * 128, :]), "ld", 16)
            p.w(scl, "ld", "ldg")
            scl.activation(out=w_dump[:], in_=w_stage[:], func=AF.Square,
                           accum_out=w_rows[:, 0:1])
            scl.drain()
            scl.activation(out=w_rows[:, 1:2], in_=w_rows[:, 0:1], func=AF.Ln)
            scl.drain()
            p.inc(scl.activation(out=w_rows[:, 1:2], in_=w_rows[:, 1:2],
                                 func=AF.Exp, scale=0.5), "se")
            p.w(vec, "se")
            vec.tensor_scalar(out=w_rows[:, 2:3], in0=w_rows[:, 1:2],
                              scalar1=1.0 / 32.0, scalar2=EPS_MP,
                              op0=ALU.mult, op1=ALU.add)
            vec.drain()
            vec.reciprocal(w_rows[:, 2:3], w_rows[:, 2:3])
            vec.drain()
            p.inc(vec.tensor_scalar(out=w_rows[:, 3:4], in0=w_rows[:, 2:3],
                                    scalar1=gain_sb[:, 0:1], scalar2=1.0 / 32.0,
                                    op0=ALU.mult, op1=ALU.mult), "ve")
            p.w(scl, "ve")
            scl.drain()
            p.inc(scl.activation(out=wn_bf[:], in_=w_stage[:], func=AF.Copy,
                                 scale=w_rows[:, 3:4]), "se")
            p.w(ten, "se", "ve")
            for e8 in range(EC):
                mm = ten.transpose(PT[:, e8 * 128:(e8 + 1) * 128],
                                   wn_bf[:, e8 * 128:(e8 + 1) * 128], ident_bf[:])
            p.inc(mm, "pe")
            p.w(vec, "pe")
            p.inc(vec.tensor_copy(
                w_dst.rearrange("p (e o) -> p e o", o=1024)
                [:, :, oc * 128:(oc + 1) * 128],
                PT[:].rearrange("p (e o) -> p e o", o=128)), "ve")

    p0ctx.close()

    TT = 512
    NTO = S // TT
    outctx = ExitStack()

    def osb(name, shape, dt):
        return outctx.enter_context(nc.sbuf_tensor(name, list(shape), dt))

    xt_sb = osb("xt_sb", [128, 4096], F32)
    sqs_sb = osb("sqs_sb", [128, 1024], F32)
    ln_sc = osb("ln_sc", [128, 8], F32)
    xn_tm = osb("xn_tm", [128, 4096], F32)
    xn_fm = osb("xn_fm", [128, 8 * 512], F32)
    xn_bf = osb("xn_bf", [128, 8 * 512], BF16)
    qn_sb = osb("qn_sb", [128, 1024], BF16)
    kn_sb = osb("kn_sb", [128, 1024], BF16)
    nrm_sc = osb("nrm_sc", [128, 48], F32)
    v_sb = osb("v_sb", [128, 512], BF16)
    attnb = osb("attnb", [128, 512], BF16)
    attnT = osb("attnT", [128, 512], BF16)
    sm_sc = osb("sm_sc", [128, 4], F32)
    o_bf = arenaB[:, 0:4096]
    s_out = osb("s_out", [128, 512], F32)
    scores_acc = osb("scores_acc", [128, 512], F32)

    SCORES = PS[5]

    def phase_A(s):
        p.w(sync, "ld", "se", "ve", "pe")
        p.w(ten, "ve", "se", "ld", "ldg")
        p.w(scl, "ve", "ld")
        p.w(vec, "se", "ld")
        for tau in range(NTO):
            xt = xt_sb[:, 0:4096]
            xnt = xn_tm[:, 0:4096]
            p.w(sync, "seA", "veA", "peA")
            p.flush(sync, "lx")
            for j in range(4):
                p.inc(sync.dma_start(
                    out=xt[:, j * 1024:(j + 1) * 1024],
                    in_=x_in[s, tau * TT + j * 128: tau * TT + (j + 1) * 128, :]),
                    "lx", 16)
            p.w(scl, "lx")
            p.w(vec, "lx")
            for j in range(4):
                xj = xt[:, j * 1024:(j + 1) * 1024]
                p.inc(scl.activation(out=sqs_sb[:], in_=xj, func=AF.Square,
                                     accum_out=ln_sc[:, 0:1]), "seA")
                p.w(vec, "seA")
                vec.reduce_sum(ln_sc[:, 1:2], xj, axis=AX.X)
                vec.drain()
                vec.tensor_scalar_mul(ln_sc[:, 2:3], ln_sc[:, 1:2], 1.0 / E)
                vec.drain()
                vec.tensor_tensor(ln_sc[:, 3:4], ln_sc[:, 2:3], ln_sc[:, 2:3],
                                  op=ALU.mult)
                vec.drain()
                p.inc(vec.scalar_tensor_tensor(
                    out=ln_sc[:, 4:5], in0=ln_sc[:, 0:1], scalar=1.0 / E,
                    in1=ln_sc[:, 3:4], op0=ALU.mult, op1=ALU.subtract), "veA")
                p.w(scl, "veA")
                scl.activation(out=ln_sc[:, 5:6], in_=ln_sc[:, 4:5], func=AF.Ln,
                               bias=consts[:, 0:1])
                scl.drain()
                p.inc(scl.activation(out=ln_sc[:, 5:6], in_=ln_sc[:, 5:6],
                                     func=AF.Exp, scale=-0.5), "seA")
                p.w(vec, "seA")
                vec.drain()
                p.inc(vec.scalar_tensor_tensor(
                    out=ln_sc[:, 6:7], in0=ln_sc[:, 2:3], scalar=-1.0,
                    in1=ln_sc[:, 5:6], op0=ALU.mult, op1=ALU.mult), "veA")
                vec.drain()
                p.inc(vec.tensor_scalar(out=xnt[:, j * 1024:(j + 1) * 1024],
                                        in0=xj, scalar1=ln_sc[:, 5:6],
                                        scalar2=ln_sc[:, 6:7],
                                        op0=ALU.mult, op1=ALU.add), "veA")
            if stage == "A1":
                continue
            p.w(ten, "seA")
            for ec in range(EC):
                bank = PS[ec % 2]
                if stage != "A2noT":
                    p.w(ten, "seA", "veA")
                    for j in range(4):
                        mm = ten.transpose(bank[:, j * 128:(j + 1) * 128],
                                           xnt[:, j * 1024 + ec * 128: j * 1024 + ec * 128 + 128],
                                           ident_f[:])
                    p.inc(mm, "peA")
                    p.w(scl, "peA", "sx")
                    p.w(vec, "peA")
                    p.inc(scl.activation(out=xn_fm[:, ec * 512:(ec + 1) * 512],
                                         in_=bank[:], func=AF.Copy), "seA")
                    p.w(vec, "seA")
                    p.inc(vec.tensor_copy(xn_bf[:, ec * 512:(ec + 1) * 512],
                                          xn_fm[:, ec * 512:(ec + 1) * 512]), "veA")
                else:
                    p.w(scl, "sx")
                    p.inc(scl.activation(out=xn_fm[:, ec * 512:(ec + 1) * 512],
                                         in_=xnt[:, ec * 512:(ec + 1) * 512],
                                         func=AF.Copy), "seA")
                    p.inc(vec.tensor_copy(xn_bf[:, ec * 512:(ec + 1) * 512],
                                          xnt[:, ec * 512:(ec + 1) * 512]), "veA")
            if stage in ("A2noS", "A2noV"):
                continue
            p.w(sync, "seA")
            p.flush(sync, "sx")
            for ec in range(EC):
                p.inc(sync.dma_start(
                    out=xn0[s, ec * 128:(ec + 1) * 128, tau * TT:(tau + 1) * TT],
                    in_=xn_fm[:, ec * 512:(ec + 1) * 512]), "sx", 16)
                p.inc(sync.dma_start(
                    out=xf1[s, ec * 128:(ec + 1) * 128, tau * TT:(tau + 1) * TT],
                    in_=xn_fm[:, ec * 512:(ec + 1) * 512]), "sx", 16)
            if stage in ("A2", "A2noT", "A2noS", "A2noV"):
                continue
            for tc in range(4):
                qb, qb2 = PS[0], PS[1]
                kb, kb2 = PS[2], PS[3]
                lhs = lambda ecc: xn_bf[:, ecc * 512 + tc * 128: ecc * 512 + tc * 128 + 128]
                p.w(ten, "veA", "seA")
                for ecc in range(EC):
                    ten.matmul(qb[:], lhs(ecc), WQT[:, ecc * 1024:ecc * 1024 + 512],
                               start=(ecc == 0), stop=(ecc == EC - 1))
                for ecc in range(EC):
                    ten.matmul(qb2[:], lhs(ecc), WQT[:, ecc * 1024 + 512:(ecc + 1) * 1024],
                               start=(ecc == 0), stop=(ecc == EC - 1))
                for ecc in range(EC):
                    ten.matmul(kb[:], lhs(ecc), WKT[:, ecc * 1024:ecc * 1024 + 512],
                               start=(ecc == 0), stop=(ecc == EC - 1))
                for ecc in range(EC):
                    mm = ten.matmul(kb2[:], lhs(ecc), WKT[:, ecc * 1024 + 512:(ecc + 1) * 1024],
                                    start=(ecc == 0), stop=(ecc == EC - 1))
                p.inc(mm, "peA")
                p.w(scl, "peA")
                p.w(vec, "peA")
                for proj, (b0, b1k, dst) in enumerate(
                        [(qb, qb2, qn_sb), (kb, kb2, kn_sb)]):
                    scl.activation(out=sqs_sb[:, 0:512], in_=b0[:], func=AF.Square)
                    p.inc(scl.activation(out=sqs_sb[:, 512:1024], in_=b1k[:],
                                         func=AF.Square), "seA")
                    p.w(vec, "seA")
                    p.inc(vec.reduce_sum(
                        nrm_sc[:, 0:16],
                        sqs_sb[:].rearrange("p (h d) -> p h d", d=64), axis=AX.X),
                        "veA")
                    p.w(scl, "veA")
                    scl.drain()
                    scl.activation(out=nrm_sc[:, 16:32], in_=nrm_sc[:, 0:16],
                                   func=AF.Ln)
                    scl.drain()
                    p.inc(scl.activation(out=nrm_sc[:, 16:32], in_=nrm_sc[:, 16:32],
                                         func=AF.Exp, scale=0.5), "seA")
                    p.w(vec, "seA")
                    vec.drain()
                    vec.tensor_scalar(out=nrm_sc[:, 32:48], in0=nrm_sc[:, 16:32],
                                      scalar1=1.0 / 8.0, scalar2=EPS_MP,
                                      op0=ALU.mult, op1=ALU.add)
                    vec.drain()
                    vec.reciprocal(nrm_sc[:, 32:48], nrm_sc[:, 32:48])
                    vec.drain()
                    if proj == 1:
                        vec.tensor_scalar_mul(nrm_sc[:, 32:48], nrm_sc[:, 32:48],
                                              1.0 / 64.0)
                        vec.drain()
                    rb = nrm_sc[:, 32:48].rearrange("p (h o) -> p h o", o=1)\
                        .broadcast_to([128, 16, 64])
                    dv = dst[:].rearrange("p (h d) -> p h d", d=64)
                    vec.tensor_tensor(dv[:, 0:8, :],
                                      b0[:].rearrange("p (h d) -> p h d", d=64),
                                      rb[:, 0:8, :], op=ALU.mult)
                    p.inc(vec.tensor_tensor(dv[:, 8:16, :],
                                            b1k[:].rearrange("p (h d) -> p h d", d=64),
                                            rb[:, 8:16, :], op=ALU.mult), "veA")
                if stage == "A3":
                    continue
                p.w(ten, "veA")
                for h in range(NH):
                    mm = ten.matmul(
                        SCORES[(h % 2) * 64:(h % 2) * 64 + 64,
                               (h // 2) * 64:(h // 2) * 64 + 64],
                        qn_sb[:, h * 64:(h + 1) * 64],
                        kn_sb[:, h * 64:(h + 1) * 64],
                        start=True, stop=True)
                p.inc(mm, "peA")
                p.w(vec, "peA")
                vec.drain()
                if tau == 0 and tc == 0:
                    p.inc(vec.tensor_copy(scores_acc[:], SCORES[:]), "veA")
                else:
                    p.inc(vec.tensor_tensor(scores_acc[:], scores_acc[:],
                                            SCORES[:], op=ALU.add), "veA")
            if stage in ("A3", "A4"):
                continue
            for oc in range(EC):
                vb = PS[4]
                p.w(ten, "seA")
                for ecc in range(EC):
                    mm = ten.matmul(vb[:],
                                    WVT[:, ecc * 1024 + oc * 128: ecc * 1024 + oc * 128 + 128],
                                    xn_bf[:, ecc * 512:(ecc + 1) * 512],
                                    start=(ecc == 0), stop=(ecc == EC - 1))
                p.inc(mm, "peA")
                p.w(scl, "peA", "sv")
                p.inc(scl.activation(out=v_sb[:], in_=vb[:], func=AF.Copy), "seA")
                p.w(gp, "seA")
                p.flush(gp, "sv")
                p.inc(gp.dma_start(
                    out=v_buf[s, oc * 128:(oc + 1) * 128, tau * TT:(tau + 1) * TT],
                    in_=v_sb[:]), "sv", 16)

    def phase_B(s):
        p.w(vec, "peA")
        p.w(scl, "peA", "veA")
        vec.drain()
        for h in range(NH):
            pb = (h % 2) * 64
            co = (h // 2) * 64
            sc_v = scores_acc[pb:pb + 64, co:co + 64]
            p.inc(vec.reduce_max(sm_sc[pb:pb + 64, 0:1], sc_v, axis=AX.X,
                                 negate=True), "veB")
            p.w(scl, "veB")
            p.inc(scl.activation(out=sqs_sb[pb:pb + 64, co:co + 64], in_=sc_v,
                                 func=AF.Exp, bias=sm_sc[pb:pb + 64, 0:1],
                                 accum_out=sm_sc[pb:pb + 64, 1:2]), "seB")
            p.w(vec, "seB")
            p.inc(vec.reciprocal(sm_sc[pb:pb + 64, 2:3], sm_sc[pb:pb + 64, 1:2]),
                  "veB")
            p.w(scl, "veB")
            scl.drain()
            p.inc(scl.activation(out=attnb[pb:pb + 64, co:co + 64],
                                 in_=sqs_sb[pb:pb + 64, co:co + 64], func=AF.Copy,
                                 scale=sm_sc[pb:pb + 64, 2:3]), "seB")
        p.w(ten, "seB", "veB2")
        for h in range(NH):
            pb = (h % 2) * 64
            co = (h // 2) * 64
            mm = ten.transpose(PT[pb:pb + 64, co:co + 64],
                               attnb[pb:pb + 64, co:co + 64],
                               ident_bf[pb:pb + 64, pb:pb + 64])
        p.inc(mm, "peB")
        p.w(vec, "peB")
        p.inc(vec.tensor_copy(attnT[:], PT[:, 0:512]), "veB2")

    def phase_C(s):
        p.w(gp, "sv")
        p.w(sync, "sx")
        for tau in range(NTO):
            p.w(sync, "veC", "peC", "veA", "seA", "peA")
            p.w(gp, "peC", "veA", "peA")
            p.flush(sync, "lc")
            p.flush(gp, "lcv")
            for ec in range(EC):
                p.inc(sync.dma_start(
                    out=xn_fm[:, ec * 512:(ec + 1) * 512],
                    in_=xn0[s, ec * 128:(ec + 1) * 128, tau * TT:(tau + 1) * TT]),
                    "lc", 16)
                p.inc(gp.dma_start(
                    out=xn_bf[:, ec * 512:(ec + 1) * 512],
                    in_=v_buf[s, ec * 128:(ec + 1) * 128, tau * TT:(tau + 1) * TT]),
                    "lcv", 16)
            p.w(ten, "lcv", "veB2")
            for hc in range(EC):
                ob = PS[hc % 2]
                p.w(ten, "seC")
                for sub in range(2):
                    h = hc * 2 + sub
                    mm = ten.matmul(ob[sub * 64:sub * 64 + 64, :],
                                    attnT[(h % 2) * 64:(h % 2) * 64 + 64,
                                          (h // 2) * 64:(h // 2) * 64 + 64],
                                    xn_bf[:, hc * 512:(hc + 1) * 512]
                                    [(h % 2) * 64:(h % 2) * 64 + 64, :],
                                    start=True, stop=True)
                p.inc(mm, "peC")
                p.w(scl, "peC")
                p.inc(scl.activation(out=o_bf[:, hc * 512:(hc + 1) * 512],
                                     in_=ob[:], func=AF.Copy), "seC")
            p.w(ten, "seC", "lc")
            for oc in range(EC):
                wb = PS[2 + oc % 2]
                p.w(ten, "veC")
                for ecc in range(EC):
                    mm = ten.matmul(wb[:],
                                    WOT[:, ecc * 1024 + oc * 128: ecc * 1024 + oc * 128 + 128],
                                    o_bf[:, ecc * 512:(ecc + 1) * 512],
                                    start=(ecc == 0), stop=(ecc == EC - 1))
                p.inc(mm, "peC")
                p.w(vec, "peC", "sc")
                vec.drain()
                p.inc(vec.tensor_tensor(s_out[:], wb[:],
                                        xn_fm[:, oc * 512:(oc + 1) * 512],
                                        op=ALU.add), "veC")
                p.w(sync, "veC")
                p.flush(sync, "sc")
                p.inc(sync.dma_start(
                    out=s_buf[s, oc * 128:(oc + 1) * 128, tau * TT:(tau + 1) * TT],
                    in_=s_out[:]), "sc", 16)

    ostage = stage if (stage.startswith("A") or stage == "P0") else "ABC"
    for s in range(BPC):
        if "A" in ostage:
            phase_A(s)
        if "B" in ostage:
            phase_B(s)
        if "C" in ostage:
            phase_C(s)
        if ostage == "P0":
            break

    outctx.close()

    if stage.startswith("L") or stage == "FULL":
        if stage == "FULL":
            n_layers, half = DEPTH, False
        elif stage.endswith("A"):
            n_layers, half = int(stage[1:-1]), True
        else:
            n_layers, half = int(stage[1:]), False

        ictx = ExitStack()

        def isb(name, shape, dt):
            return ictx.enter_context(nc.sbuf_tensor(name, list(shape), dt))

        TI = 256
        NTI = S // TI
        x_cur = isb("x_cur", [128, 8 * TI], F32)
        xq_bf = isb("xq_bf", [128, 8 * TI], BF16)
        xsh4 = isb("xsh4", [128, 4 * TI], F32)
        q0_sb = isb("q0_sb", [128, 2 * TI], F32)
        sqi_bf = isb("sqi_bf", [128, 2 * TI], BF16)
        nrmi = isb("nrmi", [128, 16], F32)
        q1_bf = isb("q1_bf", [128, 2 * TI], BF16)
        scrX = isb("scrX", [128, 4 * TI], F32)
        t1_sb = scrX[:, 0:2 * TI]
        t2_sb = scrX[:, 2 * TI:4 * TI]
        ptB = scrX.bitcast(BF16)
        rnrow = isb("rnrow", [2, 2 * TI], F32)
        qrot_bf = isb("qrot_bf", [128, 8 * TI], BF16)
        kv_ring = isb("kv_ring", [128, 4 * 8 * TI], BF16)
        k_lin = kv_ring[:, 0:2 * 8 * TI]
        v_ring = kv_ring[:, 2 * 8 * TI:4 * 8 * TI]
        rot4 = isb("rot4", [128, 4 * TI], F32)
        zm_sb = isb("zm_sb", [128, 2 * 2 * 2 * 384], BF16)
        smx = isb("smx", [128, 16], F32)
        pt_sb = isb("pt_sb", [128, 2048], BF16)
        oa_bf = isb("oa_bf", [128, 8 * TI], BF16)
        oi_small = isb("oi_small", [128, 2 * TI], F32)
        x_cur2 = x_cur
        xst_bf = qrot_bf
        brow = xsh4[0:1, 0:WIN + 1]
        b2row = xsh4[0:1, WIN + 1:2 * WIN + 2]
        r2row = xsh4[0:1, 2 * WIN + 2:3 * WIN + 2]
        murow = isb("murow", [1, WIN], F32)
        rstdrow = isb("rstdrow", [1, WIN], F32)
        xn_bfi = qrot_bf
        ga_sb = t1_sb
        tln_sb = t2_sb
        h_bf = kv_ring[:, 0:NFC * TI]

        WQKV = arenaA[:, 0:8 * 3072]
        WOI = arenaA[:, 8 * 3072:8 * 3072 + 8 * 1024]
        W1A = arenaA
        W2A = arenaB

        def load_attn_weights(l):
            p.w(gp, "peI")
            p.flush(gp, "lw")
            for e in range(EC):
                p.inc(gp.dma_start(out=WQKV[:, e * 3072:(e + 1) * 3072],
                                   in_=wqkv_in[l, e * 128:(e + 1) * 128, :]), "lw", 16)
            for e in range(EC):
                p.inc(gp.dma_start(out=WOI[:, e * 1024:(e + 1) * 1024],
                                   in_=woi_in[l, e * 128:(e + 1) * 128, :]), "lw", 16)

        def load_ffn_weights(l):
            p.w(gp, "peI")
            p.flush(gp, "lw")
            for e in range(EC):
                p.inc(gp.dma_start(out=W1A[:, e * 5460:(e + 1) * 5460],
                                   in_=w1_in[l, e * 128:(e + 1) * 128, :]), "lw", 16)
            p.w(vec, "lw", "peI")
            p.inc(vec.memset(W2A[:, 0:1024], 0.0), "veI")
            p.w(gp, "veI")
            for ci, (off, szw) in enumerate(zip(FFN_OFF[:-1], FFN_CH)):
                p.inc(gp.dma_start(out=W2A[0:szw, ci * 1024:(ci + 1) * 1024],
                                   in_=w2_in[l, off:off + szw, :]), "lw", 16)

        def attn_pass(l, XIN, XOUT):
            load_attn_weights(l)
            mk = {}
            for s in range(BPC):
                p.w(vec, "smu")
                vec.memset(brow[0:1, 0:1], 0.0)
                vec.drain()
                p.inc(vec.memset(b2row[0:1, 0:1], 0.0), "veI")
                for tau in range(NTI):
                    slot = tau % 2
                    pslot = 1 - slot
                    c0 = tau * TI
                    if tau > 0:
                        p.wat(sync, mk[("wo_done",)], "veI")
                        p.wat(sync, mk[("rot_done", 15)], "veI")
                    else:
                        p.w(sync, "seI", "veI", "peI")
                    p.flush(sync, "lxi")
                    for e in range(EC):
                        p.inc(sync.dma_start(
                            out=x_cur[:, e * TI:(e + 1) * TI],
                            in_=XIN[s, e * 128:(e + 1) * 128, c0:c0 + TI]), "lxi", 16)
                    for t4 in range(4):
                        p.inc(sync.dma_start(
                            out=rot4[:, t4 * TI:(t4 + 1) * TI],
                            in_=[rqc_in, rqs_in, rkc_in, rks_in][t4][:, c0:c0 + TI]),
                            "lxi", 16)
                    if tau > 0:
                        p.wat(gp, mk[("stats_mm",)], "peI")
                    else:
                        p.w(gp, "peI", "seI")
                    p.flush(gp, "lqi")
                    for e in range(EC):
                        if e < 4:
                            p.inc(gp.dma_start(
                                out=xq_bf[:, e * TI:(e + 1) * TI],
                                in_=XIN[s, e * 128:(e + 1) * 128, c0:c0 + TI]),
                                "lqi", 16)
                        elif tau > 0:
                            p.inc(gp.dma_start(
                                out=xq_bf[:, e * TI:(e + 1) * TI],
                                in_=XIN[s, e * 128:(e + 1) * 128, c0 - 1:c0 + TI - 1]),
                                "lqi", 16)
                        else:
                            p.w(vec, "lqi")
                            p.inc(vec.memset(xq_bf[:, e * TI:e * TI + 1], 0.0), "veI")
                            p.w(gp, "veI")
                            p.inc(gp.dma_start(
                                out=xq_bf[:, e * TI + 1:(e + 1) * TI],
                                in_=XIN[s, e * 128:(e + 1) * 128, c0:c0 + TI - 1]),
                                "lqi", 16)
                    if tau > 0:
                        p.wat(vec, mk[("sim", 7)], "peI")
                        vec.drain()
                        p.inc(vec.tensor_copy(
                            k_lin[:].rearrange("p (c w) -> p c w", w=512)[:, :, 0:256],
                            k_lin[:].rearrange("p (c w) -> p c w", w=512)[:, :, 256:512]),
                            "veI")
                        mk[("kshift",)] = p.mark()
                    p.w(ten, "lqi", "lw")
                    if tau > 0:
                        p.wat(ten, mk[("wo_done",)], "veI", "seI")
                        p.wat(ten, mk[("zm", 7)], "veI")
                    else:
                        p.w(ten, "seI", "veI")
                    for tc in range(2):
                        vb0, vb1 = PS[2 * tc], PS[2 * tc + 1]
                        for e in range(EC):
                            ten.matmul(vb0[:],
                                       xq_bf[:, e * TI + tc * 128: e * TI + tc * 128 + 128],
                                       WQKV[:, e * 3072 + 2048:e * 3072 + 2560],
                                       start=(e == 0), stop=(e == EC - 1))
                        for e in range(EC):
                            mm = ten.matmul(vb1[:],
                                            xq_bf[:, e * TI + tc * 128: e * TI + tc * 128 + 128],
                                            WQKV[:, e * 3072 + 2560:e * 3072 + 3072],
                                            start=(e == 0), stop=(e == EC - 1))
                        p.inc(mm, "peI")
                        p.w(scl, "peI")
                        scl.activation(out=v_ring[:, slot * 2048 + tc * 1024:
                                                  slot * 2048 + tc * 1024 + 512],
                                       in_=vb0[:], func=AF.Copy)
                        p.inc(scl.activation(out=v_ring[:, slot * 2048 + tc * 1024 + 512:
                                                        slot * 2048 + tc * 1024 + 1024],
                                             in_=vb1[:], func=AF.Copy), "seI")
                    vsnap = p.mark()
                    NCH = 16

                    def qkv_dst(c):
                        kind, cc = divmod(c, 8)
                        if kind == 0:
                            return qrot_bf[:, cc * TI:(cc + 1) * TI]
                        return k_lin[:, cc * 512 + 256:cc * 512 + 512]

                    base_snap = p.mark()
                    for t in range(NCH + 4):
                        c = t - 3
                        if 0 <= c < NCH:
                            par = c % 2
                            rb = PS[2 + par]
                            p.wat(ten, mk[("q1", c)], "veI")
                            if c >= 2:
                                p.wat(ten, mk[("rot_done", c - 2)], "veI")
                            mm = ten.matmul(rb[:, 0:TI], rotm_bf[:],
                                            q1_bf[:, par * TI:(par + 1) * TI],
                                            start=True, stop=True)
                            p.inc(mm, "peI")
                            mk[("rot", c)] = p.mark()
                            ro = 0 if c < 8 else 2
                            p.wat(vec, mk[("rot", c)], "peI", "lxi")
                            vec.tensor_tensor(t1_sb[:, par * TI:(par + 1) * TI],
                                              q1_bf[:, par * TI:(par + 1) * TI],
                                              rot4[:, ro * TI:(ro + 1) * TI],
                                              op=ALU.mult)
                            vec.tensor_tensor(t2_sb[:, par * TI:(par + 1) * TI],
                                              rb[:, 0:TI],
                                              rot4[:, (ro + 1) * TI:(ro + 2) * TI],
                                              op=ALU.mult)
                            vec.drain()
                            p.inc(vec.tensor_tensor(qkv_dst(c),
                                                    t1_sb[:, par * TI:(par + 1) * TI],
                                                    t2_sb[:, par * TI:(par + 1) * TI],
                                                    op=ALU.add), "veI")
                            mk[("rot_done", c)] = p.mark()
                        c = t - 2
                        if 0 <= c < NCH:
                            par = c % 2
                            p.wat(ten, mk[("min", c)], "veI", "seI")
                            if c >= 2:
                                p.wat(ten, mk[("q1", c - 2)], "veI")
                            mm = ten.matmul(PS[5][:, par * TI:(par + 1) * TI],
                                            exp2_f[:],
                                            rnrow[0:2, par * TI:(par + 1) * TI],
                                            start=True, stop=True)
                            p.inc(mm, "peI")
                            mk[("exp", c)] = p.mark()
                            p.wat(vec, mk[("exp", c)], "peI")
                            p.inc(vec.tensor_tensor(
                                q1_bf[:, par * TI:(par + 1) * TI],
                                q0_sb[:, par * TI:(par + 1) * TI],
                                PS[5][:, par * TI:(par + 1) * TI], op=ALU.mult),
                                "veI")
                            mk[("q1", c)] = p.mark()
                        c = t - 1
                        if 0 <= c < NCH:
                            par = c % 2
                            p.wat(ten, mk[("evac", c)], "seI", "veI")
                            if c >= 2:
                                p.wat(ten, mk[("min", c - 2)], "veI")
                            elif ("stats_rows",) in mk:
                                p.wat(ten, mk[("stats_rows",)], "veI")
                            mm = ten.matmul(PS[4][0:2, par * TI:(par + 1) * TI],
                                            blk2_bf[:],
                                            sqi_bf[:, par * TI:(par + 1) * TI],
                                            start=True, stop=True)
                            p.inc(mm, "peI")
                            mk[("ssq", c)] = p.mark()
                            p.wat(scl, mk[("ssq", c)], "peI")
                            scl.drain()
                            scl.activation(out=PS[4][0:2, par * TI:(par + 1) * TI],
                                           in_=PS[4][0:2, par * TI:(par + 1) * TI],
                                           func=AF.Ln)
                            scl.drain()
                            p.inc(scl.activation(out=PS[4][0:2, par * TI:(par + 1) * TI],
                                                 in_=PS[4][0:2, par * TI:(par + 1) * TI],
                                                 func=AF.Exp, scale=-0.5), "seI")
                            mk[("rn", c)] = p.mark()
                            p.wat(vec, mk[("rn", c)], "seI")
                            p.inc(vec.tensor_scalar_min(
                                rnrow[0:2, par * TI:(par + 1) * TI],
                                PS[4][0:2, par * TI:(par + 1) * TI], 1e12), "veI")
                            mk[("min", c)] = p.mark()
                        c = t
                        if c < NCH:
                            kind, cc = divmod(c, 8)
                            par = c % 2
                            qb = PS[par]
                            if c >= 2:
                                p.wat(ten, mk[("evac", c - 2)], "seI", "veI")
                            else:
                                p.wat(ten, vsnap, "seI")
                            for e in range(EC):
                                mm = ten.matmul(
                                    qb[:, 0:TI],
                                    WQKV[:, e * 3072 + kind * 1024 + cc * 128:
                                         e * 3072 + kind * 1024 + cc * 128 + 128],
                                    xq_bf[:, e * TI:(e + 1) * TI],
                                    start=(e == 0), stop=(e == EC - 1))
                            p.inc(mm, "peI")
                            mk[("mm", c)] = p.mark()
                            kindc = c // 8
                            scv = (qs_sb if kindc == 0 else ks_sb)[:, l:l + 1]
                            p.wat(scl, mk[("mm", c)], "peI")
                            if c >= 2:
                                p.wat(scl, mk[("q1", c - 2)], "veI")
                            scl.activation(out=q0_sb[:, par * TI:(par + 1) * TI],
                                           in_=qb[:, 0:TI], func=AF.Copy, scale=scv)
                            p.inc(scl.activation(out=sqi_bf[:, par * TI:(par + 1) * TI],
                                                 in_=qb[:, 0:TI], func=AF.Square),
                                  "seI")
                            mk[("evac", c)] = p.mark()
                    if tau > 0:
                        live = {0: [(0, 0), (1, 1), (2, 2)],
                                1: [(1, 0), (2, 1), (3, 2)]}
                    else:
                        live = {0: [(2, 2)], 1: [(2, 1), (3, 2)]}

                    def vv(h, which):
                        sl = pslot if which < 2 else slot
                        jj = which % 2
                        return v_ring[:, sl * 2048 + jj * 1024 + h * 64:
                                      sl * 2048 + jj * 1024 + h * 64 + 64]

                    def emit_sim(hp):
                        for sub in range(2):
                            h = 2 * hp + sub
                            cch = h // 2
                            for ic in (0, 1):
                                sim = PS[2 * sub + ic]
                                qv = qrot_bf[(h % 2) * 64:(h % 2) * 64 + 64,
                                             cch * TI + ic * 128:cch * TI + (ic + 1) * 128]
                                if tau > 0:
                                    rhs = k_lin[sub * 64:sub * 64 + 64,
                                                cch * 512 + ic * 128:
                                                cch * 512 + ic * 128 + 384]
                                    mm = ten.matmul(sim[:, 0:384], qv, rhs,
                                                    start=True, stop=True)
                                elif ic == 0:
                                    rhs = k_lin[sub * 64:sub * 64 + 64,
                                                cch * 512 + 256:cch * 512 + 384]
                                    mm = ten.matmul(sim[:, 256:384], qv, rhs,
                                                    start=True, stop=True)
                                else:
                                    rhs = k_lin[sub * 64:sub * 64 + 64,
                                                cch * 512 + 256:cch * 512 + 512]
                                    mm = ten.matmul(sim[:, 128:384], qv, rhs,
                                                    start=True, stop=True)
                        p.inc(mm, "peI")
                        mk[("sim", hp)] = p.mark()

                    def emit_zm(hp):
                        par = hp % 2
                        p.wat(vec, mk[("sim", hp)], "peI")
                        if hp >= 2:
                            p.wat(vec, mk[("T", hp - 2)], "peI")
                            p.wat(vec, mk[("pn", hp - 2)], "seI")
                        for sub in range(2):
                            for ic in (0, 1):
                                lo = live[ic][0][1] * 128
                                p.inc(vec.scalar_tensor_tensor(
                                    out=zm_sb[:, par * 1536 + sub * 768 + ic * 384 + lo:
                                              par * 1536 + sub * 768 + ic * 384 + 384],
                                    in0=PS[2 * sub + ic][:, lo:384], scalar=QK_SCALE,
                                    in1=maskband[:, lo:384],
                                    op0=ALU.mult, op1=ALU.add), "veI")
                        mk[("zm", hp)] = p.mark()
                        vec.drain()
                        for sub in range(2):
                            for ic in (0, 1):
                                lo = live[ic][0][1] * 128
                                p.inc(vec.reduce_max(
                                    smx[:, par * 8 + sub * 2 + ic:
                                        par * 8 + sub * 2 + ic + 1],
                                    zm_sb[:, par * 1536 + sub * 768 + ic * 384 + lo:
                                          par * 1536 + sub * 768 + ic * 384 + 384],
                                    axis=AX.X, negate=True), "veI")
                        mk[("rm", hp)] = p.mark()

                    def emit_softmax(hp):
                        par = hp % 2
                        p.wat(scl, mk[("rm", hp)], "veI")
                        for sub in range(2):
                            for ic in (0, 1):
                                lo = live[ic][0][1] * 128
                                co = par * 1536 + sub * 768 + ic * 384
                                p.inc(scl.activation(
                                    out=zm_sb[:, co + lo:co + 384],
                                    in_=zm_sb[:, co + lo:co + 384],
                                    func=AF.Exp,
                                    bias=smx[:, par * 8 + sub * 2 + ic:
                                             par * 8 + sub * 2 + ic + 1],
                                    accum_out=smx[:, par * 8 + 4 + sub * 2 + ic:
                                                  par * 8 + 5 + sub * 2 + ic]), "seI")
                        mk[("exp", "a", hp)] = p.mark()
                        p.wat(vec, mk[("exp", "a", hp)], "seI")
                        vec.drain()
                        p.inc(vec.reciprocal(smx[:, par * 8 + 12 - 8:par * 8 + 16 - 8]
                                             if False else
                                             smx[:, par * 8 + 4:par * 8 + 8],
                                             smx[:, par * 8 + 4:par * 8 + 8]), "veI")
                        mk[("rcp", hp)] = p.mark()
                        p.wat(scl, mk[("rcp", hp)], "veI")
                        scl.drain()
                        for sub in range(2):
                            for ic in (0, 1):
                                lo = live[ic][0][1] * 128
                                co = par * 1536 + sub * 768 + ic * 384
                                p.inc(scl.activation(
                                    out=zm_sb[:, co + lo:co + 384],
                                    in_=zm_sb[:, co + lo:co + 384],
                                    func=AF.Copy,
                                    scale=smx[:, par * 8 + 4 + sub * 2 + ic:
                                              par * 8 + 5 + sub * 2 + ic]), "seI")
                        mk[("pn", hp)] = p.mark()

                    def emit_T(hp):
                        par = hp % 2
                        ptx = pt_sb if par == 0 else ptB
                        p.wat(ten, mk[("pn", hp)], "seI")
                        if hp >= 1:
                            p.wat(ten, mk[("pte", hp - 1)], "seI", "veI")
                        for sub in range(2):
                            PTx = PT if sub == 0 else PT2
                            for ic in (0, 1):
                                co = par * 1536 + sub * 768 + ic * 384
                                for which, simcol in live[ic]:
                                    mm = ten.transpose(
                                        PTx[:, which * 256 + ic * 128:
                                            which * 256 + (ic + 1) * 128],
                                        zm_sb[:, co + simcol * 128:
                                              co + (simcol + 1) * 128],
                                        ident_bf[:])
                        p.inc(mm, "peI")
                        mk[("T", hp)] = p.mark()
                        p.wat(scl, mk[("T", hp)], "peI")
                        if hp >= 2:
                            p.wat(scl, mk[("av", hp - 2)], "peI")
                        p.inc(scl.activation(out=ptx[:, 0:1024], in_=PT[:],
                                             func=AF.Copy), "seI")
                        p.wat(vec, mk[("T", hp)], "peI")
                        if hp >= 2:
                            p.wat(vec, mk[("av", hp - 2)], "peI")
                        p.inc(vec.tensor_copy(ptx[:, 1024:2048], PT2[:]), "veI")
                        mk[("pte", hp)] = p.mark()

                    def emit_av(hp):
                        ptx = pt_sb if hp % 2 == 0 else ptB
                        p.wat(ten, mk[("pte", hp)], "seI", "veI")
                        if hp >= 1:
                            p.wat(ten, mk[("oe", hp - 1)], "seI", "veI")
                        if tau > 0:
                            order = [(1, 0, 256, True), (0, 0, 128, False),
                                     (2, 0, 256, False), (3, 128, 256, False)]
                        else:
                            order = [(2, 0, 256, True), (3, 128, 256, False)]
                        for sub in range(2):
                            h = 2 * hp + sub
                            ob = PS[4 + sub]
                            for oi_, (which, i0, i1, st_) in enumerate(order):
                                mm = ten.matmul(
                                    ob[0:64, i0:i1],
                                    vv(h, which),
                                    ptx[:, sub * 1024 + which * 256 + i0:
                                        sub * 1024 + which * 256 + i1],
                                    start=st_, stop=(oi_ == len(order) - 1))
                        p.inc(mm, "peI")
                        mk[("av", hp)] = p.mark()
                        h0 = 2 * hp
                        p.wat(scl, mk[("av", hp)], "peI")
                        p.inc(scl.activation(
                            out=oa_bf[0:64, (h0 // 2) * TI:(h0 // 2 + 1) * TI],
                            in_=PS[4][0:64, 0:TI], func=AF.Copy), "seI")
                        p.wat(vec, mk[("av", hp)], "peI")
                        p.inc(vec.tensor_copy(
                            oa_bf[64:128, (h0 // 2) * TI:(h0 // 2 + 1) * TI],
                            PS[5][0:64, 0:TI]), "veI")
                        mk[("oe", hp)] = p.mark()

                    for hp in range(11):
                        if hp < 8:
                            if hp >= 1:
                                p.wat(ten, mk[("zm", hp - 1)], "veI")
                            if hp == 0:
                                p.wat(ten, mk[("rot_done", 14)], "veI")
                                p.wat(ten, mk[("rot_done", 15)], "veI", "seI")
                                if tau > 0:
                                    p.wat(ten, mk[("kshift",)], "veI")
                            emit_sim(hp)
                        if 2 <= hp < 10:
                            emit_T(hp - 2)
                        if hp < 8:
                            emit_zm(hp)
                            emit_softmax(hp)
                        if 3 <= hp:
                            emit_av(hp - 3)
                    p.wat(ten, mk[("oe", 7)], "seI", "veI")
                    p.wat(ten, mk[("zm", 7)], "veI")
                    wo_snap = {}
                    for oc in range(EC):
                        wb = PS[oc % 2]
                        if oc >= 2:
                            p.wat(ten, wo_snap[oc - 2], "veI", "seI")
                        for e in range(EC):
                            mm = ten.matmul(
                                wb[:, 0:TI],
                                WOI[:, e * 1024 + oc * 128:e * 1024 + oc * 128 + 128],
                                oa_bf[:, e * TI:(e + 1) * TI],
                                start=(e == 0), stop=(e == EC - 1))
                        p.inc(mm, "peI")
                        par = oc % 2
                        sxn = "sxa" if par == 0 else "sxb"
                        p.w(vec, "lxi")
                        p.wat(vec, mk[("wo", oc)] if ("wo", oc) in mk else p.mark(),
                              "peI")
                        p.w(vec, sxn)
                        p.inc(vec.tensor_tensor(oi_small[:, par * TI:(par + 1) * TI],
                                                wb[:, 0:TI],
                                                x_cur[:, oc * TI:(oc + 1) * TI],
                                                op=ALU.add), "veI")
                        p.w(scl, "veI")
                        scl.activation(out=xst_bf[:, oc * TI:(oc + 1) * TI],
                                       in_=oi_small[:, par * TI:(par + 1) * TI],
                                       func=AF.Copy)
                        p.inc(scl.activation(out=xq_bf[:, oc * TI:(oc + 1) * TI],
                                             in_=oi_small[:, par * TI:(par + 1) * TI],
                                             func=AF.Square), "seI")
                        p.w(sync, "seI")
                        p.inc(sync.dma_start(
                            out=XOUT[s, oc * 128:(oc + 1) * 128, c0:c0 + TI],
                            in_=oi_small[:, par * TI:(par + 1) * TI]), sxn, 16)
                        wo_snap[oc] = p.mark()
                    mk[("wo_done",)] = p.mark()
                    p.w(ten, "seI", "veI")
                    for e in range(4):
                        ten.matmul(PS[4][0:1, 0:TI], onescol_bf[:],
                                   xst_bf[:, e * TI:(e + 1) * TI],
                                   start=(e == 0), stop=(e == 3))
                    for e in range(4, EC):
                        ten.matmul(PS[4][0:1, TI:2 * TI], onescol_bf[:],
                                   xst_bf[:, e * TI:(e + 1) * TI],
                                   start=(e == 4), stop=(e == 7))
                    for e in range(4):
                        ten.matmul(PS[5][0:1, 0:TI], onescol_bf[:],
                                   xq_bf[:, e * TI:(e + 1) * TI],
                                   start=(e == 0), stop=(e == 3))
                    for e in range(4, EC):
                        mm = ten.matmul(PS[5][0:1, TI:2 * TI], onescol_bf[:],
                                        xq_bf[:, e * TI:(e + 1) * TI],
                                        start=(e == 4), stop=(e == 7))
                    p.inc(mm, "peI")
                    mk[("stats_mm",)] = p.mark()
                    p.w(vec, "peI", "smu")
                    vec.drain()
                    if tau > 0:
                        vec.tensor_copy(brow[0:1, 0:1], brow[0:1, WIN:WIN + 1])
                        vec.drain()
                        vec.tensor_copy(b2row[0:1, 0:1], b2row[0:1, WIN:WIN + 1])
                        vec.drain()
                    vec.tensor_copy(brow[0:1, 1:WIN + 1], PS[4][0:1, TI:2 * TI])
                    vec.tensor_copy(b2row[0:1, 1:WIN + 1], PS[5][0:1, TI:2 * TI])
                    vec.drain()
                    vec.tensor_tensor(murow[0:1, :], PS[4][0:1, 0:TI],
                                      brow[0:1, 0:WIN], op=ALU.add)
                    vec.tensor_tensor(r2row[0:1, :], PS[5][0:1, 0:TI],
                                      b2row[0:1, 0:WIN], op=ALU.add)
                    vec.drain()
                    vec.tensor_scalar_mul(murow[0:1, :], murow[0:1, :], 1.0 / E)
                    vec.drain()
                    vec.tensor_tensor(rstdrow[0:1, :], murow[0:1, :], murow[0:1, :],
                                      op=ALU.mult)
                    vec.drain()
                    p.inc(vec.scalar_tensor_tensor(
                        out=r2row[0:1, :], in0=r2row[0:1, :], scalar=1.0 / E,
                        in1=rstdrow[0:1, :], op0=ALU.mult, op1=ALU.subtract), "veI")
                    mk[("stats_rows",)] = p.mark()
                    p.w(scl, "veI")
                    scl.drain()
                    scl.activation(out=rstdrow[0:1, :], in_=r2row[0:1, :],
                                   func=AF.Ln, bias=consts[0:1, 0:1])
                    scl.drain()
                    p.inc(scl.activation(out=rstdrow[0:1, :], in_=rstdrow[0:1, :],
                                         func=AF.Exp, scale=-0.5), "seI")
                    p.w(sync, "veI", "seI")
                    p.flush(sync, "smu")
                    p.inc(sync.dma_start(out=mu_dram[s, tau], in_=murow[0:1, :]),
                          "smu", 16)
                    p.inc(sync.dma_start(out=rstd_dram[s, tau], in_=rstdrow[0:1, :]),
                          "smu", 16)

        def ffn_pass(l, XIN, XOUT):
            load_ffn_weights(l)
            mk = {}
            for s in range(BPC):
                for tau in range(NTI):
                    c0 = tau * TI
                    xc = x_cur
                    if tau > 0:
                        p.wat(sync, mk[("ffn_ep", tau - 1)], "veI")
                    else:
                        p.w(sync, "seI", "veI", "peI")
                    p.flush(sync, "lxi")
                    for e in range(EC):
                        p.inc(sync.dma_start(
                            out=xc[:, e * TI:(e + 1) * TI],
                            in_=XIN[s, e * 128:(e + 1) * 128, c0:c0 + TI]), "lxi", 16)
                    if tau > 0:
                        p.wat(sync, mk[("ffn_xn", tau - 1)], "veI")
                    for e in range(4, EC):
                        if tau > 0:
                            p.inc(sync.dma_start(
                                out=xsh4[:, (e - 4) * TI:(e - 3) * TI],
                                in_=XIN[s, e * 128:(e + 1) * 128, c0 - 1:c0 + TI - 1]),
                                "lxi", 16)
                        else:
                            p.w(vec, "lxi")
                            p.inc(vec.memset(xsh4[:, (e - 4) * TI:(e - 4) * TI + 1],
                                             0.0), "veI")
                            p.w(sync, "veI")
                            p.inc(sync.dma_start(
                                out=xsh4[:, (e - 4) * TI + 1:(e - 3) * TI],
                                in_=XIN[s, e * 128:(e + 1) * 128, c0:c0 + TI - 1]),
                                "lxi", 16)
                    p.flush(sync, "lmu")
                    p.flush(sync, "smu")
                    p.inc(sync.dma_start(out=murow[0:1, :], in_=mu_dram[s, tau]),
                          "lmu", 16)
                    p.inc(sync.dma_start(out=rstdrow[0:1, :], in_=rstd_dram[s, tau]),
                          "lmu", 16)
                    xnb = qrot_bf if tau % 2 == 0 else oa_bf
                    p.w(ten, "lmu", "lw")
                    if tau > 0:
                        p.wat(ten, mk[("ffn_xn", tau - 1)], "veI")
                    else:
                        p.w(ten, "veI", "seI")
                    ten.matmul(PS[4][:, 0:TI], onesrow_f[:], murow[0:1, :],
                               start=True, stop=True)
                    mm = ten.matmul(PS[5][:, 0:TI], onesrow_f[:], rstdrow[0:1, :],
                                    start=True, stop=True)
                    p.inc(mm, "peI")
                    mk[("ffn_bc", tau)] = p.mark()
                    p.wat(vec, mk[("ffn_bc", tau)], "peI")
                    p.w(vec, "lxi")
                    for e in range(EC):
                        src = (xc[:, e * TI:(e + 1) * TI] if e < 4 else
                               xsh4[:, (e - 4) * TI:(e - 3) * TI])
                        tl = tln_sb[:, (e % 2) * TI:(e % 2 + 1) * TI]
                        vec.tensor_tensor(tl, src, PS[4][:, 0:TI],
                                          op=ALU.subtract)
                        vec.drain()
                        vec.scalar_tensor_tensor(
                            out=tl, in0=tl,
                            scalar=lng_sb[:, l * EC + e:l * EC + e + 1],
                            in1=PS[5][:, 0:TI], op0=ALU.mult, op1=ALU.mult)
                        vec.drain()
                        p.inc(vec.tensor_scalar_add(
                            xnb[:, e * TI:(e + 1) * TI], tl,
                            lnb_sb[:, l * EC + e:l * EC + e + 1]), "veI")
                    mk[("ffn_xn", tau)] = p.mark()
                    p.wat(ten, mk[("ffn_xn", tau)], "veI")
                    p.w(ten, "seI")
                    pair_snap = {}
                    for ci, (off, szw) in enumerate(zip(FFN_OFF[:-1], FFN_CH)):
                        ab = PS[2 * (ci % 2)]
                        gb = PS[2 * (ci % 2) + 1]
                        if ci >= 2:
                            p.wat(ten, pair_snap[ci - 2], "veI", "seI")
                        for e in range(EC):
                            ten.matmul(ab[0:szw, 0:TI],
                                       W1A[:, e * 5460 + off:e * 5460 + off + szw],
                                       xnb[:, e * TI:(e + 1) * TI],
                                       start=(e == 0), stop=(e == EC - 1))
                        for e in range(EC):
                            mm = ten.matmul(gb[0:szw, 0:TI],
                                            W1A[:, e * 5460 + DI + off:
                                                e * 5460 + DI + off + szw],
                                            xnb[:, e * TI:(e + 1) * TI],
                                            start=(e == 0), stop=(e == EC - 1))
                        p.inc(mm, "peI")
                        gv = (ga_sb if ci % 2 == 0 else tln_sb)[:, 0:TI]
                        p.w(scl, "peI")
                        p.inc(scl.activation(out=gv, in_=gb[0:128, 0:TI],
                                             func=AF.Gelu,
                                             bias=b1g_sb[:, l * NFC + ci:
                                                         l * NFC + ci + 1]), "seI")
                        p.w(vec, "seI")
                        p.inc(vec.scalar_tensor_tensor(
                            out=h_bf[:, ci * TI:(ci + 1) * TI], in0=ab[0:128, 0:TI],
                            scalar=b1a_sb[:, l * NFC + ci:l * NFC + ci + 1],
                            in1=gv, op0=ALU.add, op1=ALU.mult), "veI")
                        pair_snap[ci] = p.mark()
                    p.w(ten, "veI")
                    w2_snap = {}
                    for oc in range(EC):
                        wb = PS[oc % 2]
                        par = oc % 2
                        sxn = "sxa" if par == 0 else "sxb"
                        if oc >= 2:
                            p.wat(ten, w2_snap[oc - 2], "veI")
                        else:
                            p.wat(ten, pair_snap[NFC - 2 + oc], "veI", "seI")
                        for ci in range(NFC):
                            mm = ten.matmul(
                                wb[:, 0:TI],
                                W2A[:, ci * 1024 + oc * 128:ci * 1024 + oc * 128 + 128],
                                h_bf[:, ci * TI:(ci + 1) * TI],
                                start=(ci == 0), stop=(ci == NFC - 1))
                        p.inc(mm, "peI")
                        p.w(vec, "peI")
                        p.w(vec, sxn)
                        p.inc(vec.scalar_tensor_tensor(
                            out=oi_small[:, par * TI:(par + 1) * TI], in0=wb[:, 0:TI],
                            scalar=b2_sb[:, l * EC + oc:l * EC + oc + 1],
                            in1=xc[:, oc * TI:(oc + 1) * TI],
                            op0=ALU.add, op1=ALU.add), "veI")
                        p.w(sync, "veI")
                        p.inc(sync.dma_start(
                            out=XOUT[s, oc * 128:(oc + 1) * 128, c0:c0 + TI],
                            in_=oi_small[:, par * TI:(par + 1) * TI]), sxn, 16)
                        w2_snap[oc] = p.mark()
                        if oc == EC - 1:
                            mk[("ffn_ep", tau)] = p.mark()

        for eng in (sync, scl, vec, ten, gp):
            p.w(eng, "lc", "lcv", "sc", "sx", "sv", "seC", "veC", "peC",
                "seA", "veA", "peA", "seB", "veB", "veB2", "peB", "lx", "ld",
                "ldg", "ve", "se", "pe")

        scl.memzero(PT[:])
        p.inc(scl.memzero(PT2[:]), "seI")

        for l in range(n_layers):
            attn_pass(l, xf1, xf2)
            if half and l == n_layers - 1:
                break
            ffn_pass(l, xf2, xf1)

        ictx.close()

    if stage == "FULL":
        fctx = ExitStack()

        def fsb(name, shape, dt):
            return fctx.enter_context(nc.sbuf_tensor(name, list(shape), dt))

        f_sb = arenaA[:, 0 * 8192:1 * 8192].bitcast(F32)
        sF_sb = arenaA[:, 1 * 8192:2 * 8192].bitcast(F32)
        g_sb = arenaA[:, 2 * 8192:3 * 8192].bitcast(F32)
        pF_sb = arenaA[:, 3 * 8192:4 * 8192].bitcast(F32)
        xF_sb = arenaA[:, 4 * 8192:5 * 8192].bitcast(F32)
        yF_sb = fsb("yF_sb", [128, 4096], F32)

        for eng in (sync, scl, vec, ten, gp):
            p.w(eng, "sxa", "sxb", "smu", "seI", "veI", "peI", "lxi", "lqi", "lmu", "lw")

        for s in range(BPC):
            for tau in range(NTO):
                c0 = tau * TT
                p.w(sync, "seF", "veF", "peF", "syF")
                p.flush(sync, "lf")
                for e in range(EC):
                    p.inc(sync.dma_start(
                        out=f_sb[:, e * 512:(e + 1) * 512],
                        in_=xf1[s, e * 128:(e + 1) * 128, c0:c0 + TT]), "lf", 16)
                    p.inc(sync.dma_start(
                        out=sF_sb[:, e * 512:(e + 1) * 512],
                        in_=s_buf[s, e * 128:(e + 1) * 128, c0:c0 + TT]), "lf", 16)
                for j in range(4):
                    p.inc(sync.dma_start(
                        out=xF_sb[:, j * 1024:(j + 1) * 1024],
                        in_=x_in[s, c0 + j * 128:c0 + (j + 1) * 128, :]), "lf", 16)
                p.w(scl, "lf", "veF")
                for e in range(EC):
                    p.inc(scl.activation(out=g_sb[:, e * 512:(e + 1) * 512],
                                         in_=f_sb[:, e * 512:(e + 1) * 512],
                                         func=AF.Silu), "seF")
                p.w(vec, "seF", "lf", "peF")
                for e in range(EC):
                    vec.drain()
                    p.inc(vec.tensor_tensor(pF_sb[:, e * 512:(e + 1) * 512],
                                            g_sb[:, e * 512:(e + 1) * 512],
                                            sF_sb[:, e * 512:(e + 1) * 512],
                                            op=ALU.mult), "veF")
                p.w(ten, "veF")
                for j in range(4):
                    pa = PS[2 * (j % 2)]
                    pb_ = PS[2 * (j % 2) + 1]
                    p.w(ten, "veF2")
                    for e in range(EC):
                        bank = pa if e < 4 else pb_
                        mm = ten.transpose(
                            bank[:, (e % 4) * 128:(e % 4) * 128 + 128],
                            pF_sb[:, e * 512 + j * 128:e * 512 + j * 128 + 128],
                            ident_f[:])
                    p.inc(mm, "peF")
                    p.w(vec, "peF")
                    vec.drain()
                    p.inc(vec.scalar_tensor_tensor(
                        out=yF_sb[:, j * 1024:j * 1024 + 512], in0=pa[:],
                        scalar=ATTN_MIX, in1=xF_sb[:, j * 1024:j * 1024 + 512],
                        op0=ALU.mult, op1=ALU.add), "veF2")
                    vec.drain()
                    p.inc(vec.scalar_tensor_tensor(
                        out=yF_sb[:, j * 1024 + 512:(j + 1) * 1024], in0=pb_[:],
                        scalar=ATTN_MIX, in1=xF_sb[:, j * 1024 + 512:(j + 1) * 1024],
                        op0=ALU.mult, op1=ALU.add), "veF2")
                p.w(sync, "veF2")
                p.flush(sync, "syF")
                for j in range(4):
                    p.inc(sync.dma_start(
                        out=y_out[s, c0 + j * 128:c0 + (j + 1) * 128, :],
                        in_=yF_sb[:, j * 1024:(j + 1) * 1024]), "syF", 16)
        fctx.close()

    p.w(sync, "sc", "sx", "sv", "sxa", "sxb", "smu", "seI", "veI", "peI")
    for name, ext in dbg_outs.items():
        p.inc(sync.dma_start(out=ext, in_=scratch_map[name]), "sdbg", 16)
    if stage != "FULL":
        p.inc(sync.dma_start(out=y_out, in_=x_in), "sy", 16)
        p.w(sync, "sy")
    else:
        p.w(sync, "syF")
    p.w(sync, "sdbg")

    ctx.close()
    return nc


def kernel(**inputs):
    nc = build_program()
    in_maps = [_layout_inputs(inputs, c) for c in range(NCORE)]
    res = run_bass_kernel_spmd(nc, in_maps, core_ids=list(range(NCORE)))
    return np.concatenate([r["y"] for r in res.results], axis=0)
